# version 14
# baseline (speedup 1.0000x reference)
from contextlib import ExitStack

import numpy as np
import ml_dtypes

import concourse.bass as bass
import concourse.mybir as mybir
from concourse.bass_utils import run_bass_kernel_spmd

F32 = mybir.dt.float32
BF16 = mybir.dt.bfloat16
OP = mybir.AluOpType
ACT = mybir.ActivationFunctionType
AX = mybir.AxisListType.X

NCORES = 8
SEQ = 2048
D = 2048
NH = 16
DK = 128
NPASS = 2
T = SEQ // NPASS
CH = 128
NCH = T // CH
NTC = T // 128
KC = D // 128
FF = 1024
NEXP = 8
EGRPS = [(0, 1), (2, 3), (4, 5), (6, 7)]
NOG = D // 512

DEBUG_TAPS = ()
LIM_HEADS = NH
LIM_CHUNKS = NCH
CHUNK_LVL = 6
SKIP_FFN = False
SKIP_WOUT = False

ENGS = ("t", "v", "a", "g", "s")


class Tok:
    __slots__ = ("sem", "val")

    def __init__(self, sem, val):
        self.sem = sem
        self.val = val


class Sched:
    def __init__(self, nc, ctx):
        self.nc = nc
        self.ctx = ctx
        self.sems = {}
        self.counts = {}
        self.dma_streams = set()
        for name in ("t", "v", "a", "g", "s"):
            self.sems[name] = ctx.enter_context(nc.semaphore(f"sem_{name}"))
            self.counts[name] = 0
        self.hw = {e: {} for e in ENGS}
        self.pend = {e: [] for e in ENGS}

    def stream(self, name):
        if name not in self.sems:
            self.sems[name] = self.ctx.enter_context(self.nc.semaphore(f"sem_{name}"))
            self.counts[name] = 0
            self.dma_streams.add(name)
        return name

    def _waits(self, eng, deps):
        want = {}
        for d in deps:
            if d is None:
                continue
            if d.sem == eng and eng == "t":
                continue
            v = self.counts[d.sem] if d.sem in self.dma_streams else d.val
            if want.get(d.sem, 0) < v:
                want[d.sem] = v
        ws = []
        for s, v in want.items():
            if self.hw[eng].get(s, 0) >= v:
                continue
            self.hw[eng][s] = v
            ws.append((self.sems[s], v))
        return ws

    def op(self, eng, deps, fn):
        ws = self._waits(eng, deps)
        self.counts[eng] += 1
        val = self.counts[eng]
        sem = self.sems[eng]

        def run(e, ws=ws, fn=fn, sem=sem):
            for s, v in ws:
                e.wait_ge(s, v)
            fn(e).then_inc(sem, 1)

        self.pend[eng].append(run)
        return Tok(eng, val)

    def dma(self, eng, stream, deps, fn):
        stream = self.stream(stream)
        ws = self._waits(eng, deps)
        self.counts[stream] += 16
        val = self.counts[stream]
        sem = self.sems[stream]

        def run(e, ws=ws, fn=fn, sem=sem):
            for s, v in ws:
                e.wait_ge(s, v)
            fn(e).then_inc(sem, 16)

        self.pend[eng].append(run)
        return Tok(stream, val)

    def raw(self, eng, deps, fn):
        ws = self._waits(eng, deps)

        def run(e, ws=ws, fn=fn):
            for s, v in ws:
                e.wait_ge(s, v)
            fn(e)

        self.pend[eng].append(run)

    def fence(self):
        return [Tok(k, self.counts[k]) for k in self.counts]

    def flush(self, block):
        for eng, fns in self.pend.items():
            if not fns:
                continue
            meth = {"t": block.tensor, "v": block.vector, "a": block.scalar,
                    "g": block.gpsimd, "s": block.sync}[eng]

            def body(e, fns=list(fns)):
                for f in fns:
                    f(e)
            meth(body)
            self.pend[eng] = []


def _host_consts():
    bf = ml_dtypes.bfloat16
    i = np.arange(CH)
    c = {}
    c["IDENT"] = np.eye(128, dtype=bf)
    c["IDENT16F"] = np.eye(16, dtype=np.float32)
    c["M_SL"] = (i[:, None] > i[None, :]).astype(bf)
    c["M_SU"] = (i[:, None] < i[None, :]).astype(bf)
    c["M_IU"] = (i[:, None] <= i[None, :]).astype(bf)
    c["MB_SL"] = np.where(i[:, None] > i[None, :], 0.0, -1e30).astype(np.float32)
    c["MB_IU"] = np.where(i[:, None] <= i[None, :], 0.0, -1e30).astype(np.float32)
    sel16 = np.zeros((NH, NH, 128), dtype=bf)
    for h in range(NH):
        sel16[h, h, :] = 1.0
    c["SEL16B"] = sel16
    nsel16 = np.zeros((NH, NH, 128), dtype=np.float32)
    for h in range(NH):
        nsel16[h, h, :] = -1.0
    c["NSEL16F"] = nsel16
    sel8 = np.zeros((NEXP, NEXP, 128), dtype=bf)
    for e in range(NEXP):
        sel8[e, e, :] = 1.0
    c["SEL8B"] = sel8
    c["ONES1"] = np.ones((1, 128), dtype=bf)
    c["ONESC"] = np.full((1, 128), DK ** -0.5, dtype=bf)
    c["ONECOL"] = np.ones((128, 1), dtype=bf)
    c["CONSTS"] = np.broadcast_to(
        np.array([1e-5, 1e-6, 1.0, 0.0], np.float32), (128, 4)).copy()
    return c


def build_nc():
    nc = bass.Bass("TRN2", target_bir_lowering=False)

    x_d = nc.dram_tensor("x", [SEQ, D], F32, kind="ExternalInput").ap()
    wqkvz_d = nc.dram_tensor("wqkvz", [D, 8192], F32, kind="ExternalInput").ap()
    wba_d = nc.dram_tensor("wba", [D, 32], F32, kind="ExternalInput").ap()
    convw_d = nc.dram_tensor("convw", [128, 192], F32, kind="ExternalInput").ap()
    dtb_d = nc.dram_tensor("dtb", [16, 1], F32, kind="ExternalInput").ap()
    alog_d = nc.dram_tensor("alog", [16, 1], F32, kind="ExternalInput").ap()
    nw_d = nc.dram_tensor("nw", [128, 1], F32, kind="ExternalInput").ap()
    wout_d = nc.dram_tensor("wout", [D, D], F32, kind="ExternalInput").ap()
    rw_d = nc.dram_tensor("rw", [D, 8], F32, kind="ExternalInput").ap()
    wg_d = nc.dram_tensor("wg", [NEXP, D, FF], F32, kind="ExternalInput").ap()
    wu_d = nc.dram_tensor("wu", [NEXP, D, FF], F32, kind="ExternalInput").ap()
    wd_d = nc.dram_tensor("wd", [NEXP, FF, D], F32, kind="ExternalInput").ap()
    out_d = nc.dram_tensor("out", [SEQ, D], F32, kind="ExternalOutput").ap()

    taps = {}
    tapshapes = {
        "t_xnf": ([128, KC, T], BF16), "t_gc16": ([16, T], F32),
        "t_b16": ([16, T], F32), "t_qf": ([128, T], BF16),
        "t_kf": ([128, T], BF16), "t_vf": ([128, T], BF16),
        "t_coref": ([128, NH, T], BF16), "t_cw8": ([8, T], BF16),
        "t_acts": ([128, 2, 8, T], BF16), "t_zsil": ([128, T], BF16),
        "t_kn": ([128, T], BF16), "t_x0": ([128, 256], BF16),
        "t_xfin": ([128, 256], BF16), "t_s": ([128, NH * 128], F32),
        "t_attnt": ([128, CH], BF16), "t_lc": ([128, CH], BF16),
    }
    for name in DEBUG_TAPS:
        shp, dt = tapshapes[name]
        taps[name] = nc.dram_tensor(name, shp, dt, kind="ExternalOutput").ap()

    hc = _host_consts()
    inl = {k: nc.inline_tensor(v, name=k.lower()) for k, v in hc.items()}

    ctx = ExitStack()
    with ctx:
        sb = lambda name, shape, dt: ctx.enter_context(nc.sbuf_tensor(name, shape, dt))
        sch = Sched(nc, ctx)
        block = ctx.enter_context(nc.Block())

        B = [ctx.enter_context(nc.psum_tensor(f"bank{i}", [128, 512], F32))
             for i in range(8)]
        bank_w = [None] * 8
        bank_r = [[] for _ in range(8)]

        def bank_write(i, tok):
            bank_w[i] = tok
            bank_r[i] = []

        def bw_deps(i):
            return list(bank_r[i]) + ([bank_w[i]] if bank_w[i] else [])

        def bank_read(i, tok):
            bank_r[i].append(tok)

        def tap(name, src_ap, deps):
            if name in taps:
                return [sch.dma("s", "dtap", deps, lambda e, a=src_ap, d=taps[name]:
                                e.dma_start(out=d, in_=a))]
            return []

        IDENT = sb("c_ident", [128, 128], BF16)
        IDENT16F = sb("c_id16f", [16, 16], F32)
        M_SL = sb("c_msl", [128, 128], BF16)
        M_SU = sb("c_msu", [128, 128], BF16)
        M_IU = sb("c_miu", [128, 128], BF16)
        MB_SL = sb("c_mbsl", [128, 128], F32)
        MB_IU = sb("c_mbiu", [128, 128], F32)
        SEL16B = sb("c_sel16", [16, NH * 128], BF16)
        NSEL16F = sb("c_nsel16", [16, NH * 128], F32)
        SEL8B = sb("c_sel8", [8, NEXP * 128], BF16)
        ONES1 = sb("c_ones1", [1, 128], BF16)
        ONESC = sb("c_onesc", [1, 128], BF16)
        ONECOL = sb("c_onecol", [128, 1], BF16)
        CONSTS = sb("c_consts", [128, 4], F32)

        wba_sb = sb("wba_sb", [128, KC, 32], BF16)
        rw_sb = sb("rw_sb", [128, KC, 8], BF16)
        convw_sb = sb("convw_sb", [128, 192], F32)
        dtb_sb = sb("dtb_sb", [16, 1], F32)
        alog_sb = sb("alog_sb", [16, 1], F32)
        negea_sb = sb("negea_sb", [16, 1], F32)
        nw_sb = sb("nw_sb", [128, 1], F32)

        xnF = sb("xnF", [128, KC, T], BF16)
        coreF = sb("coreF", [128, NH, T], BF16)
        S_sb = sb("S_sb", [128, NH * 128], F32)
        halo = sb("halo", [128, 48, 3], BF16)

        B16b = sb("B16b", [16, T], BF16)
        GC16 = sb("GC16", [16, T], F32)
        EGC16b = sb("EGC16b", [16, T], BF16)
        GCtok = sb("GCtok", [128, NCH * 16], F32)
        Btok = sb("Btok", [128, NCH * 16], F32)
        EGCtok = sb("EGCtok", [128, NCH * 16], F32)
        BEGCtok = sb("BEGCtok", [128, NCH * 16], F32)
        CW8 = sb("CW8", [8, T], BF16)

        f0 = sch.fence()
        ld = []
        for name, dst in [("IDENT", IDENT), ("IDENT16F", IDENT16F), ("M_SL", M_SL),
                          ("M_SU", M_SU), ("M_IU", M_IU), ("MB_SL", MB_SL),
                          ("MB_IU", MB_IU), ("ONES1", ONES1),
                          ("ONESC", ONESC), ("ONECOL", ONECOL), ("CONSTS", CONSTS)]:
            src = inl[name].ap()
            ld.append(sch.dma("s", "dc", f0, lambda e, s=src, d=dst: e.dma_start(out=d[:], in_=s)))
        for name, dst, nh in [("SEL16B", SEL16B, NH), ("NSEL16F", NSEL16F, NH),
                              ("SEL8B", SEL8B, NEXP)]:
            src = inl[name].ap().rearrange("h p m -> p h m")
            d3 = dst[:].rearrange("p (h m) -> p h m", h=nh)
            ld.append(sch.dma("s", "dc", f0, lambda e, s=src, d=d3: e.dma_start(out=d, in_=s)))
        ld.append(sch.dma("g", "dcg", f0, lambda e: e.dma_start(
            out=wba_sb[:], in_=wba_d.rearrange("(k p) c -> p k c", p=128))))
        ld.append(sch.dma("g", "dcg", f0, lambda e: e.dma_start(
            out=rw_sb[:], in_=rw_d.rearrange("(k p) c -> p k c", p=128))))
        ld.append(sch.dma("s", "dc", f0, lambda e: e.dma_start(out=convw_sb[:], in_=convw_d)))
        ld.append(sch.dma("s", "dc", f0, lambda e: e.dma_start(out=dtb_sb[:], in_=dtb_d)))
        ld.append(sch.dma("s", "dc", f0, lambda e: e.dma_start(out=alog_sb[:], in_=alog_d)))
        ld.append(sch.dma("s", "dc", f0, lambda e: e.dma_start(out=nw_sb[:], in_=nw_d)))
        t_ea = sch.op("a", ld, lambda e: e.activation(negea_sb[:], alog_sb[:], ACT.Exp))
        t_ea = sch.op("v", [t_ea], lambda e: e.tensor_scalar_mul(negea_sb[:], negea_sb[:], -1.0))
        sch.op("v", f0, lambda e: e.memset(S_sb[:], 0.0))
        sch.op("v", f0, lambda e: e.memset(halo[:], 0.0))

        EPS_LN = CONSTS[:, 0:1]
        EPS_L2 = CONSTS[:, 1:2]
        ONE_C = CONSTS[:, 2:3]

        sch.flush(block)

        for p in range(NPASS):
            t0 = p * T
            with ExitStack() as pass_ctx:
                psb = lambda n, s, d: pass_ctx.enter_context(
                    nc.sbuf_tensor(f"{n}_p{p}", s, d))
                acc = psb("acc", [128, NTC, D], BF16)

                pfence = sch.fence()
                ph1_scope = ExitStack()
                p1sb = lambda n, s, d: ph1_scope.enter_context(
                    nc.sbuf_tensor(f"{n}_p{p}", s, d))
                xln = p1sb("xln0", [128, D], F32)
                xnt = p1sb("xnt", [128, D], BF16)
                stat = p1sb("stat", [128, 8], F32)
                B16 = p1sb("B16", [16, T], F32)
                A16 = p1sb("A16", [16, T], F32)
                G16 = p1sb("G16", [16, T], F32)
                xln_tok = None
                prev_rs = None
                prev_xn = None
                prev_tp = None

                for tc in range(NTC):
                    dep = xln_tok if xln_tok else list(pfence)
                    t_x = sch.dma("s", f"dx{p}", dep, lambda e, tc=tc: e.dma_start(
                        out=xln[:], in_=x_d[t0 + tc * 128: t0 + (tc + 1) * 128, :]))
                    t_mean = sch.op("v", [t_x, prev_rs], lambda e: e.tensor_reduce(
                        stat[:, 0:1], xln[:], AX, OP.add))
                    t_sq = sch.op("a", [t_x, prev_xn], lambda e: e.activation(
                        xnt[:], xln[:], ACT.Square, accum_out=stat[:, 1:2]))
                    t_mean = sch.op("v", [t_mean], lambda e: e.tensor_scalar_mul(
                        stat[:, 0:1], stat[:, 0:1], 1.0 / D))
                    t_msq = sch.op("v", [t_mean], lambda e: e.tensor_mul(
                        stat[:, 2:3], stat[:, 0:1], stat[:, 0:1]))
                    t_var = sch.op("v", [t_msq, t_sq, prev_rs], lambda e: e.scalar_tensor_tensor(
                        stat[:, 3:4], stat[:, 1:2], 1.0 / D, stat[:, 2:3],
                        OP.mult, OP.subtract))
                    t_ln = sch.op("a", [t_var], lambda e: e.activation(
                        stat[:, 4:5], stat[:, 3:4], ACT.Ln, bias=EPS_LN))
                    t_rs = sch.op("a", [t_ln, prev_xn], lambda e: e.activation(
                        stat[:, 4:5], stat[:, 4:5], ACT.Exp, scale=-0.5))
                    t_xn = sch.op("v", [t_rs, t_mean, prev_tp, t_sq], lambda e: e.tensor_scalar(
                        xnt[:], xln[:], stat[:, 0:1], stat[:, 4:5],
                        OP.subtract, OP.mult))
                    xln_tok = [t_xn, t_sq]
                    prev_rs = t_rs
                    prev_xn = t_xn
                    for k in range(KC):
                        bi = k % 2
                        t_mm = sch.op("t", [t_xn] + bw_deps(bi),
                                      lambda e, k=k, bi=bi: e.matmul(
                                          B[bi][:, 0:128], xnt[:, k * 128:(k + 1) * 128],
                                          IDENT[:], start=True, stop=True))
                        bank_write(bi, t_mm)
                        eng = "v" if (k % 4) < 2 else "a"
                        if eng == "v":
                            t_e = sch.op("v", [t_mm], lambda e, k=k, tc=tc, bi=bi:
                                         e.tensor_copy(xnF[:, k, tc * 128:(tc + 1) * 128],
                                                       B[bi][:, 0:128]))
                        else:
                            t_e = sch.op("a", [t_mm], lambda e, k=k, tc=tc, bi=bi:
                                         e.copy(xnF[:, k, tc * 128:(tc + 1) * 128],
                                                B[bi][:, 0:128]))
                        bank_read(bi, t_e)
                        prev_tp = t_mm

                xnF_done = sch.fence()
                tap("t_xnf", xnF[:], xnF_done)

                for n in range(2):
                    for part, dst in ((0, B16), (1, A16)):
                        bi = 2 + part
                        deps = bw_deps(bi) + xnF_done
                        tk = None
                        for k in range(KC):
                            tk = sch.op("t", deps if k == 0 else [],
                                        lambda e, k=k, n=n, part=part, bi=bi: e.matmul(
                                B[bi][0:16, 0:512],
                                wba_sb[:, k, 16 * part:16 * part + 16],
                                xnF[:, k, n * 512:(n + 1) * 512],
                                start=(k == 0), stop=(k == KC - 1)))
                        bank_write(bi, tk)
                        t_e = sch.op("v", [tk], lambda e, n=n, dst=dst, bi=bi:
                                     e.tensor_copy(dst[:, n * 512:(n + 1) * 512],
                                                   B[bi][0:16, 0:512]))
                        bank_read(bi, t_e)

                ba_done = sch.fence()
                ph2_scope = ExitStack()
                rsc = ph2_scope.enter_context(nc.sbuf_tensor(f"rsc_p{p}", [128, 8], F32))
                rtmp = ph2_scope.enter_context(nc.sbuf_tensor(f"rtmp_p{p}", [128, 16], F32))
                cwb = ph2_scope.enter_context(nc.sbuf_tensor(f"cwb_p{p}", [128, 8], BF16))
                prev_cw = None
                for tc in range(NTC):
                    deps = bw_deps(6) + xnF_done
                    tk = None
                    for k in range(KC):
                        tk = sch.op("t", deps if k == 0 else [], lambda e, k=k, tc=tc:
                                    e.matmul(B[6][:, 0:8],
                                             xnF[:, k, tc * 128:(tc + 1) * 128],
                                             rw_sb[:, k, :], start=(k == 0),
                                             stop=(k == KC - 1)))
                    bank_write(6, tk)
                    t_e = sch.op("v", [tk, prev_cw] + ba_done, lambda e: e.tensor_copy(
                        rsc[:], B[6][:, 0:8]))
                    bank_read(6, t_e)
                    t_m = sch.op("v", [t_e], lambda e: e.tensor_reduce(
                        rtmp[:, 0:1], rsc[:], AX, OP.max, negate=True))
                    t_x2 = sch.op("a", [t_m], lambda e: e.activation(
                        rsc[:], rsc[:], ACT.Exp, bias=rtmp[:, 0:1]))
                    t_sm = sch.op("v", [t_x2], lambda e: e.tensor_reduce(
                        rtmp[:, 1:2], rsc[:], AX, OP.add))
                    t_r = sch.op("v", [t_sm], lambda e: e.reciprocal(
                        rtmp[:, 2:3], rtmp[:, 1:2]))
                    t_p = sch.op("v", [t_r], lambda e: e.tensor_scalar_mul(
                        rsc[:], rsc[:], rtmp[:, 2:3]))
                    t_m8 = sch.op("v", [t_p], lambda e: e.max(out=rtmp[:, 8:16], in_=rsc[:]))
                    t_dn = sch.op("v", [t_m8], lambda e: e.tensor_add(
                        rtmp[:, 3:4], rtmp[:, 8:9], rtmp[:, 9:10]))
                    t_dn = sch.op("v", [t_dn], lambda e: e.reciprocal(
                        rtmp[:, 4:5], rtmp[:, 3:4]))
                    t_msk = sch.op("v", [t_dn], lambda e: e.tensor_scalar(
                        rtmp[:, 8:16], rsc[:], rtmp[:, 9:10], None, OP.is_ge))
                    t_cw = sch.op("v", [t_msk], lambda e: e.tensor_mul(
                        rsc[:], rsc[:], rtmp[:, 8:16]))
                    t_cw = sch.op("v", [t_cw], lambda e: e.tensor_scalar(
                        cwb[:], rsc[:], rtmp[:, 4:5], None, OP.mult))
                    deps2 = bw_deps(7) + [t_cw]
                    t_tp = sch.op("t", deps2, lambda e: e.matmul(
                        B[7][0:8, 0:128], cwb[:], IDENT[:], start=True, stop=True))
                    bank_write(7, t_tp)
                    t_ec = sch.op("v", [t_tp], lambda e, tc=tc: e.tensor_copy(
                        CW8[:, tc * 128:(tc + 1) * 128], B[7][0:8, 0:128]))
                    bank_read(7, t_ec)
                    prev_cw = t_ec
                comb_done = sch.fence()
                ph2_scope.close()
                tap("t_cw8", CW8[:], comb_done)

                t1 = sch.op("a", ba_done, lambda e: e.activation(
                    B16[:], B16[:], ACT.Exp, scale=-1.0))
                t1 = sch.op("v", [t1], lambda e: e.tensor_scalar_add(B16[:], B16[:], 1.0))
                t1 = sch.op("v", [t1], lambda e: e.reciprocal(B16[:], B16[:]))
                tbb = sch.op("v", [t1], lambda e: e.tensor_copy(B16b[:], B16[:]))
                t2 = sch.op("v", ba_done, lambda e: e.tensor_scalar(
                    A16[:], A16[:], dtb_sb[:], 85.0, OP.add, OP.min))
                t2 = sch.op("a", [t2], lambda e: e.activation(G16[:], A16[:], ACT.Exp))
                t2 = sch.op("a", [t2], lambda e: e.activation(
                    G16[:], G16[:], ACT.Ln, bias=ONE_C[0:16, 0:1]))
                t2 = sch.op("v", [t2, t_ea], lambda e: e.tensor_scalar_mul(
                    G16[:], G16[:], negea_sb[:]))
                tgc = t2
                for c in range(NCH):
                    tgc = sch.op("v", [tgc], lambda e, c=c: e.tensor_tensor_scan(
                        GC16[:, c * CH:(c + 1) * CH], G16[:, c * CH:(c + 1) * CH],
                        G16[:, c * CH:(c + 1) * CH], 0.0, OP.add, OP.bypass))
                tegc = sch.op("a", [tgc], lambda e: e.activation(
                    EGC16b[:], GC16[:], ACT.Exp))
                tap("t_gc16", GC16[:], [tgc])
                tap("t_b16", B16[:], [t1])

                tok_toks = [tgc, t1]
                for c in range(NCH):
                    bi = 4 + (c % 2)
                    for src, dst in ((GC16, GCtok), (B16, Btok)):
                        deps = bw_deps(bi) + [tgc, t1]
                        t_mm = sch.op("t", deps, lambda e, c=c, bi=bi, src=src: e.matmul(
                            B[bi][:, 0:16], src[:, c * CH:(c + 1) * CH],
                            IDENT16F[:], start=True, stop=True))
                        bank_write(bi, t_mm)
                        t_e = sch.op("v", [t_mm], lambda e, c=c, bi=bi, dst=dst:
                                     e.tensor_copy(dst[:, c * 16:(c + 1) * 16],
                                                   B[bi][:, 0:16]))
                        bank_read(bi, t_e)
                        tok_toks.append(t_e)
                t_eg = sch.op("a", tok_toks, lambda e: e.activation(
                    EGCtok[:], GCtok[:], ACT.Exp))
                t_bg = sch.op("v", [t_eg], lambda e: e.tensor_mul(
                    BEGCtok[:], Btok[:], EGCtok[:]))
                scal_done = sch.fence()
                ph1_scope.close()

                with ExitStack() as head_scope:
                    hsb = lambda n, s, d: head_scope.enter_context(
                        nc.sbuf_tensor(f"{n}_p{p}", s, d))
                    wq_buf = [hsb("wq0", [128, KC, 256], BF16),
                              hsb("wq1", [128, KC, 256], BF16)]
                    wq_tok = [None, None]
                    wq_rd = [None, None]
                    xmix = hsb("xmix", [128, 3, T + 3], BF16)
                    zraw = hsb("zraw", [128, T], BF16)
                    zsil = zraw
                    qF = hsb("qF", [128, T], BF16)
                    kF = hsb("kF", [128, T], BF16)
                    vF = hsb("vF", [128, T], BF16)
                    sqs = hsb("sqs", [128, 512], BF16)
                    rrow = hsb("rrow", [1, 2 * T], BF16)
                    diag = hsb("diag", [128, 8, 128], BF16)
                    Dneg = hsb("Dneg", [128, T], F32)
                    BCb = hsb("BCb", [128, T], BF16)
                    BCe = hsb("BCe", [128, T], BF16)
                    BCq = hsb("BCq", [128, T], BF16)
                    BCk = hsb("BCk", [128, T], BF16)
                    KN = hsb("KN", [128, T], BF16)
                    KB = hsb("KB", [128, T], BF16)
                    QN = hsb("QN", [128, T], BF16)
                    QG = hsb("QG", [128, T], BF16)
                    kntok = hsb("kntok", [128, T], BF16)
                    vtok = hsb("vtok", [128, T], BF16)
                    Dch2 = [hsb("Dch0", [128, CH], F32), hsb("Dch1", [128, CH], F32)]
                    ETch2 = [hsb("ETch0", [128, CH], F32), hsb("ETch1", [128, CH], F32)]
                    EmS2 = [hsb("EmS0", [128, CH], BF16), hsb("EmS1", [128, CH], BF16)]
                    ETmU2 = [hsb("ETmU0", [128, CH], BF16), hsb("ETmU1", [128, CH], BF16)]
                    ETmI2 = [hsb("ETmI0", [128, CH], BF16), hsb("ETmI1", [128, CH], BF16)]
                    Lc2 = [hsb("Lc0", [128, CH], BF16), hsb("Lc1", [128, CH], BF16)]
                    LcT2 = [hsb("LcT0", [128, CH], BF16), hsb("LcT1", [128, CH], BF16)]
                    Lp22 = [hsb("Lp20", [128, 2 * CH], BF16), hsb("Lp21", [128, 2 * CH], BF16)]
                    attnT2 = [hsb("attnT0", [128, CH], BF16), hsb("attnT1", [128, CH], BF16)]
                    Xsb2 = [hsb("Xsb0", [128, 256], BF16), hsb("Xsb1", [128, 256], BF16)]
                    WFn2 = [hsb("WFn0", [128, CH], BF16), hsb("WFn1", [128, CH], BF16)]
                    vnew2 = [hsb("vnew0", [128, 128], BF16), hsb("vnew1", [128, 128], BF16)]
                    Sb16 = hsb("Sb16", [128, 128], BF16)
                    kdt2 = [hsb("kdt0", [128, 128], BF16), hsb("kdt1", [128, 128], BF16)]
                    colscr = hsb("colscr", [128, 8], F32)
                    rmsjunk = hsb("rmsjunk", [128, 128], BF16)
                    coretok = hsb("coretok", [128, NCH, 128], BF16)
                    rmsc = hsb("rmsc", [128, NCH], F32)

                    def prefetch_wq(m, deps):
                        slot = m % 2
                        src = wqkvz_d[:, 256 * m:256 * (m + 1)].rearrange(
                            "(k p) c -> p k c", p=128)
                        extra = [wq_rd[slot]] if wq_rd[slot] else []
                        wq_tok[slot] = sch.dma("g", f"dwq{p}s{slot}", deps + extra,
                                               lambda e, s=src, d=wq_buf[slot]:
                                               e.dma_start(out=d[:], in_=s))

                    prefetch_wq(0, scal_done)
                    prefetch_wq(1, scal_done)

                    from collections import deque

                    head_state = {"prev_conv": [], "prev_ho": None,
                                  "prev_gate": [], "sb_chain": None}

                    def make_prepA(h):
                        stp = {"ev": {}, "convt": {}, "conv_mms": []}
                        th = []
                        zth = []
                        gate_x = list(head_state["prev_conv"]) + \
                            ([head_state["prev_ho"]] if head_state["prev_ho"] else [])

                        def t_hi_f(stp=stp, h=h, gate_x=gate_x):
                            stp["hi"] = sch.op("v", gate_x, lambda e: e.tensor_copy(
                                xmix[:, :, 0:3], halo[:, 3 * h:3 * h + 3, :]))
                        th.append(t_hi_f)

                        for ob in range(3):
                            m = 2 * h + (ob // 2)
                            slot = m % 2
                            obh = ob % 2
                            for n in range(2):
                                bi = (ob * 2 + n) % 2
                                for kq in range(4):
                                    def mmf(ob=ob, n=n, bi=bi, kq=kq, slot=slot,
                                            obh=obh, stp=stp, gate_x=gate_x):
                                        wq = wq_buf[slot]
                                        deps = (bw_deps(bi) + [wq_tok[slot]]
                                                ) if kq == 0 else []
                                        tk = None
                                        for k in range(4 * kq, 4 * kq + 4):
                                            tk = sch.op("t", deps if k == 4 * kq == 0 or
                                                        (kq == 0 and k == 0) else [],
                                                        lambda e, k=k, obh=obh, n=n,
                                                        bi=bi, wq=wq: e.matmul(
                                                B[bi][:, 0:512],
                                                wq[:, k, obh * 128:(obh + 1) * 128],
                                                xnF[:, k, n * 512:(n + 1) * 512],
                                                start=(k == 0), stop=(k == KC - 1),
                                                skip_group_check=True))
                                        if kq == 3:
                                            bank_write(bi, tk)
                                            wq_rd[slot] = tk
                                            t_e = sch.op("a", [tk] + gate_x,
                                                         lambda e, ob=ob, n=n, bi=bi:
                                                         e.copy(
                                                xmix[:, ob, 3 + n * 512:
                                                     3 + (n + 1) * 512],
                                                B[bi][:, 0:512]))
                                            bank_read(bi, t_e)
                                            stp["ev"][(ob, n)] = t_e
                                    th.append(mmf)

                        def t_ho_f(stp=stp, h=h):
                            stp["ho"] = sch.op("v", [stp["ev"][(0, 1)], stp["ev"][(1, 1)],
                                                     stp["ev"][(2, 1)], stp["hi"]],
                                               lambda e: e.tensor_copy(
                                halo[:, 3 * h:3 * h + 3, :], xmix[:, :, T:T + 3]))
                        th.append(t_ho_f)

                        prev_cmm = {}
                        conv_parts = [(0, "qF"), (1, "kF"), (2, "vF")]
                        for part, dstn in conv_parts:
                            dst = {"qF": qF, "kF": kF, "vF": vF}[dstn]
                            ps = part % 2

                            def diagf(part=part, ps=ps, h=h, stp=stp):
                                gate_d = list(head_state["prev_conv"]) + \
                                    ([prev_cmm[ps]] if ps in prev_cmm else [])
                                stp[f"dg{part}"] = []
                                for j in range(4):
                                    col = (part * 16 + h) * 4 + j
                                    stp[f"dg{part}"].append(sch.op(
                                        "v", gate_d, lambda e, ps=ps, j=j, col=col:
                                        e.tensor_scalar_mul(diag[:, ps * 4 + j, :],
                                                            IDENT[:],
                                                            convw_sb[:, col:col + 1])))
                            th.append(diagf)
                            for n in range(2):
                                bi = 2 + (part * 2 + n) % 2
                                bi = (part * 2 + n) % 2

                                def convf(part=part, n=n, bi=bi, ps=ps, dst=dst,
                                          dstn=dstn, stp=stp):
                                    deps = (bw_deps(bi) + [stp["hi"],
                                            stp["ev"][(part, 0)], stp["ev"][(part, 1)]]
                                            + stp[f"dg{part}"])
                                    tk = None
                                    for j in range(4):
                                        tk = sch.op("t", deps if j == 0 else [],
                                                    lambda e, j=j, part=part, n=n,
                                                    bi=bi, ps=ps: e.matmul(
                                            B[bi][:, 0:512], diag[:, ps * 4 + j, :],
                                            xmix[:, part, j + n * 512:
                                                 j + n * 512 + 512],
                                            start=(j == 0), stop=(j == 3),
                                            skip_group_check=True))
                                    bank_write(bi, tk)
                                    prev_cmm[ps] = tk
                                    stp["conv_mms"].append(tk)
                                    t_e = sch.op("a", [tk], lambda e, dst=dst, n=n,
                                                 bi=bi: e.activation(
                                        dst[:, n * 512:(n + 1) * 512],
                                        B[bi][:, 0:512], ACT.Silu))
                                    bank_read(bi, t_e)
                                    stp["convt"][(part, n)] = t_e
                                th.append(convf)

                        def l2f(stp=stp):
                            ss_toks = []
                            t_sq2 = None
                            for idx, srcn in enumerate(("qF", "kF")):
                                srct = {"qF": qF, "kF": kF}[srcn]
                                cv = [stp["convt"][(idx, 0)], stp["convt"][(idx, 1)]]
                                for n in range(2):
                                    dep_s = cv + ([t_sq2] if t_sq2 else [])
                                    t_sq2 = sch.op("v", dep_s, lambda e, srct=srct,
                                                   n=n: e.tensor_mul(
                                        sqs[:], srct[:, n * 512:(n + 1) * 512],
                                        srct[:, n * 512:(n + 1) * 512]))
                                    bi = n % 2
                                    deps = bw_deps(bi) + [t_sq2]
                                    t_mm = sch.op("t", deps, lambda e, bi=bi: e.matmul(
                                        B[bi][0:1, 0:512], ONECOL[:], sqs[:],
                                        start=True, stop=True))
                                    bank_write(bi, t_mm)
                                    t_sq2 = t_mm
                                    t_e = sch.op("v", [t_mm], lambda e, idx=idx, n=n,
                                                 bi=bi: e.tensor_copy(
                                        rrow[:, idx * T + n * 512:
                                             idx * T + (n + 1) * 512],
                                        B[bi][0:1, 0:512]))
                                    bank_read(bi, t_e)
                                    ss_toks.append(t_e)
                            t_ln2 = sch.op("a", ss_toks, lambda e: e.activation(
                                rrow[:], rrow[:], ACT.Ln, bias=EPS_L2[0:1, 0:1]))
                            stp["rr"] = sch.op("a", [t_ln2], lambda e: e.activation(
                                rrow[:], rrow[:], ACT.Exp, scale=-0.5))
                        th.append(l2f)

                        if h + 1 < NH:
                            def pf(h=h):
                                prefetch_wq(2 * h + 2, [])
                            th.append(pf)

                        i_l2 = len(th) - (2 if h + 1 < NH else 1)
                        l2_thunk = th.pop(i_l2)
                        th.insert(i_l2 - 3, l2_thunk)
                        m = 2 * h + 1
                        slot = m % 2
                        for n in range(2):
                            bi = 6 + n
                            bi = n % 2
                            for kq in range(4):
                                def zmf(n=n, bi=bi, kq=kq, slot=slot, stp=stp):
                                    wq = wq_buf[slot]
                                    gate_z = list(head_state["prev_gate"])
                                    deps = (bw_deps(bi) + [wq_tok[slot]]
                                            ) if kq == 0 else []
                                    tk = None
                                    for k in range(4 * kq, 4 * kq + 4):
                                        tk = sch.op("t", deps if (kq == 0 and k == 0)
                                                    else [],
                                                    lambda e, k=k, n=n, bi=bi, wq=wq:
                                                    e.matmul(
                                            B[bi][:, 0:512], wq[:, k, 128:256],
                                            xnF[:, k, n * 512:(n + 1) * 512],
                                            start=(k == 0), stop=(k == KC - 1),
                                            skip_group_check=True))
                                    if kq == 3:
                                        bank_write(bi, tk)
                                        wq_rd[slot] = tk
                                        t_e = sch.op("a", [tk] + gate_z,
                                                     lambda e, n=n, bi=bi: e.copy(
                                            zraw[:, n * 512:(n + 1) * 512],
                                            B[bi][:, 0:512]))
                                        bank_read(bi, t_e)
                                        stp["ev"][(3, n)] = t_e
                                zth.append(zmf)

                        def zsf(stp=stp, h=h):
                            stp["zs"] = sch.op("a", [stp["ev"][(3, 0)],
                                                     stp["ev"][(3, 1)]],
                                               lambda e: e.activation(
                                zsil[:], zraw[:], ACT.Silu))
                            if h + 1 < NH:
                                prefetch_wq(2 * h + 3, [])
                        zth.append(zsf)
                        return th, zth, stp

                    def emit_prepB(h, stp):
                        out = {}
                        conv_q = [stp["convt"][(0, 0)], stp["convt"][(0, 1)]]
                        conv_k = [stp["convt"][(1, 0)], stp["convt"][(1, 1)]]
                        conv_v = [stp["convt"][(2, 0)], stp["convt"][(2, 1)]]
                        gate_b = list(head_state["prev_gate"])
                        t_nd = []
                        for n in range(2):
                            bi = 2 + n
                            deps = bw_deps(bi) + [tgc] + gate_b
                            t_mm = sch.op("t", deps, lambda e, h=h, n=n, bi=bi:
                                          e.matmul(
                                B[bi][:, 0:512], NSEL16F[:, h * 128:(h + 1) * 128],
                                GC16[:, n * 512:(n + 1) * 512],
                                start=True, stop=True))
                            bank_write(bi, t_mm)
                            t_e = sch.op("v", [t_mm] + gate_b, lambda e, n=n, bi=bi:
                                         e.tensor_copy(
                                Dneg[:, n * 512:(n + 1) * 512], B[bi][:, 0:512]))
                            bank_read(bi, t_e)
                            t_nd.append(t_e)
                        out["t_nd"] = t_nd
                        t_vt = []
                        for c in range(NCH):
                            bi = 6 + (c % 2)
                            deps = bw_deps(bi) + conv_v + gate_b
                            t_mm = sch.op("t", deps, lambda e, c=c, bi=bi: e.matmul(
                                B[bi][:, 0:128], vF[:, c * 128:(c + 1) * 128],
                                IDENT[:], start=True, stop=True))
                            bank_write(bi, t_mm)
                            if c % 2 == 0:
                                t_e = sch.op("v", [t_mm], lambda e, c=c, bi=bi:
                                             e.tensor_copy(
                                    vtok[:, c * 128:(c + 1) * 128], B[bi][:, 0:128]))
                            else:
                                t_e = sch.op("a", [t_mm], lambda e, c=c, bi=bi:
                                             e.copy(
                                    vtok[:, c * 128:(c + 1) * 128], B[bi][:, 0:128]))
                            bank_read(bi, t_e)
                            t_vt.append(t_e)
                        out["t_vt"] = t_vt
                        out["t_sb"] = sch.op("v", gate_b + t_nd, lambda e, h=h:
                                             e.tensor_copy(
                            Sb16[:], S_sb[:, h * 128:(h + 1) * 128]))
                        t_kn = t_kb = t_qn = t_qg = None
                        for i, (kind, srcf, dstf) in enumerate((
                                ("k", kF, KN), ("b", None, KB),
                                ("q", qF, QN), ("e", None, QG))):
                            for n in range(2):
                                bi = 4 + (n % 2)
                                deps = bw_deps(bi) + [stp["rr"], tbb, tegc] + gate_b
                                if kind == "b":
                                    t_mm = sch.op("t", deps, lambda e, h=h, n=n, bi=bi:
                                                  e.matmul(
                                        B[bi][:, 0:512],
                                        SEL16B[:, h * 128:(h + 1) * 128],
                                        B16b[:, n * 512:(n + 1) * 512],
                                        start=True, stop=True))
                                elif kind == "e":
                                    t_mm = sch.op("t", deps, lambda e, h=h, n=n, bi=bi:
                                                  e.matmul(
                                        B[bi][:, 0:512],
                                        SEL16B[:, h * 128:(h + 1) * 128],
                                        EGC16b[:, n * 512:(n + 1) * 512],
                                        start=True, stop=True))
                                elif kind == "q":
                                    t_mm = sch.op("t", deps, lambda e, n=n, bi=bi:
                                                  e.matmul(
                                        B[bi][:, 0:512], ONESC[:],
                                        rrow[:, n * 512:(n + 1) * 512],
                                        start=True, stop=True))
                                else:
                                    t_mm = sch.op("t", deps, lambda e, n=n, bi=bi:
                                                  e.matmul(
                                        B[bi][:, 0:512], ONES1[:],
                                        rrow[:, T + n * 512: T + (n + 1) * 512],
                                        start=True, stop=True))
                                bank_write(bi, t_mm)
                                if kind == "k":
                                    dep2 = conv_k + [t_mm] + gate_b
                                    t = sch.op("v", dep2, lambda e, n=n, bi=bi:
                                               e.tensor_mul(
                                        KN[:, n * 512:(n + 1) * 512],
                                        kF[:, n * 512:(n + 1) * 512],
                                        B[bi][:, 0:512]))
                                    t_kn = t if n == 1 else t_kn or t
                                    out.setdefault("kn", []).append(t)
                                elif kind == "b":
                                    dep2 = out["kn"] + [t_mm]
                                    t = sch.op("v", dep2, lambda e, n=n, bi=bi:
                                               e.tensor_mul(
                                        KB[:, n * 512:(n + 1) * 512],
                                        KN[:, n * 512:(n + 1) * 512],
                                        B[bi][:, 0:512]))
                                    out.setdefault("kb", []).append(t)
                                elif kind == "q":
                                    dep2 = conv_q + [t_mm] + gate_b
                                    t = sch.op("v", dep2, lambda e, n=n, bi=bi:
                                               e.tensor_mul(
                                        QN[:, n * 512:(n + 1) * 512],
                                        qF[:, n * 512:(n + 1) * 512],
                                        B[bi][:, 0:512]))
                                    out.setdefault("qn", []).append(t)
                                else:
                                    dep2 = out["qn"] + [t_mm]
                                    t = sch.op("v", dep2, lambda e, n=n, bi=bi:
                                               e.tensor_mul(
                                        QG[:, n * 512:(n + 1) * 512],
                                        QN[:, n * 512:(n + 1) * 512],
                                        B[bi][:, 0:512]))
                                    out.setdefault("qg", []).append(t)
                                bank_read(bi, t)
                        out["t_kb"] = out["kb"][-1]
                        out["t_qg"] = out["qg"][-1]

                        t_knt = []
                        for c in range(NCH):
                            bi = 6 + (c % 2)
                            deps = bw_deps(bi) + out["kn"]
                            t_mm = sch.op("t", deps, lambda e, c=c, bi=bi: e.matmul(
                                B[bi][:, 0:128], KN[:, c * 128:(c + 1) * 128],
                                IDENT[:], start=True, stop=True))
                            bank_write(bi, t_mm)
                            if c % 2 == 0:
                                t_e = sch.op("v", [t_mm], lambda e, c=c, bi=bi:
                                             e.tensor_copy(
                                    kntok[:, c * 128:(c + 1) * 128], B[bi][:, 0:128]))
                            else:
                                t_e = sch.op("a", [t_mm], lambda e, c=c, bi=bi:
                                             e.copy(
                                    kntok[:, c * 128:(c + 1) * 128], B[bi][:, 0:128]))
                            bank_read(bi, t_e)
                            t_knt.append(t_e)
                        out["t_knt"] = t_knt
                        return out

                    thA, zthA, stpA = make_prepA(0)
                    for f in thA:
                        f()
                    for f in zthA:
                        f()

                    fillq = deque()

                    def drain(n=1, reserve=12):
                        for _ in range(n):
                            if len(fillq) > reserve:
                                fillq.popleft()()

                    for h in range(LIM_HEADS):
                        drain(6, 0)
                        while "rr" not in stpA and fillq:
                            fillq.popleft()()
                        pb_out = emit_prepB(h, stpA)
                        t_kb = pb_out["t_kb"]
                        t_qg = pb_out["t_qg"]
                        t_knt = pb_out["t_knt"]
                        t_vt = pb_out["t_vt"]
                        t_nd = pb_out["t_nd"]
                        t_sb = pb_out["t_sb"]
                        head_state["prev_conv"] = list(stpA["conv_mms"])
                        head_state["prev_ho"] = stpA["ho"]
                        cur_stp = stpA
                        if h + 1 < LIM_HEADS:
                            thN, zthN, stpN = make_prepA(h + 1)
                            fillq.extend(thN)
                        else:
                            thN, zthN, stpN = [], [], None
                        prev_end = [[], []]

                        def emit_chunk(c):
                            pb = c % 2
                            XB = 3 if pb == 0 else 6
                            QB = 4 if pb == 0 else 7
                            pe = prev_end[pb]
                            Dch, ETch = Dch2[pb], ETch2[pb]
                            EmS, ETmU, ETmI = EmS2[pb], ETmU2[pb], ETmI2[pb]
                            Lc, LcT, Lp2 = Lc2[pb], LcT2[pb], Lp22[pb]
                            attnT, Xsb, WFn = attnT2[pb], Xsb2[pb], WFn2[pb]
                            vnew, kdt = vnew2[pb], kdt2[pb]
                            cs0 = 4 * pb
                            gc_col = GCtok[:, c * 16 + h: c * 16 + h + 1]
                            b_col = Btok[:, c * 16 + h: c * 16 + h + 1]
                            be_col = BEGCtok[:, c * 16 + h: c * 16 + h + 1]
                            csl = slice(c * 128, (c + 1) * 128)
                            st = {"toks": []}

                            def tk(t):
                                st["toks"].append(t)
                                return t

                            def s_prep():
                                st["da"] = tk(sch.op("v", t_nd + pe, lambda e:
                                                     e.tensor_scalar(
                                    Dch[:], Dneg[:, csl], gc_col, None, OP.add)))
                                e2p = tk(sch.op("v", [st["da"]] + pe, lambda e:
                                                e.scalar_tensor_tensor(
                                    ETch[:], Dch[:], -1.0, MB_IU[:],
                                    OP.mult, OP.add)))
                                e1p = tk(sch.op("v", [e2p], lambda e:
                                                e.tensor_add(Dch[:], Dch[:], MB_SL[:])))
                                st["m1"] = tk(sch.op("a", [e1p] + pe, lambda e:
                                                     e.activation(EmS[:], Dch[:], ACT.Exp)))
                                st["m3"] = tk(sch.op("a", [e2p] + pe, lambda e:
                                                     e.activation(ETmI[:], ETch[:], ACT.Exp)))
                                st["m2"] = tk(sch.op("v", [st["m3"]] + pe, lambda e:
                                                     e.tensor_mul(ETmU[:], ETmI[:], M_SU[:])))

                            def s_grams():
                                depsg = bw_deps(2) + [t_kb, t_qg]
                                g2 = tk(sch.op("t", depsg, lambda e: e.matmul(
                                    B[2][:, 0:128], KB[:, csl], KN[:, csl],
                                    start=True, stop=True)))
                                g2t = tk(sch.op("t", [], lambda e: e.matmul(
                                    B[2][:, 128:256], KN[:, csl], KB[:, csl],
                                    start=True, stop=True)))
                                aqk = tk(sch.op("t", [], lambda e: e.matmul(
                                    B[2][:, 256:384], KN[:, csl], QN[:, csl],
                                    start=True, stop=True)))
                                bank_write(2, aqk)
                                st["L"] = tk(sch.op("v", [g2, st["m1"]] + pe, lambda e:
                                                    e.scalar_tensor_tensor(
                                    Lc[:], B[2][:, 0:128], -1.0, EmS[:],
                                    OP.mult, OP.mult)))
                                st["LT"] = tk(sch.op("v", [g2t, st["m2"]] + pe, lambda e:
                                                     e.scalar_tensor_tensor(
                                    LcT[:], B[2][:, 128:256], -1.0, ETmU[:],
                                    OP.mult, OP.mult)))
                                st["at"] = tk(sch.op("v", [aqk, st["m3"]] + pe, lambda e:
                                                     e.tensor_mul(
                                    attnT[:], B[2][:, 256:384], ETmI[:])))
                                for tt in (st["L"], st["LT"], st["at"]):
                                    bank_read(2, tt)

                            def s_x0():
                                st["x0a"] = tk(sch.op("v", t_vt + pe, lambda e:
                                                      e.tensor_scalar(
                                    Xsb[:, 0:128], vtok[:, csl], b_col, None, OP.mult)))
                                st["x0b"] = tk(sch.op("v", t_knt + [t_bg] + pe, lambda e:
                                                      e.tensor_scalar(
                                    Xsb[:, 128:256], kntok[:, csl], be_col,
                                    None, OP.mult)))
                                st["lp"] = (Lc, st["L"])
                                st["lpt"] = (LcT, st["LT"])
                                st["xev"] = None

                            def s_lev(lev):
                                lp_cur, t_lp = st["lp"]
                                lpt_cur, t_lpt = st["lpt"]
                                if lev == 0:
                                    depsx = bw_deps(XB) + [st["x0a"], st["x0b"]]
                                    tk(sch.op("t", depsx, lambda e: e.matmul(
                                        B[XB][:, 0:256], IDENT[:], Xsb[:],
                                        start=True, stop=False, skip_group_check=True)))
                                    t_ap = tk(sch.op("t", [t_lpt], lambda e, lpt=lpt_cur:
                                                     e.matmul(
                                        B[XB][:, 0:256], lpt[:], Xsb[:],
                                        start=False, stop=False, skip_group_check=True)))
                                else:
                                    t_ap = tk(sch.op("t", [t_lpt, st["xev"]],
                                                     lambda e, lpt=lpt_cur, lev=lev:
                                                     e.matmul(
                                        B[XB][:, 0:256], lpt[:], Xsb[:],
                                        start=False, stop=(lev == 6),
                                        skip_group_check=True)))
                                bank_write(XB, t_ap)
                                if lev % 2 == 0:
                                    st["xev"] = tk(sch.op("v", [t_ap], lambda e:
                                                          e.tensor_copy(Xsb[:], B[XB][:, 0:256])))
                                else:
                                    st["xev"] = tk(sch.op("a", [t_ap], lambda e:
                                                          e.copy(Xsb[:], B[XB][:, 0:256])))
                                bank_read(XB, st["xev"])
                                if lev < 6:
                                    depsq = bw_deps(QB) + [t_lp, t_lpt]
                                    tk(sch.op("t", depsq, lambda e, lp=lp_cur,
                                              lpt=lpt_cur: e.matmul(
                                        B[QB][:, 0:128], lpt[:], lp[:],
                                        start=True, stop=True, skip_group_check=True)))
                                    t_s2 = tk(sch.op("t", [], lambda e, lp=lp_cur,
                                                     lpt=lpt_cur: e.matmul(
                                        B[QB][:, 128:256], lp[:], lpt[:],
                                        start=True, stop=True, skip_group_check=True)))
                                    bank_write(QB, t_s2)
                                    if lev % 2 == 0:
                                        t_qe = tk(sch.op("v", [t_s2] + pe, lambda e:
                                                         e.tensor_copy(Lp2[:], B[QB][:, 0:256])))
                                    else:
                                        t_qe = tk(sch.op("a", [t_s2] + pe, lambda e:
                                                         e.copy(Lp2[:], B[QB][:, 0:256])))
                                    bank_read(QB, t_qe)
                                    st["lp"] = (Lp2[:, 0:128], t_qe)
                                    st["lpt"] = (Lp2[:, 128:256], t_qe)

                            def s_wf():
                                depsw = bw_deps(XB) + [st["xev"]]
                                t_wf = tk(sch.op("t", depsw, lambda e: e.matmul(
                                    B[XB][:, 256:384], Xsb[:, 128:256], IDENT[:],
                                    start=True, stop=True, skip_group_check=True)))
                                bank_write(XB, t_wf)
                                st["wfe"] = tk(sch.op("v", [t_wf] + pe, lambda e:
                                                      e.tensor_scalar_mul(
                                    WFn[:], B[XB][:, 256:384], -1.0)))
                                bank_read(XB, st["wfe"])

                            def s_scan():
                                t_sb_in = st["sb_in"]
                                depsv = bw_deps(5) + [st["xev"], t_sb_in]
                                t_v1 = tk(sch.op("t", depsv, lambda e: e.matmul(
                                    B[5][:, 0:128], IDENT[:], Xsb[:, 0:128],
                                    start=True, stop=False, skip_group_check=True)))
                                t_v2 = tk(sch.op("t", [st["wfe"]], lambda e: e.matmul(
                                    B[5][:, 0:128], WFn[:], Sb16[:],
                                    start=False, stop=True, skip_group_check=True)))
                                bank_write(5, t_v2)
                                t_vn = tk(sch.op("v", [t_v2] + pe, lambda e:
                                                 e.tensor_copy(vnew[:], B[5][:, 0:128])))
                                bank_read(5, t_vn)
                                depso = bw_deps(XB) + [t_qg, t_sb_in, st["wfe"],
                                                       st["xev"]]
                                t_o1 = tk(sch.op("t", depso, lambda e: e.matmul(
                                    B[XB][:, 384:512], QG[:, csl], Sb16[:],
                                    start=True, stop=False, skip_group_check=True)))
                                t_o2 = tk(sch.op("t", [t_vn, st["at"]], lambda e:
                                                 e.matmul(
                                    B[XB][:, 384:512], attnT[:], vnew[:],
                                    start=False, stop=True, skip_group_check=True)))
                                bank_write(XB, t_o2)
                                t_ce = tk(sch.op("v", [t_o2], lambda e, c=c:
                                                 e.tensor_copy(coretok[:, c, :],
                                                               B[XB][:, 384:512])))
                                bank_read(XB, t_ce)
                                tk(sch.op("a", [t_ce], lambda e, c=c: e.activation(
                                    rmsjunk[:], coretok[:, c, :], ACT.Square,
                                    accum_out=rmsc[:, c:c + 1])))
                                t_kd1 = tk(sch.op("v", t_nd + pe, lambda e, c=c:
                                                  e.tensor_scalar(
                                    colscr[:, cs0:cs0 + 1], gc_col,
                                    Dneg[:, c * 128 + 127:c * 128 + 128],
                                    None, OP.add)))
                                t_kd2 = tk(sch.op("a", [t_kd1] + pe, lambda e:
                                                  e.activation(
                                    colscr[:, cs0 + 1:cs0 + 2], colscr[:, cs0:cs0 + 1],
                                    ACT.Exp, scale=-1.0)))
                                t_egl = tk(sch.op("a", t_nd + pe, lambda e, c=c:
                                                  e.activation(
                                    colscr[:, cs0 + 2:cs0 + 3],
                                    Dneg[:, c * 128 + 127:c * 128 + 128],
                                    ACT.Exp, scale=-1.0)))
                                t_kd3 = tk(sch.op("v", [t_kd2] + t_knt + pe, lambda e:
                                                  e.tensor_scalar(
                                    kdt[:], kntok[:, csl],
                                    colscr[:, cs0 + 1:cs0 + 2], None, OP.mult)))
                                depss = bw_deps(5) + [t_kd3, t_vn]
                                t_su = tk(sch.op("t", depss, lambda e: e.matmul(
                                    B[5][:, 128:256], kdt[:], vnew[:],
                                    start=True, stop=True, skip_group_check=True)))
                                bank_write(5, t_su)
                                t_sn = tk(sch.op("v", [t_su, t_egl], lambda e, h=h:
                                                 e.scalar_tensor_tensor(
                                    S_sb[:, h * 128:(h + 1) * 128],
                                    S_sb[:, h * 128:(h + 1) * 128],
                                    colscr[:, cs0 + 2:cs0 + 3], B[5][:, 128:256],
                                    OP.mult, OP.add)))
                                bank_read(5, t_sn)
                                st["sb_out"] = tk(sch.op("v", [t_sn], lambda e, h=h:
                                                         e.tensor_copy(
                                    Sb16[:], S_sb[:, h * 128:(h + 1) * 128])))

                            return st, s_prep, s_grams, s_x0, s_lev, s_wf, s_scan

                        t_sb_chain = t_sb
                        for c0 in range(0, LIM_CHUNKS, 2):
                            cA, cB = c0, c0 + 1
                            has_b = cB < LIM_CHUNKS
                            stA, prepA, gramsA, x0A, levA, wfA, scanA = emit_chunk(cA)
                            if has_b:
                                stB, prepB, gramsB, x0B, levB, wfB, scanB = emit_chunk(cB)
                            prepA()
                            if has_b:
                                prepB()
                            drain(2)
                            gramsA()
                            if has_b:
                                gramsB()
                            drain(2)
                            x0A()
                            if has_b:
                                x0B()
                            drain(1)
                            for lev in range(7):
                                levA(lev)
                                drain(1)
                                if has_b:
                                    levB(lev)
                                drain(1)
                            wfA()
                            if has_b:
                                wfB()
                            drain(2)
                            stA["sb_in"] = t_sb_chain
                            scanA()
                            drain(2)
                            t_sb_chain = stA["sb_out"]
                            if has_b:
                                stB["sb_in"] = t_sb_chain
                                scanB()
                                drain(2)
                                t_sb_chain = stB["sb_out"]
                            prev_end[cA % 2] = stA["toks"]
                            if has_b:
                                prev_end[cB % 2] = stB["toks"]


                        drain(len(fillq), 0)
                        t_zs = cur_stp["zs"]
                        chunks_done = sch.fence()
                        gate_toks = []
                        if h == NH - 1:
                            tap("t_s", S_sb[:], chunks_done)
                        t_rstd = sch.op("v", chunks_done, lambda e: e.tensor_scalar_mul(
                            rmsc[:], rmsc[:], 1.0 / 128.0))
                        t_rstd = sch.op("a", [t_rstd], lambda e: e.activation(
                            rmsc[:], rmsc[:], ACT.Ln, bias=EPS_L2[:, 0:1]))
                        t_rstd = sch.op("a", [t_rstd], lambda e: e.activation(
                            rmsc[:], rmsc[:], ACT.Exp, scale=-0.5))
                        for c in range(LIM_CHUNKS):
                            t_gt = sch.op("v", [t_rstd], lambda e, c=c: e.tensor_scalar(
                                coretok[:, c, :], coretok[:, c, :], rmsc[:, c:c + 1],
                                None, OP.mult))
                            bi = 2 + (c % 2)
                            depsc = bw_deps(bi) + [t_gt]
                            t_tp = sch.op("t", depsc, lambda e, c=c, bi=bi: e.matmul(
                                B[bi][:, 0:128], coretok[:, c, :], IDENT[:],
                                start=True, stop=True))
                            bank_write(bi, t_tp)
                            t_cf = sch.op("v", [t_tp, t_zs], lambda e, c=c, h=h, bi=bi:
                                          e.tensor_mul(coreF[:, h, c * 128:(c + 1) * 128],
                                                       B[bi][:, 0:128],
                                                       zsil[:, c * 128:(c + 1) * 128]))
                            bank_read(bi, t_cf)
                            gate_toks.append(t_cf)

                        head_state["prev_gate"] = gate_toks
                        fillq.extend(zthN)
                        stpA = stpN

                heads_done = sch.fence()
                tap("t_coref", coreF[:], heads_done)

                with ExitStack() as wo_scope:
                    wo_sb = wo_scope.enter_context(
                        nc.sbuf_tensor(f"wo_p{p}", [128, 2, KC, 256], BF16))
                    xog = wo_scope.enter_context(
                        nc.sbuf_tensor(f"xog_p{p}", [128, 2, 256], F32))
                    t_wo = [None, None]
                    t_wo_rd = [None, None]
                    t_xog = [None, None]
                    for og in range(2 * NOG if not SKIP_WOUT else 0):
                        ws = og % 2
                        dep_wo = ([t_wo[ws]] if t_wo[ws] else heads_done) + \
                            ([t_wo_rd[ws]] if t_wo_rd[ws] else [])
                        t_l = sch.dma("g", f"dwo{p}s{ws}", dep_wo, lambda e, og=og, ws=ws:
                                      e.dma_start(
                            out=wo_sb[:, ws],
                            in_=wout_d[:, og * 256:(og + 1) * 256]
                            .rearrange("(k p) c -> p k c", p=128)))
                        t_wos = sch.op("v", [t_l], lambda e, ws=ws: e.tensor_scalar_mul(
                            wo_sb[:, ws].rearrange("p k c -> p (k c)"),
                            wo_sb[:, ws].rearrange("p k c -> p (k c)"), nw_sb[:]))
                        t_wo[ws] = t_wos
                        for tc in range(NTC):
                            oslot = tc % 2
                            dep_x = [t_xog[oslot]] if t_xog[oslot] else list(heads_done)
                            t_x = sch.dma("s", f"dxo{p}s{oslot}", dep_x,
                                          lambda e, tc=tc, og=og, oslot=oslot:
                                          e.dma_start(
                                out=xog[:, oslot, :],
                                in_=x_d[t0 + tc * 128: t0 + (tc + 1) * 128,
                                        og * 256:(og + 1) * 256]))
                            bslot = 4 + (tc % 2)
                            deps0 = bw_deps(bslot) + [t_wos] + heads_done
                            tk = None
                            for k in range(KC):
                                tk = sch.op("t", deps0 if k == 0 else [],
                                            lambda e, k=k, tc=tc, bslot=bslot, ws=ws:
                                            e.matmul(
                                    B[bslot][:, 0:256],
                                    coreF[:, k, tc * 128:(tc + 1) * 128],
                                    wo_sb[:, ws, k, :], start=(k == 0),
                                    stop=(k == KC - 1)))
                            bank_write(bslot, tk)
                            t_wo_rd[ws] = tk
                            t_e = sch.op("v", [tk, t_x], lambda e, tc=tc, og=og,
                                         oslot=oslot, bslot=bslot: e.tensor_add(
                                acc[:, tc, og * 256:(og + 1) * 256],
                                B[bslot][:, 0:256], xog[:, oslot, :]))
                            bank_read(bslot, t_e)
                            t_xog[oslot] = t_e
                wo_done = sch.fence()

                with ExitStack() as ffn_scope:
                    fsb = lambda n, s, d: ffn_scope.enter_context(
                        nc.sbuf_tensor(f"{n}_p{p}", s, d))
                    acts = fsb("acts", [128, 2, 8, T], BF16)
                    wgu_sb = fsb("wgu", [128, 2, 2, KC, 128], BF16)
                    wd_sb = fsb("wd", [128, 2, 2, 8, 256], BF16)
                    obuf = fsb("obuf", [128, 2, 512], F32)
                    gsil = fsb("gsil", [128, 512], BF16)
                    upsc = fsb("upsc", [128, 512], BF16)
                    cwbt = fsb("cwbt", [128, T], BF16)

                    t_wgu = None
                    wgu_tok = [None, None]
                    wgu_rd = [None, None]
                    t_ob = [None, None]
                    t_ac_last = None
                    wd_toks = {}

                    for gi, grp in enumerate(EGRPS if not SKIP_FFN else []):
                        last_grp = gi == len(EGRPS) - 1
                        gfence = sch.fence()
                        act_toks = []
                        for ei, ex in enumerate(grp):
                            t_cb = []
                            for n in range(2):
                                deps = bw_deps(0) + comb_done + gfence
                                t_mm = sch.op("t", deps, lambda e, ex=ex, n=n: e.matmul(
                                    B[0][:, 0:512], SEL8B[:, ex * 128:(ex + 1) * 128],
                                    CW8[:, n * 512:(n + 1) * 512],
                                    start=True, stop=True))
                                bank_write(0, t_mm)
                                t_e = sch.op("v", [t_mm], lambda e, n=n: e.tensor_copy(
                                    cwbt[:, n * 512:(n + 1) * 512], B[0][:, 0:512]))
                                bank_read(0, t_e)
                                t_cb.append(t_e)
                            for fc in range(8):
                                fslot = fc % 2
                                dep_w = ([wgu_tok[fslot]] if wgu_tok[fslot] else gfence) + \
                                    ([wgu_rd[fslot]] if wgu_rd[fslot] else [])
                                t_wg1 = sch.dma("g", f"dwg{p}s{fslot}", dep_w,
                                                lambda e, ex=ex, fc=fc, fslot=fslot:
                                                e.dma_start(
                                    out=wgu_sb[:, fslot, 0, :, :],
                                    in_=wg_d[ex, :, fc * 128:(fc + 1) * 128]
                                    .rearrange("(k p) c -> p k c", p=128)))
                                t_wgu = sch.dma("g", f"dwu{p}s{fslot}", [],
                                                lambda e, ex=ex, fc=fc, fslot=fslot:
                                                e.dma_start(
                                    out=wgu_sb[:, fslot, 1, :, :],
                                    in_=wu_d[ex, :, fc * 128:(fc + 1) * 128]
                                    .rearrange("(k p) c -> p k c", p=128)))
                                wgu_tok[fslot] = t_wgu
                                gb = (1, 2) if fc % 2 == 0 else (4, 5)
                                ub = (3, 6) if fc % 2 == 0 else (0, 7)
                                depsk = (bw_deps(gb[0]) + bw_deps(gb[1]) +
                                         bw_deps(ub[0]) + bw_deps(ub[1]) +
                                         [t_wg1, t_wgu])
                                tg = tu = None
                                for k in range(KC):
                                    st = k == 0
                                    sp = k == KC - 1
                                    for n in range(2):
                                        tg = sch.op("t", depsk if (k == 0 and n == 0)
                                                    else [],
                                                    lambda e, k=k, n=n, fslot=fslot,
                                                    st=st, sp=sp, gb=gb: e.matmul(
                                            B[gb[n]][:, 0:512],
                                            wgu_sb[:, fslot, 0, k, :],
                                            xnF[:, k, n * 512:(n + 1) * 512],
                                            start=st, stop=sp,
                                            skip_group_check=True))
                                    for n in range(2):
                                        tu = sch.op("t", [], lambda e, k=k, n=n,
                                                    fslot=fslot, st=st, sp=sp, ub=ub:
                                                    e.matmul(
                                            B[ub[n]][:, 0:512],
                                            wgu_sb[:, fslot, 1, k, :],
                                            xnF[:, k, n * 512:(n + 1) * 512],
                                            start=st, stop=sp,
                                            skip_group_check=True))
                                for bb in (*gb, *ub):
                                    bank_write(bb, tu)
                                wgu_rd[fslot] = tu
                                for n in range(2):
                                    t_sg = sch.op("a", [tu, t_ac_last],
                                                  lambda e, n=n, gb=gb: e.activation(
                                        gsil[:], B[gb[n]][:, 0:512], ACT.Silu))
                                    bank_read(gb[n], t_sg)
                                    t_up = sch.op("v", [tu, t_ac_last] + t_cb,
                                                  lambda e, n=n, ub=ub: e.tensor_mul(
                                        upsc[:], B[ub[n]][:, 0:512],
                                        cwbt[:, n * 512:(n + 1) * 512]))
                                    bank_read(ub[n], t_up)
                                    t_ac = sch.op("v", [t_sg, t_up],
                                                  lambda e, ei=ei, fc=fc, n=n:
                                                  e.tensor_mul(
                                        acts[:, ei, fc, n * 512:(n + 1) * 512],
                                        gsil[:], upsc[:]))
                                    act_toks.append(t_ac)
                                    t_ac_last = t_ac
                        if gi == 0:
                            tap("t_acts", acts[:], act_toks)

                        t_down_last = [None, None]
                        for og in range(2 * NOG):
                            ws = og % 2
                            new_wd = []
                            for ei, ex in enumerate(grp):
                                dep_wd = ([wd_toks[(ws, ei)]] if wd_toks.get((ws, ei))
                                          else list(wo_done)) + \
                                    ([t_down_last[ws]] if t_down_last[ws] else [])
                                wd_toks[(ws, ei)] = sch.dma(
                                    "g", f"dwd{p}s{ws}e{ei}", dep_wd,
                                    lambda e, ei=ei, ex=ex, og=og, ws=ws: e.dma_start(
                                        out=wd_sb[:, ws, ei, :, :],
                                        in_=wd_d[ex, :, og * 256:(og + 1) * 256]
                                        .rearrange("(f p) c -> p f c", p=128)))
                                new_wd.append(wd_toks[(ws, ei)])
                            for tc in range(NTC):
                                bslot = 4 + (tc % 2)
                                nmm = len(grp) * 8
                                deps0 = bw_deps(bslot) + act_toks + new_wd + wo_done
                                tk = None
                                i = 0
                                for ei in range(len(grp)):
                                    for fc in range(8):
                                        first = i == 0
                                        last = i == nmm - 1
                                        tk = sch.op("t", deps0 if first else [],
                                                    lambda e, ei=ei, fc=fc, tc=tc,
                                                    bslot=bslot, first=first, last=last,
                                                    ws=ws: e.matmul(
                                            B[bslot][:, 0:256],
                                            acts[:, ei, fc, tc * 128:(tc + 1) * 128],
                                            wd_sb[:, ws, ei, fc, :],
                                            start=first, stop=last,
                                            skip_group_check=True))
                                        i += 1
                                t_down_last[ws] = tk
                                bank_write(bslot, tk)
                                if not last_grp:
                                    t_e = sch.op("v", [tk], lambda e, tc=tc, og=og,
                                                 bslot=bslot: e.tensor_add(
                                        acc[:, tc, og * 256:(og + 1) * 256],
                                        acc[:, tc, og * 256:(og + 1) * 256],
                                        B[bslot][:, 0:256]))
                                    bank_read(bslot, t_e)
                                else:
                                    oslot = tc % 2
                                    dep_o = [t_ob[oslot]] if t_ob[oslot] else []
                                    t_s1 = sch.op("v", [tk] + dep_o,
                                                  lambda e, tc=tc, og=og, oslot=oslot,
                                                  bslot=bslot: e.tensor_add(
                                        obuf[:, oslot, 0:256], B[bslot][:, 0:256],
                                        acc[:, tc, og * 256:(og + 1) * 256]))
                                    bank_read(bslot, t_s1)
                                    t_st = sch.dma("s", f"do{p}s{oslot}", [t_s1],
                                                   lambda e, tc=tc, og=og, oslot=oslot:
                                                   e.dma_start(
                                        out=out_d[t0 + tc * 128: t0 + (tc + 1) * 128,
                                                  og * 256:(og + 1) * 256],
                                        in_=obuf[:, oslot, 0:256]))
                                    t_ob[oslot] = t_st

                sch.flush(block)

        sch.raw("s", sch.fence(), lambda e: e.nop())
        sch.flush(block)

    return nc


def _prep_inputs(inp):
    x = np.ascontiguousarray(inp["L0_attn_hidden_states"])
    wba = np.ascontiguousarray(inp["L0_attn_W_ba"])
    wba_re = np.ascontiguousarray(wba[:, list(range(0, 32, 2)) + list(range(1, 32, 2))])
    cw = np.ascontiguousarray(inp["L0_attn_conv_weight"])
    convw_re = np.zeros((128, 192), np.float32)
    for part in range(3):
        for h in range(NH):
            blk = cw[part * 2048 + h * 128: part * 2048 + (h + 1) * 128, :]
            convw_re[:, (part * 16 + h) * 4:(part * 16 + h) * 4 + 4] = blk
    shared = {
        "wqkvz": np.ascontiguousarray(inp["L0_attn_W_qkvz"]),
        "wba": wba_re,
        "convw": convw_re,
        "dtb": np.ascontiguousarray(inp["L0_attn_dt_bias"].reshape(16, 1)),
        "alog": np.ascontiguousarray(inp["L0_attn_A_log"].reshape(16, 1)),
        "nw": np.ascontiguousarray(inp["L0_attn_norm_weight"].reshape(128, 1)),
        "wout": np.ascontiguousarray(inp["L0_attn_W_out"]),
        "rw": np.ascontiguousarray(inp["L0_ffn_router_w"]),
        "wg": np.ascontiguousarray(inp["L0_ffn_gate_proj"]),
        "wu": np.ascontiguousarray(inp["L0_ffn_up_proj"]),
        "wd": np.ascontiguousarray(inp["L0_ffn_down_proj"]),
    }
    in_maps = []
    for c in range(NCORES):
        m = dict(shared)
        m["x"] = np.ascontiguousarray(x[c])
        in_maps.append(m)
    return in_maps


def kernel(**inputs):
    in_maps = _prep_inputs(inputs)
    nc = build_nc()
    res = run_bass_kernel_spmd(nc, in_maps, core_ids=list(range(NCORES)))
    out = np.stack([np.asarray(r["out"]) for r in res.results], axis=0)
    return out.astype(np.float32)


# version 15
# speedup vs baseline: 1.0022x; 1.0022x over previous
from contextlib import ExitStack

import numpy as np
import ml_dtypes

import concourse.bass as bass
import concourse.mybir as mybir
from concourse.bass_utils import run_bass_kernel_spmd

F32 = mybir.dt.float32
BF16 = mybir.dt.bfloat16
OP = mybir.AluOpType
ACT = mybir.ActivationFunctionType
AX = mybir.AxisListType.X

NCORES = 8
SEQ = 2048
D = 2048
NH = 16
DK = 128
NPASS = 2
T = SEQ // NPASS
CH = 128
NCH = T // CH
NTC = T // 128
KC = D // 128
FF = 1024
NEXP = 8
EGRPS = [(0, 1), (2, 3), (4, 5), (6, 7)]
NOG = D // 512

DEBUG_TAPS = ()
LIM_HEADS = NH
LIM_CHUNKS = NCH
CHUNK_LVL = 6
SKIP_FFN = False
SKIP_WOUT = False

ENGS = ("t", "v", "a", "g", "s")


class Tok:
    __slots__ = ("sem", "val")

    def __init__(self, sem, val):
        self.sem = sem
        self.val = val


class Sched:
    def __init__(self, nc, ctx):
        self.nc = nc
        self.ctx = ctx
        self.sems = {}
        self.counts = {}
        self.dma_streams = set()
        for name in ("t", "v", "a", "g", "s"):
            self.sems[name] = ctx.enter_context(nc.semaphore(f"sem_{name}"))
            self.counts[name] = 0
        self.hw = {e: {} for e in ENGS}
        self.pend = {e: [] for e in ENGS}

    def stream(self, name):
        if name not in self.sems:
            self.sems[name] = self.ctx.enter_context(self.nc.semaphore(f"sem_{name}"))
            self.counts[name] = 0
            self.dma_streams.add(name)
        return name

    def _waits(self, eng, deps):
        want = {}
        for d in deps:
            if d is None:
                continue
            if d.sem == eng and eng == "t":
                continue
            v = self.counts[d.sem] if d.sem in self.dma_streams else d.val
            if want.get(d.sem, 0) < v:
                want[d.sem] = v
        ws = []
        for s, v in want.items():
            if self.hw[eng].get(s, 0) >= v:
                continue
            self.hw[eng][s] = v
            ws.append((self.sems[s], v))
        return ws

    def op(self, eng, deps, fn):
        ws = self._waits(eng, deps)
        self.counts[eng] += 1
        val = self.counts[eng]
        sem = self.sems[eng]

        def run(e, ws=ws, fn=fn, sem=sem):
            for s, v in ws:
                e.wait_ge(s, v)
            fn(e).then_inc(sem, 1)

        self.pend[eng].append(run)
        return Tok(eng, val)

    def dma(self, eng, stream, deps, fn):
        stream = self.stream(stream)
        ws = self._waits(eng, deps)
        self.counts[stream] += 16
        val = self.counts[stream]
        sem = self.sems[stream]

        def run(e, ws=ws, fn=fn, sem=sem):
            for s, v in ws:
                e.wait_ge(s, v)
            fn(e).then_inc(sem, 16)

        self.pend[eng].append(run)
        return Tok(stream, val)

    def raw(self, eng, deps, fn):
        ws = self._waits(eng, deps)

        def run(e, ws=ws, fn=fn):
            for s, v in ws:
                e.wait_ge(s, v)
            fn(e)

        self.pend[eng].append(run)

    def fence(self):
        return [Tok(k, self.counts[k]) for k in self.counts]

    def flush(self, block):
        for eng, fns in self.pend.items():
            if not fns:
                continue
            meth = {"t": block.tensor, "v": block.vector, "a": block.scalar,
                    "g": block.gpsimd, "s": block.sync}[eng]

            def body(e, fns=list(fns)):
                for f in fns:
                    f(e)
            meth(body)
            self.pend[eng] = []


def _host_consts():
    bf = ml_dtypes.bfloat16
    i = np.arange(CH)
    c = {}
    c["IDENT"] = np.eye(128, dtype=bf)
    c["IDENT16F"] = np.eye(16, dtype=np.float32)
    c["M_SL"] = (i[:, None] > i[None, :]).astype(bf)
    c["M_SU"] = (i[:, None] < i[None, :]).astype(bf)
    c["M_IU"] = (i[:, None] <= i[None, :]).astype(bf)
    c["MB_SL"] = np.where(i[:, None] > i[None, :], 0.0, -1e30).astype(np.float32)
    c["MB_IU"] = np.where(i[:, None] <= i[None, :], 0.0, -1e30).astype(np.float32)
    sel16 = np.zeros((NH, NH, 128), dtype=bf)
    for h in range(NH):
        sel16[h, h, :] = 1.0
    c["SEL16B"] = sel16
    nsel16 = np.zeros((NH, NH, 128), dtype=np.float32)
    for h in range(NH):
        nsel16[h, h, :] = -1.0
    c["NSEL16F"] = nsel16
    sel8 = np.zeros((NEXP, NEXP, 128), dtype=bf)
    for e in range(NEXP):
        sel8[e, e, :] = 1.0
    c["SEL8B"] = sel8
    c["ONES1"] = np.ones((1, 128), dtype=bf)
    c["ONESC"] = np.full((1, 128), DK ** -0.5, dtype=bf)
    c["ONECOL"] = np.ones((128, 1), dtype=bf)
    c["CONSTS"] = np.broadcast_to(
        np.array([1e-5, 1e-6, 1.0, 0.0], np.float32), (128, 4)).copy()
    return c


def build_nc():
    nc = bass.Bass("TRN2", target_bir_lowering=False)

    x_d = nc.dram_tensor("x", [SEQ, D], F32, kind="ExternalInput").ap()
    wqkvz_d = nc.dram_tensor("wqkvz", [D, 8192], F32, kind="ExternalInput").ap()
    wba_d = nc.dram_tensor("wba", [D, 32], F32, kind="ExternalInput").ap()
    convw_d = nc.dram_tensor("convw", [128, 192], F32, kind="ExternalInput").ap()
    dtb_d = nc.dram_tensor("dtb", [16, 1], F32, kind="ExternalInput").ap()
    alog_d = nc.dram_tensor("alog", [16, 1], F32, kind="ExternalInput").ap()
    nw_d = nc.dram_tensor("nw", [128, 1], F32, kind="ExternalInput").ap()
    wout_d = nc.dram_tensor("wout", [D, D], F32, kind="ExternalInput").ap()
    rw_d = nc.dram_tensor("rw", [D, 8], F32, kind="ExternalInput").ap()
    wg_d = nc.dram_tensor("wg", [NEXP, D, FF], F32, kind="ExternalInput").ap()
    wu_d = nc.dram_tensor("wu", [NEXP, D, FF], F32, kind="ExternalInput").ap()
    wd_d = nc.dram_tensor("wd", [NEXP, FF, D], F32, kind="ExternalInput").ap()
    out_d = nc.dram_tensor("out", [SEQ, D], F32, kind="ExternalOutput").ap()

    taps = {}
    tapshapes = {
        "t_xnf": ([128, KC, T], BF16), "t_gc16": ([16, T], F32),
        "t_b16": ([16, T], F32), "t_qf": ([128, T], BF16),
        "t_kf": ([128, T], BF16), "t_vf": ([128, T], BF16),
        "t_coref": ([128, NH, T], BF16), "t_cw8": ([8, T], BF16),
        "t_acts": ([128, 2, 8, T], BF16), "t_zsil": ([128, T], BF16),
        "t_kn": ([128, T], BF16), "t_x0": ([128, 256], BF16),
        "t_xfin": ([128, 256], BF16), "t_s": ([128, NH * 128], F32),
        "t_attnt": ([128, CH], BF16), "t_lc": ([128, CH], BF16),
    }
    for name in DEBUG_TAPS:
        shp, dt = tapshapes[name]
        taps[name] = nc.dram_tensor(name, shp, dt, kind="ExternalOutput").ap()

    hc = _host_consts()
    inl = {k: nc.inline_tensor(v, name=k.lower()) for k, v in hc.items()}

    ctx = ExitStack()
    with ctx:
        sb = lambda name, shape, dt: ctx.enter_context(nc.sbuf_tensor(name, shape, dt))
        sch = Sched(nc, ctx)
        block = ctx.enter_context(nc.Block())

        B = [ctx.enter_context(nc.psum_tensor(f"bank{i}", [128, 512], F32))
             for i in range(8)]
        bank_w = [None] * 8
        bank_r = [[] for _ in range(8)]

        def bank_write(i, tok):
            bank_w[i] = tok
            bank_r[i] = []

        def bw_deps(i):
            return list(bank_r[i]) + ([bank_w[i]] if bank_w[i] else [])

        def bank_read(i, tok):
            bank_r[i].append(tok)

        def tap(name, src_ap, deps):
            if name in taps:
                return [sch.dma("s", "dtap", deps, lambda e, a=src_ap, d=taps[name]:
                                e.dma_start(out=d, in_=a))]
            return []

        IDENT = sb("c_ident", [128, 128], BF16)
        IDENT16F = sb("c_id16f", [16, 16], F32)
        M_SL = sb("c_msl", [128, 128], BF16)
        M_SU = sb("c_msu", [128, 128], BF16)
        M_IU = sb("c_miu", [128, 128], BF16)
        MB_SL = sb("c_mbsl", [128, 128], F32)
        MB_IU = sb("c_mbiu", [128, 128], F32)
        SEL16B = sb("c_sel16", [16, NH * 128], BF16)
        NSEL16F = sb("c_nsel16", [16, NH * 128], F32)
        SEL8B = sb("c_sel8", [8, NEXP * 128], BF16)
        ONES1 = sb("c_ones1", [1, 128], BF16)
        ONESC = sb("c_onesc", [1, 128], BF16)
        ONECOL = sb("c_onecol", [128, 1], BF16)
        CONSTS = sb("c_consts", [128, 4], F32)

        wba_sb = sb("wba_sb", [128, KC, 32], BF16)
        rw_sb = sb("rw_sb", [128, KC, 8], BF16)
        convw_sb = sb("convw_sb", [128, 192], F32)
        dtb_sb = sb("dtb_sb", [16, 1], F32)
        alog_sb = sb("alog_sb", [16, 1], F32)
        negea_sb = sb("negea_sb", [16, 1], F32)
        nw_sb = sb("nw_sb", [128, 1], F32)

        xnF = sb("xnF", [128, KC, T], BF16)
        coreF = sb("coreF", [128, NH, T], BF16)
        S_sb = sb("S_sb", [128, NH * 128], F32)
        halo = sb("halo", [128, 48, 3], BF16)

        B16b = sb("B16b", [16, T], BF16)
        GC16 = sb("GC16", [16, T], F32)
        EGC16b = sb("EGC16b", [16, T], BF16)
        GCtok = sb("GCtok", [128, NCH * 16], F32)
        Btok = sb("Btok", [128, NCH * 16], F32)
        EGCtok = sb("EGCtok", [128, NCH * 16], F32)
        BEGCtok = sb("BEGCtok", [128, NCH * 16], F32)
        CW8 = sb("CW8", [8, T], BF16)

        f0 = sch.fence()
        ld = []
        for name, dst in [("IDENT", IDENT), ("IDENT16F", IDENT16F), ("M_SL", M_SL),
                          ("M_SU", M_SU), ("M_IU", M_IU), ("MB_SL", MB_SL),
                          ("MB_IU", MB_IU), ("ONES1", ONES1),
                          ("ONESC", ONESC), ("ONECOL", ONECOL), ("CONSTS", CONSTS)]:
            src = inl[name].ap()
            ld.append(sch.dma("s", "dc", f0, lambda e, s=src, d=dst: e.dma_start(out=d[:], in_=s)))
        for name, dst, nh in [("SEL16B", SEL16B, NH), ("NSEL16F", NSEL16F, NH),
                              ("SEL8B", SEL8B, NEXP)]:
            src = inl[name].ap().rearrange("h p m -> p h m")
            d3 = dst[:].rearrange("p (h m) -> p h m", h=nh)
            ld.append(sch.dma("s", "dc", f0, lambda e, s=src, d=d3: e.dma_start(out=d, in_=s)))
        ld.append(sch.dma("g", "dcg", f0, lambda e: e.dma_start(
            out=wba_sb[:], in_=wba_d.rearrange("(k p) c -> p k c", p=128))))
        ld.append(sch.dma("g", "dcg", f0, lambda e: e.dma_start(
            out=rw_sb[:], in_=rw_d.rearrange("(k p) c -> p k c", p=128))))
        ld.append(sch.dma("s", "dc", f0, lambda e: e.dma_start(out=convw_sb[:], in_=convw_d)))
        ld.append(sch.dma("s", "dc", f0, lambda e: e.dma_start(out=dtb_sb[:], in_=dtb_d)))
        ld.append(sch.dma("s", "dc", f0, lambda e: e.dma_start(out=alog_sb[:], in_=alog_d)))
        ld.append(sch.dma("s", "dc", f0, lambda e: e.dma_start(out=nw_sb[:], in_=nw_d)))
        t_ea = sch.op("a", ld, lambda e: e.activation(negea_sb[:], alog_sb[:], ACT.Exp))
        t_ea = sch.op("v", [t_ea], lambda e: e.tensor_scalar_mul(negea_sb[:], negea_sb[:], -1.0))
        sch.op("v", f0, lambda e: e.memset(S_sb[:], 0.0))
        sch.op("v", f0, lambda e: e.memset(halo[:], 0.0))

        EPS_LN = CONSTS[:, 0:1]
        EPS_L2 = CONSTS[:, 1:2]
        ONE_C = CONSTS[:, 2:3]

        sch.flush(block)

        for p in range(NPASS):
            t0 = p * T
            with ExitStack() as pass_ctx:
                psb = lambda n, s, d: pass_ctx.enter_context(
                    nc.sbuf_tensor(f"{n}_p{p}", s, d))
                acc = psb("acc", [128, NTC, D], BF16)

                pfence = sch.fence()
                ph1_scope = ExitStack()
                p1sb = lambda n, s, d: ph1_scope.enter_context(
                    nc.sbuf_tensor(f"{n}_p{p}", s, d))
                xln = p1sb("xln0", [128, D], F32)
                xnt = p1sb("xnt", [128, D], BF16)
                stat = p1sb("stat", [128, 8], F32)
                B16 = p1sb("B16", [16, T], F32)
                A16 = p1sb("A16", [16, T], F32)
                G16 = p1sb("G16", [16, T], F32)
                xln_tok = None
                prev_rs = None
                prev_xn = None
                prev_tp = None

                for tc in range(NTC):
                    dep = xln_tok if xln_tok else list(pfence)
                    t_x = sch.dma("s", f"dx{p}", dep, lambda e, tc=tc: e.dma_start(
                        out=xln[:], in_=x_d[t0 + tc * 128: t0 + (tc + 1) * 128, :]))
                    t_mean = sch.op("v", [t_x, prev_rs], lambda e: e.tensor_reduce(
                        stat[:, 0:1], xln[:], AX, OP.add))
                    t_sq = sch.op("a", [t_x, prev_xn], lambda e: e.activation(
                        xnt[:], xln[:], ACT.Square, accum_out=stat[:, 1:2]))
                    t_mean = sch.op("v", [t_mean], lambda e: e.tensor_scalar_mul(
                        stat[:, 0:1], stat[:, 0:1], 1.0 / D))
                    t_msq = sch.op("v", [t_mean], lambda e: e.tensor_mul(
                        stat[:, 2:3], stat[:, 0:1], stat[:, 0:1]))
                    t_var = sch.op("v", [t_msq, t_sq, prev_rs], lambda e: e.scalar_tensor_tensor(
                        stat[:, 3:4], stat[:, 1:2], 1.0 / D, stat[:, 2:3],
                        OP.mult, OP.subtract))
                    t_ln = sch.op("a", [t_var], lambda e: e.activation(
                        stat[:, 4:5], stat[:, 3:4], ACT.Ln, bias=EPS_LN))
                    t_rs = sch.op("a", [t_ln, prev_xn], lambda e: e.activation(
                        stat[:, 4:5], stat[:, 4:5], ACT.Exp, scale=-0.5))
                    t_xn = sch.op("v", [t_rs, t_mean, prev_tp, t_sq], lambda e: e.tensor_scalar(
                        xnt[:], xln[:], stat[:, 0:1], stat[:, 4:5],
                        OP.subtract, OP.mult))
                    xln_tok = [t_xn, t_sq]
                    prev_rs = t_rs
                    prev_xn = t_xn
                    for k in range(KC):
                        bi = k % 2
                        t_mm = sch.op("t", [t_xn] + bw_deps(bi),
                                      lambda e, k=k, bi=bi: e.matmul(
                                          B[bi][:, 0:128], xnt[:, k * 128:(k + 1) * 128],
                                          IDENT[:], start=True, stop=True))
                        bank_write(bi, t_mm)
                        eng = "v" if (k % 4) < 2 else "a"
                        if eng == "v":
                            t_e = sch.op("v", [t_mm], lambda e, k=k, tc=tc, bi=bi:
                                         e.tensor_copy(xnF[:, k, tc * 128:(tc + 1) * 128],
                                                       B[bi][:, 0:128]))
                        else:
                            t_e = sch.op("a", [t_mm], lambda e, k=k, tc=tc, bi=bi:
                                         e.copy(xnF[:, k, tc * 128:(tc + 1) * 128],
                                                B[bi][:, 0:128]))
                        bank_read(bi, t_e)
                        prev_tp = t_mm

                xnF_done = sch.fence()
                tap("t_xnf", xnF[:], xnF_done)

                for n in range(2):
                    for part, dst in ((0, B16), (1, A16)):
                        bi = 2 + part
                        deps = bw_deps(bi) + xnF_done
                        tk = None
                        for k in range(KC):
                            tk = sch.op("t", deps if k == 0 else [],
                                        lambda e, k=k, n=n, part=part, bi=bi: e.matmul(
                                B[bi][0:16, 0:512],
                                wba_sb[:, k, 16 * part:16 * part + 16],
                                xnF[:, k, n * 512:(n + 1) * 512],
                                start=(k == 0), stop=(k == KC - 1)))
                        bank_write(bi, tk)
                        t_e = sch.op("v", [tk], lambda e, n=n, dst=dst, bi=bi:
                                     e.tensor_copy(dst[:, n * 512:(n + 1) * 512],
                                                   B[bi][0:16, 0:512]))
                        bank_read(bi, t_e)

                ba_done = sch.fence()
                ph2_scope = ExitStack()
                rsc = ph2_scope.enter_context(nc.sbuf_tensor(f"rsc_p{p}", [128, 8], F32))
                rtmp = ph2_scope.enter_context(nc.sbuf_tensor(f"rtmp_p{p}", [128, 16], F32))
                cwb = ph2_scope.enter_context(nc.sbuf_tensor(f"cwb_p{p}", [128, 8], BF16))
                prev_cw = None
                for tc in range(NTC):
                    deps = bw_deps(6) + xnF_done
                    tk = None
                    for k in range(KC):
                        tk = sch.op("t", deps if k == 0 else [], lambda e, k=k, tc=tc:
                                    e.matmul(B[6][:, 0:8],
                                             xnF[:, k, tc * 128:(tc + 1) * 128],
                                             rw_sb[:, k, :], start=(k == 0),
                                             stop=(k == KC - 1)))
                    bank_write(6, tk)
                    t_e = sch.op("v", [tk, prev_cw] + ba_done, lambda e: e.tensor_copy(
                        rsc[:], B[6][:, 0:8]))
                    bank_read(6, t_e)
                    t_m = sch.op("v", [t_e], lambda e: e.tensor_reduce(
                        rtmp[:, 0:1], rsc[:], AX, OP.max, negate=True))
                    t_x2 = sch.op("a", [t_m], lambda e: e.activation(
                        rsc[:], rsc[:], ACT.Exp, bias=rtmp[:, 0:1]))
                    t_sm = sch.op("v", [t_x2], lambda e: e.tensor_reduce(
                        rtmp[:, 1:2], rsc[:], AX, OP.add))
                    t_r = sch.op("v", [t_sm], lambda e: e.reciprocal(
                        rtmp[:, 2:3], rtmp[:, 1:2]))
                    t_p = sch.op("v", [t_r], lambda e: e.tensor_scalar_mul(
                        rsc[:], rsc[:], rtmp[:, 2:3]))
                    t_m8 = sch.op("v", [t_p], lambda e: e.max(out=rtmp[:, 8:16], in_=rsc[:]))
                    t_dn = sch.op("v", [t_m8], lambda e: e.tensor_add(
                        rtmp[:, 3:4], rtmp[:, 8:9], rtmp[:, 9:10]))
                    t_dn = sch.op("v", [t_dn], lambda e: e.reciprocal(
                        rtmp[:, 4:5], rtmp[:, 3:4]))
                    t_msk = sch.op("v", [t_dn], lambda e: e.tensor_scalar(
                        rtmp[:, 8:16], rsc[:], rtmp[:, 9:10], None, OP.is_ge))
                    t_cw = sch.op("v", [t_msk], lambda e: e.tensor_mul(
                        rsc[:], rsc[:], rtmp[:, 8:16]))
                    t_cw = sch.op("v", [t_cw], lambda e: e.tensor_scalar(
                        cwb[:], rsc[:], rtmp[:, 4:5], None, OP.mult))
                    deps2 = bw_deps(7) + [t_cw]
                    t_tp = sch.op("t", deps2, lambda e: e.matmul(
                        B[7][0:8, 0:128], cwb[:], IDENT[:], start=True, stop=True))
                    bank_write(7, t_tp)
                    t_ec = sch.op("v", [t_tp], lambda e, tc=tc: e.tensor_copy(
                        CW8[:, tc * 128:(tc + 1) * 128], B[7][0:8, 0:128]))
                    bank_read(7, t_ec)
                    prev_cw = t_ec
                comb_done = sch.fence()
                ph2_scope.close()
                tap("t_cw8", CW8[:], comb_done)

                t1 = sch.op("a", ba_done, lambda e: e.activation(
                    B16[:], B16[:], ACT.Exp, scale=-1.0))
                t1 = sch.op("v", [t1], lambda e: e.tensor_scalar_add(B16[:], B16[:], 1.0))
                t1 = sch.op("v", [t1], lambda e: e.reciprocal(B16[:], B16[:]))
                tbb = sch.op("v", [t1], lambda e: e.tensor_copy(B16b[:], B16[:]))
                t2 = sch.op("v", ba_done, lambda e: e.tensor_scalar(
                    A16[:], A16[:], dtb_sb[:], 85.0, OP.add, OP.min))
                t2 = sch.op("a", [t2], lambda e: e.activation(G16[:], A16[:], ACT.Exp))
                t2 = sch.op("a", [t2], lambda e: e.activation(
                    G16[:], G16[:], ACT.Ln, bias=ONE_C[0:16, 0:1]))
                t2 = sch.op("v", [t2, t_ea], lambda e: e.tensor_scalar_mul(
                    G16[:], G16[:], negea_sb[:]))
                tgc = t2
                for c in range(NCH):
                    tgc = sch.op("v", [tgc], lambda e, c=c: e.tensor_tensor_scan(
                        GC16[:, c * CH:(c + 1) * CH], G16[:, c * CH:(c + 1) * CH],
                        G16[:, c * CH:(c + 1) * CH], 0.0, OP.add, OP.bypass))
                tegc = sch.op("a", [tgc], lambda e: e.activation(
                    EGC16b[:], GC16[:], ACT.Exp))
                tap("t_gc16", GC16[:], [tgc])
                tap("t_b16", B16[:], [t1])

                tok_toks = [tgc, t1]
                for c in range(NCH):
                    bi = 4 + (c % 2)
                    for src, dst in ((GC16, GCtok), (B16, Btok)):
                        deps = bw_deps(bi) + [tgc, t1]
                        t_mm = sch.op("t", deps, lambda e, c=c, bi=bi, src=src: e.matmul(
                            B[bi][:, 0:16], src[:, c * CH:(c + 1) * CH],
                            IDENT16F[:], start=True, stop=True))
                        bank_write(bi, t_mm)
                        t_e = sch.op("v", [t_mm], lambda e, c=c, bi=bi, dst=dst:
                                     e.tensor_copy(dst[:, c * 16:(c + 1) * 16],
                                                   B[bi][:, 0:16]))
                        bank_read(bi, t_e)
                        tok_toks.append(t_e)
                t_eg = sch.op("a", tok_toks, lambda e: e.activation(
                    EGCtok[:], GCtok[:], ACT.Exp))
                t_bg = sch.op("v", [t_eg], lambda e: e.tensor_mul(
                    BEGCtok[:], Btok[:], EGCtok[:]))
                scal_done = sch.fence()
                ph1_scope.close()

                with ExitStack() as head_scope:
                    hsb = lambda n, s, d: head_scope.enter_context(
                        nc.sbuf_tensor(f"{n}_p{p}", s, d))
                    wq_buf = [hsb("wq0", [128, KC, 256], BF16),
                              hsb("wq1", [128, KC, 256], BF16)]
                    wq_tok = [None, None]
                    wq_rd = [None, None]
                    xmix = hsb("xmix", [128, 3, T + 3], BF16)
                    zraw = hsb("zraw", [128, T], BF16)
                    zsil = zraw
                    qF = hsb("qF", [128, T], BF16)
                    kF = hsb("kF", [128, T], BF16)
                    vF = hsb("vF", [128, T], BF16)
                    sqs = hsb("sqs", [128, 512], BF16)
                    rrow = hsb("rrow", [1, 2 * T], BF16)
                    diag = hsb("diag", [128, 8, 128], BF16)
                    Dneg = hsb("Dneg", [128, T], F32)
                    BCb = hsb("BCb", [128, T], BF16)
                    BCe = hsb("BCe", [128, T], BF16)
                    BCq = hsb("BCq", [128, T], BF16)
                    BCk = hsb("BCk", [128, T], BF16)
                    KN = hsb("KN", [128, T], BF16)
                    KB = hsb("KB", [128, T], BF16)
                    QN = hsb("QN", [128, T], BF16)
                    QG = hsb("QG", [128, T], BF16)
                    kntok = hsb("kntok", [128, T], BF16)
                    vtok = hsb("vtok", [128, T], BF16)
                    Dch2 = [hsb("Dch0", [128, CH], F32), hsb("Dch1", [128, CH], F32)]
                    ETch2 = [hsb("ETch0", [128, CH], F32), hsb("ETch1", [128, CH], F32)]
                    EmS2 = [hsb("EmS0", [128, CH], BF16), hsb("EmS1", [128, CH], BF16)]
                    ETmU2 = [hsb("ETmU0", [128, CH], BF16), hsb("ETmU1", [128, CH], BF16)]
                    ETmI2 = [hsb("ETmI0", [128, CH], BF16), hsb("ETmI1", [128, CH], BF16)]
                    Lc2 = [hsb("Lc0", [128, CH], BF16), hsb("Lc1", [128, CH], BF16)]
                    LcT2 = [hsb("LcT0", [128, CH], BF16), hsb("LcT1", [128, CH], BF16)]
                    Lp22 = [hsb("Lp20", [128, 2 * CH], BF16), hsb("Lp21", [128, 2 * CH], BF16)]
                    attnT2 = [hsb("attnT0", [128, CH], BF16), hsb("attnT1", [128, CH], BF16)]
                    Xsb2 = [hsb("Xsb0", [128, 256], BF16), hsb("Xsb1", [128, 256], BF16)]
                    WFn2 = [hsb("WFn0", [128, CH], BF16), hsb("WFn1", [128, CH], BF16)]
                    vnew2 = [hsb("vnew0", [128, 128], BF16), hsb("vnew1", [128, 128], BF16)]
                    Sb16 = hsb("Sb16", [128, 128], BF16)
                    kdt2 = [hsb("kdt0", [128, 128], BF16), hsb("kdt1", [128, 128], BF16)]
                    colscr = hsb("colscr", [128, 8], F32)
                    rmsjunk = hsb("rmsjunk", [128, 128], BF16)
                    coretok = hsb("coretok", [128, NCH, 128], BF16)
                    rmsc = hsb("rmsc", [128, NCH], F32)

                    def prefetch_wq(m, deps):
                        slot = m % 2
                        src = wqkvz_d[:, 256 * m:256 * (m + 1)].rearrange(
                            "(k p) c -> p k c", p=128)
                        extra = [wq_rd[slot]] if wq_rd[slot] else []
                        wq_tok[slot] = sch.dma("g", f"dwq{p}s{slot}", deps + extra,
                                               lambda e, s=src, d=wq_buf[slot]:
                                               e.dma_start(out=d[:], in_=s))

                    prefetch_wq(0, scal_done)
                    prefetch_wq(1, scal_done)

                    from collections import deque

                    head_state = {"prev_conv": [], "prev_ho": None,
                                  "prev_gate": [], "sb_chain": None}

                    def make_prepA(h):
                        stp = {"ev": {}, "convt": {}, "conv_mms": []}
                        th = []
                        zth = []
                        gate_x = list(head_state["prev_conv"]) + \
                            ([head_state["prev_ho"]] if head_state["prev_ho"] else [])

                        def t_hi_f(stp=stp, h=h, gate_x=gate_x):
                            stp["hi"] = sch.op("v", gate_x, lambda e: e.tensor_copy(
                                xmix[:, :, 0:3], halo[:, 3 * h:3 * h + 3, :]))
                        th.append(t_hi_f)

                        for ob in range(3):
                            m = 2 * h + (ob // 2)
                            slot = m % 2
                            obh = ob % 2
                            for n in range(2):
                                bi = (ob * 2 + n) % 2
                                for kq in range(4):
                                    def mmf(ob=ob, n=n, bi=bi, kq=kq, slot=slot,
                                            obh=obh, stp=stp, gate_x=gate_x):
                                        wq = wq_buf[slot]
                                        deps = (bw_deps(bi) + [wq_tok[slot]]
                                                ) if kq == 0 else []
                                        tk = None
                                        for k in range(4 * kq, 4 * kq + 4):
                                            tk = sch.op("t", deps if k == 4 * kq == 0 or
                                                        (kq == 0 and k == 0) else [],
                                                        lambda e, k=k, obh=obh, n=n,
                                                        bi=bi, wq=wq: e.matmul(
                                                B[bi][:, 0:512],
                                                wq[:, k, obh * 128:(obh + 1) * 128],
                                                xnF[:, k, n * 512:(n + 1) * 512],
                                                start=(k == 0), stop=(k == KC - 1),
                                                skip_group_check=True))
                                        if kq == 3:
                                            bank_write(bi, tk)
                                            wq_rd[slot] = tk
                                            t_e = sch.op("a", [tk] + gate_x,
                                                         lambda e, ob=ob, n=n, bi=bi:
                                                         e.copy(
                                                xmix[:, ob, 3 + n * 512:
                                                     3 + (n + 1) * 512],
                                                B[bi][:, 0:512]))
                                            bank_read(bi, t_e)
                                            stp["ev"][(ob, n)] = t_e
                                    th.append(mmf)

                        def t_ho_f(stp=stp, h=h):
                            stp["ho"] = sch.op("v", [stp["ev"][(0, 1)], stp["ev"][(1, 1)],
                                                     stp["ev"][(2, 1)], stp["hi"]],
                                               lambda e: e.tensor_copy(
                                halo[:, 3 * h:3 * h + 3, :], xmix[:, :, T:T + 3]))
                        th.append(t_ho_f)

                        prev_cmm = {}
                        for part, dstn in ((0, "qF"), (1, "kF"), (2, "vF")):
                            dst = {"qF": qF, "kF": kF, "vF": vF}[dstn]
                            ps = part % 2

                            def diagf(part=part, ps=ps, h=h, stp=stp):
                                gate_d = list(head_state["prev_conv"]) + \
                                    ([prev_cmm[ps]] if ps in prev_cmm else [])
                                stp[f"dg{part}"] = []
                                for j in range(4):
                                    col = (part * 16 + h) * 4 + j
                                    stp[f"dg{part}"].append(sch.op(
                                        "v", gate_d, lambda e, ps=ps, j=j, col=col:
                                        e.tensor_scalar_mul(diag[:, ps * 4 + j, :],
                                                            IDENT[:],
                                                            convw_sb[:, col:col + 1])))
                            th.append(diagf)
                            for n in range(2):
                                bi = 2 + (part * 2 + n) % 2
                                bi = (part * 2 + n) % 2

                                def convf(part=part, n=n, bi=bi, ps=ps, dst=dst,
                                          dstn=dstn, stp=stp):
                                    deps = (bw_deps(bi) + [stp["hi"],
                                            stp["ev"][(part, 0)], stp["ev"][(part, 1)]]
                                            + stp[f"dg{part}"])
                                    tk = None
                                    for j in range(4):
                                        tk = sch.op("t", deps if j == 0 else [],
                                                    lambda e, j=j, part=part, n=n,
                                                    bi=bi, ps=ps: e.matmul(
                                            B[bi][:, 0:512], diag[:, ps * 4 + j, :],
                                            xmix[:, part, j + n * 512:
                                                 j + n * 512 + 512],
                                            start=(j == 0), stop=(j == 3),
                                            skip_group_check=True))
                                    bank_write(bi, tk)
                                    prev_cmm[ps] = tk
                                    stp["conv_mms"].append(tk)
                                    t_e = sch.op("a", [tk], lambda e, dst=dst, n=n,
                                                 bi=bi: e.activation(
                                        dst[:, n * 512:(n + 1) * 512],
                                        B[bi][:, 0:512], ACT.Silu))
                                    bank_read(bi, t_e)
                                    stp["convt"][(part, n)] = t_e
                                th.append(convf)

                        def l2f(stp=stp):
                            ss_toks = []
                            t_sq2 = None
                            for idx, srcn in enumerate(("qF", "kF")):
                                srct = {"qF": qF, "kF": kF}[srcn]
                                cv = [stp["convt"][(idx, 0)], stp["convt"][(idx, 1)]]
                                for n in range(2):
                                    dep_s = cv + ([t_sq2] if t_sq2 else [])
                                    t_sq2 = sch.op("v", dep_s, lambda e, srct=srct,
                                                   n=n: e.tensor_mul(
                                        sqs[:], srct[:, n * 512:(n + 1) * 512],
                                        srct[:, n * 512:(n + 1) * 512]))
                                    bi = n % 2
                                    deps = bw_deps(bi) + [t_sq2]
                                    t_mm = sch.op("t", deps, lambda e, bi=bi: e.matmul(
                                        B[bi][0:1, 0:512], ONECOL[:], sqs[:],
                                        start=True, stop=True))
                                    bank_write(bi, t_mm)
                                    t_sq2 = t_mm
                                    t_e = sch.op("v", [t_mm], lambda e, idx=idx, n=n,
                                                 bi=bi: e.tensor_copy(
                                        rrow[:, idx * T + n * 512:
                                             idx * T + (n + 1) * 512],
                                        B[bi][0:1, 0:512]))
                                    bank_read(bi, t_e)
                                    ss_toks.append(t_e)
                            t_ln2 = sch.op("a", ss_toks, lambda e: e.activation(
                                rrow[:], rrow[:], ACT.Ln, bias=EPS_L2[0:1, 0:1]))
                            stp["rr"] = sch.op("a", [t_ln2], lambda e: e.activation(
                                rrow[:], rrow[:], ACT.Exp, scale=-0.5))
                        th.append(l2f)

                        if h + 1 < NH:
                            def pf(h=h):
                                prefetch_wq(2 * h + 2, [])
                            th.append(pf)

                        m = 2 * h + 1
                        slot = m % 2
                        for n in range(2):
                            bi = 6 + n
                            bi = n % 2
                            for kq in range(4):
                                def zmf(n=n, bi=bi, kq=kq, slot=slot, stp=stp):
                                    wq = wq_buf[slot]
                                    gate_z = list(head_state["prev_gate"])
                                    deps = (bw_deps(bi) + [wq_tok[slot]]
                                            ) if kq == 0 else []
                                    tk = None
                                    for k in range(4 * kq, 4 * kq + 4):
                                        tk = sch.op("t", deps if (kq == 0 and k == 0)
                                                    else [],
                                                    lambda e, k=k, n=n, bi=bi, wq=wq:
                                                    e.matmul(
                                            B[bi][:, 0:512], wq[:, k, 128:256],
                                            xnF[:, k, n * 512:(n + 1) * 512],
                                            start=(k == 0), stop=(k == KC - 1),
                                            skip_group_check=True))
                                    if kq == 3:
                                        bank_write(bi, tk)
                                        wq_rd[slot] = tk
                                        t_e = sch.op("a", [tk] + gate_z,
                                                     lambda e, n=n, bi=bi: e.copy(
                                            zraw[:, n * 512:(n + 1) * 512],
                                            B[bi][:, 0:512]))
                                        bank_read(bi, t_e)
                                        stp["ev"][(3, n)] = t_e
                                zth.append(zmf)

                        def zsf(stp=stp, h=h):
                            stp["zs"] = sch.op("a", [stp["ev"][(3, 0)],
                                                     stp["ev"][(3, 1)]],
                                               lambda e: e.activation(
                                zsil[:], zraw[:], ACT.Silu))
                            if h + 1 < NH:
                                prefetch_wq(2 * h + 3, [])
                        zth.append(zsf)
                        return th, zth, stp

                    def emit_prepB(h, stp):
                        out = {}
                        conv_q = [stp["convt"][(0, 0)], stp["convt"][(0, 1)]]
                        conv_k = [stp["convt"][(1, 0)], stp["convt"][(1, 1)]]
                        conv_v = [stp["convt"][(2, 0)], stp["convt"][(2, 1)]]
                        gate_b = list(head_state["prev_gate"])
                        t_kn = t_kb = t_qn = t_qg = None
                        for i, (kind, srcf, dstf) in enumerate((
                                ("k", kF, KN), ("b", None, KB),
                                ("q", qF, QN), ("e", None, QG))):
                            for n in range(2):
                                bi = 4 + (n % 2)
                                deps = bw_deps(bi) + [stp["rr"], tbb, tegc] + gate_b
                                if kind == "b":
                                    t_mm = sch.op("t", deps, lambda e, h=h, n=n, bi=bi:
                                                  e.matmul(
                                        B[bi][:, 0:512],
                                        SEL16B[:, h * 128:(h + 1) * 128],
                                        B16b[:, n * 512:(n + 1) * 512],
                                        start=True, stop=True))
                                elif kind == "e":
                                    t_mm = sch.op("t", deps, lambda e, h=h, n=n, bi=bi:
                                                  e.matmul(
                                        B[bi][:, 0:512],
                                        SEL16B[:, h * 128:(h + 1) * 128],
                                        EGC16b[:, n * 512:(n + 1) * 512],
                                        start=True, stop=True))
                                elif kind == "q":
                                    t_mm = sch.op("t", deps, lambda e, n=n, bi=bi:
                                                  e.matmul(
                                        B[bi][:, 0:512], ONESC[:],
                                        rrow[:, n * 512:(n + 1) * 512],
                                        start=True, stop=True))
                                else:
                                    t_mm = sch.op("t", deps, lambda e, n=n, bi=bi:
                                                  e.matmul(
                                        B[bi][:, 0:512], ONES1[:],
                                        rrow[:, T + n * 512: T + (n + 1) * 512],
                                        start=True, stop=True))
                                bank_write(bi, t_mm)
                                if kind == "k":
                                    dep2 = conv_k + [t_mm] + gate_b
                                    t = sch.op("v", dep2, lambda e, n=n, bi=bi:
                                               e.tensor_mul(
                                        KN[:, n * 512:(n + 1) * 512],
                                        kF[:, n * 512:(n + 1) * 512],
                                        B[bi][:, 0:512]))
                                    t_kn = t if n == 1 else t_kn or t
                                    out.setdefault("kn", []).append(t)
                                elif kind == "b":
                                    dep2 = out["kn"] + [t_mm]
                                    t = sch.op("v", dep2, lambda e, n=n, bi=bi:
                                               e.tensor_mul(
                                        KB[:, n * 512:(n + 1) * 512],
                                        KN[:, n * 512:(n + 1) * 512],
                                        B[bi][:, 0:512]))
                                    out.setdefault("kb", []).append(t)
                                elif kind == "q":
                                    dep2 = conv_q + [t_mm] + gate_b
                                    t = sch.op("v", dep2, lambda e, n=n, bi=bi:
                                               e.tensor_mul(
                                        QN[:, n * 512:(n + 1) * 512],
                                        qF[:, n * 512:(n + 1) * 512],
                                        B[bi][:, 0:512]))
                                    out.setdefault("qn", []).append(t)
                                else:
                                    dep2 = out["qn"] + [t_mm]
                                    t = sch.op("v", dep2, lambda e, n=n, bi=bi:
                                               e.tensor_mul(
                                        QG[:, n * 512:(n + 1) * 512],
                                        QN[:, n * 512:(n + 1) * 512],
                                        B[bi][:, 0:512]))
                                    out.setdefault("qg", []).append(t)
                                bank_read(bi, t)
                        out["t_kb"] = out["kb"][-1]
                        out["t_qg"] = out["qg"][-1]

                        t_knt, t_vt = [], []
                        for c in range(NCH):
                            bi = 6 + (c % 2)
                            for srcd, dst, tok_list, dd in (
                                    (KN, kntok, t_knt, out["kn"]),
                                    (vF, vtok, t_vt, conv_v + gate_b)):
                                deps = bw_deps(bi) + dd
                                t_mm = sch.op("t", deps, lambda e, c=c, bi=bi,
                                              srcd=srcd: e.matmul(
                                    B[bi][:, 0:128], srcd[:, c * 128:(c + 1) * 128],
                                    IDENT[:], start=True, stop=True))
                                bank_write(bi, t_mm)
                                if c % 2 == 0:
                                    t_e = sch.op("v", [t_mm], lambda e, c=c, bi=bi,
                                                 dst=dst: e.tensor_copy(
                                        dst[:, c * 128:(c + 1) * 128], B[bi][:, 0:128]))
                                else:
                                    t_e = sch.op("a", [t_mm], lambda e, c=c, bi=bi,
                                                 dst=dst: e.copy(
                                        dst[:, c * 128:(c + 1) * 128], B[bi][:, 0:128]))
                                bank_read(bi, t_e)
                                tok_list.append(t_e)
                        out["t_knt"] = t_knt
                        out["t_vt"] = t_vt

                        t_nd = []
                        for n in range(2):
                            bi = 2 + n
                            deps = bw_deps(bi) + [tgc] + gate_b
                            t_mm = sch.op("t", deps, lambda e, h=h, n=n, bi=bi:
                                          e.matmul(
                                B[bi][:, 0:512], NSEL16F[:, h * 128:(h + 1) * 128],
                                GC16[:, n * 512:(n + 1) * 512],
                                start=True, stop=True))
                            bank_write(bi, t_mm)
                            t_e = sch.op("v", [t_mm] + gate_b, lambda e, n=n, bi=bi:
                                         e.tensor_copy(
                                Dneg[:, n * 512:(n + 1) * 512], B[bi][:, 0:512]))
                            bank_read(bi, t_e)
                            t_nd.append(t_e)
                        out["t_nd"] = t_nd
                        out["t_sb"] = sch.op("v", gate_b + t_nd, lambda e, h=h:
                                             e.tensor_copy(
                            Sb16[:], S_sb[:, h * 128:(h + 1) * 128]))
                        return out

                    thA, zthA, stpA = make_prepA(0)
                    for f in thA:
                        f()
                    for f in zthA:
                        f()

                    fillq = deque()

                    def drain(n=1, reserve=12):
                        for _ in range(n):
                            if len(fillq) > reserve:
                                fillq.popleft()()

                    for h in range(LIM_HEADS):
                        drain(6, 0)
                        while "rr" not in stpA and fillq:
                            fillq.popleft()()
                        pb_out = emit_prepB(h, stpA)
                        t_kb = pb_out["t_kb"]
                        t_qg = pb_out["t_qg"]
                        t_knt = pb_out["t_knt"]
                        t_vt = pb_out["t_vt"]
                        t_nd = pb_out["t_nd"]
                        t_sb = pb_out["t_sb"]
                        head_state["prev_conv"] = list(stpA["conv_mms"])
                        head_state["prev_ho"] = stpA["ho"]
                        cur_stp = stpA
                        if h + 1 < LIM_HEADS:
                            thN, zthN, stpN = make_prepA(h + 1)
                            fillq.extend(thN)
                        else:
                            thN, zthN, stpN = [], [], None
                        prev_end = [[], []]

                        def emit_chunk(c):
                            pb = c % 2
                            XB = 3 if pb == 0 else 6
                            QB = 4 if pb == 0 else 7
                            pe = prev_end[pb]
                            Dch, ETch = Dch2[pb], ETch2[pb]
                            EmS, ETmU, ETmI = EmS2[pb], ETmU2[pb], ETmI2[pb]
                            Lc, LcT, Lp2 = Lc2[pb], LcT2[pb], Lp22[pb]
                            attnT, Xsb, WFn = attnT2[pb], Xsb2[pb], WFn2[pb]
                            vnew, kdt = vnew2[pb], kdt2[pb]
                            cs0 = 4 * pb
                            gc_col = GCtok[:, c * 16 + h: c * 16 + h + 1]
                            b_col = Btok[:, c * 16 + h: c * 16 + h + 1]
                            be_col = BEGCtok[:, c * 16 + h: c * 16 + h + 1]
                            csl = slice(c * 128, (c + 1) * 128)
                            st = {"toks": []}

                            def tk(t):
                                st["toks"].append(t)
                                return t

                            def s_prep():
                                st["da"] = tk(sch.op("v", t_nd + pe, lambda e:
                                                     e.tensor_scalar(
                                    Dch[:], Dneg[:, csl], gc_col, None, OP.add)))
                                e2p = tk(sch.op("v", [st["da"]] + pe, lambda e:
                                                e.scalar_tensor_tensor(
                                    ETch[:], Dch[:], -1.0, MB_IU[:],
                                    OP.mult, OP.add)))
                                e1p = tk(sch.op("v", [e2p], lambda e:
                                                e.tensor_add(Dch[:], Dch[:], MB_SL[:])))
                                st["m1"] = tk(sch.op("a", [e1p] + pe, lambda e:
                                                     e.activation(EmS[:], Dch[:], ACT.Exp)))
                                st["m3"] = tk(sch.op("a", [e2p] + pe, lambda e:
                                                     e.activation(ETmI[:], ETch[:], ACT.Exp)))
                                st["m2"] = tk(sch.op("v", [st["m3"]] + pe, lambda e:
                                                     e.tensor_mul(ETmU[:], ETmI[:], M_SU[:])))

                            def s_grams():
                                depsg = bw_deps(2) + [t_kb, t_qg]
                                g2 = tk(sch.op("t", depsg, lambda e: e.matmul(
                                    B[2][:, 0:128], KB[:, csl], KN[:, csl],
                                    start=True, stop=True)))
                                g2t = tk(sch.op("t", [], lambda e: e.matmul(
                                    B[2][:, 128:256], KN[:, csl], KB[:, csl],
                                    start=True, stop=True)))
                                aqk = tk(sch.op("t", [], lambda e: e.matmul(
                                    B[2][:, 256:384], KN[:, csl], QN[:, csl],
                                    start=True, stop=True)))
                                bank_write(2, aqk)
                                st["L"] = tk(sch.op("v", [g2, st["m1"]] + pe, lambda e:
                                                    e.scalar_tensor_tensor(
                                    Lc[:], B[2][:, 0:128], -1.0, EmS[:],
                                    OP.mult, OP.mult)))
                                st["LT"] = tk(sch.op("v", [g2t, st["m2"]] + pe, lambda e:
                                                     e.scalar_tensor_tensor(
                                    LcT[:], B[2][:, 128:256], -1.0, ETmU[:],
                                    OP.mult, OP.mult)))
                                st["at"] = tk(sch.op("v", [aqk, st["m3"]] + pe, lambda e:
                                                     e.tensor_mul(
                                    attnT[:], B[2][:, 256:384], ETmI[:])))
                                for tt in (st["L"], st["LT"], st["at"]):
                                    bank_read(2, tt)

                            def s_x0():
                                st["x0a"] = tk(sch.op("v", t_vt + pe, lambda e:
                                                      e.tensor_scalar(
                                    Xsb[:, 0:128], vtok[:, csl], b_col, None, OP.mult)))
                                st["x0b"] = tk(sch.op("v", t_knt + [t_bg] + pe, lambda e:
                                                      e.tensor_scalar(
                                    Xsb[:, 128:256], kntok[:, csl], be_col,
                                    None, OP.mult)))
                                st["lp"] = (Lc, st["L"])
                                st["lpt"] = (LcT, st["LT"])
                                st["xev"] = None

                            def s_lev(lev):
                                lp_cur, t_lp = st["lp"]
                                lpt_cur, t_lpt = st["lpt"]
                                if lev == 0:
                                    depsx = bw_deps(XB) + [st["x0a"], st["x0b"]]
                                    tk(sch.op("t", depsx, lambda e: e.matmul(
                                        B[XB][:, 0:256], IDENT[:], Xsb[:],
                                        start=True, stop=False, skip_group_check=True)))
                                    t_ap = tk(sch.op("t", [t_lpt], lambda e, lpt=lpt_cur:
                                                     e.matmul(
                                        B[XB][:, 0:256], lpt[:], Xsb[:],
                                        start=False, stop=False, skip_group_check=True)))
                                else:
                                    t_ap = tk(sch.op("t", [t_lpt, st["xev"]],
                                                     lambda e, lpt=lpt_cur, lev=lev:
                                                     e.matmul(
                                        B[XB][:, 0:256], lpt[:], Xsb[:],
                                        start=False, stop=(lev == 6),
                                        skip_group_check=True)))
                                bank_write(XB, t_ap)
                                if lev % 2 == 0:
                                    st["xev"] = tk(sch.op("v", [t_ap], lambda e:
                                                          e.tensor_copy(Xsb[:], B[XB][:, 0:256])))
                                else:
                                    st["xev"] = tk(sch.op("a", [t_ap], lambda e:
                                                          e.copy(Xsb[:], B[XB][:, 0:256])))
                                bank_read(XB, st["xev"])
                                if lev < 6:
                                    depsq = bw_deps(QB) + [t_lp, t_lpt]
                                    tk(sch.op("t", depsq, lambda e, lp=lp_cur,
                                              lpt=lpt_cur: e.matmul(
                                        B[QB][:, 0:128], lpt[:], lp[:],
                                        start=True, stop=True, skip_group_check=True)))
                                    t_s2 = tk(sch.op("t", [], lambda e, lp=lp_cur,
                                                     lpt=lpt_cur: e.matmul(
                                        B[QB][:, 128:256], lp[:], lpt[:],
                                        start=True, stop=True, skip_group_check=True)))
                                    bank_write(QB, t_s2)
                                    if lev % 2 == 0:
                                        t_qe = tk(sch.op("v", [t_s2] + pe, lambda e:
                                                         e.tensor_copy(Lp2[:], B[QB][:, 0:256])))
                                    else:
                                        t_qe = tk(sch.op("a", [t_s2] + pe, lambda e:
                                                         e.copy(Lp2[:], B[QB][:, 0:256])))
                                    bank_read(QB, t_qe)
                                    st["lp"] = (Lp2[:, 0:128], t_qe)
                                    st["lpt"] = (Lp2[:, 128:256], t_qe)

                            def s_wf():
                                depsw = bw_deps(XB) + [st["xev"]]
                                t_wf = tk(sch.op("t", depsw, lambda e: e.matmul(
                                    B[XB][:, 256:384], Xsb[:, 128:256], IDENT[:],
                                    start=True, stop=True, skip_group_check=True)))
                                bank_write(XB, t_wf)
                                st["wfe"] = tk(sch.op("v", [t_wf] + pe, lambda e:
                                                      e.tensor_scalar_mul(
                                    WFn[:], B[XB][:, 256:384], -1.0)))
                                bank_read(XB, st["wfe"])

                            def s_scan():
                                t_sb_in = st["sb_in"]
                                depsv = bw_deps(5) + [st["xev"], t_sb_in]
                                t_v1 = tk(sch.op("t", depsv, lambda e: e.matmul(
                                    B[5][:, 0:128], IDENT[:], Xsb[:, 0:128],
                                    start=True, stop=False, skip_group_check=True)))
                                t_v2 = tk(sch.op("t", [st["wfe"]], lambda e: e.matmul(
                                    B[5][:, 0:128], WFn[:], Sb16[:],
                                    start=False, stop=True, skip_group_check=True)))
                                bank_write(5, t_v2)
                                t_vn = tk(sch.op("v", [t_v2] + pe, lambda e:
                                                 e.tensor_copy(vnew[:], B[5][:, 0:128])))
                                bank_read(5, t_vn)
                                depso = bw_deps(XB) + [t_qg, t_sb_in, st["wfe"],
                                                       st["xev"]]
                                t_o1 = tk(sch.op("t", depso, lambda e: e.matmul(
                                    B[XB][:, 384:512], QG[:, csl], Sb16[:],
                                    start=True, stop=False, skip_group_check=True)))
                                t_o2 = tk(sch.op("t", [t_vn, st["at"]], lambda e:
                                                 e.matmul(
                                    B[XB][:, 384:512], attnT[:], vnew[:],
                                    start=False, stop=True, skip_group_check=True)))
                                bank_write(XB, t_o2)
                                t_ce = tk(sch.op("v", [t_o2], lambda e, c=c:
                                                 e.tensor_copy(coretok[:, c, :],
                                                               B[XB][:, 384:512])))
                                bank_read(XB, t_ce)
                                tk(sch.op("a", [t_ce], lambda e, c=c: e.activation(
                                    rmsjunk[:], coretok[:, c, :], ACT.Square,
                                    accum_out=rmsc[:, c:c + 1])))
                                t_kd1 = tk(sch.op("v", t_nd + pe, lambda e, c=c:
                                                  e.tensor_scalar(
                                    colscr[:, cs0:cs0 + 1], gc_col,
                                    Dneg[:, c * 128 + 127:c * 128 + 128],
                                    None, OP.add)))
                                t_kd2 = tk(sch.op("a", [t_kd1] + pe, lambda e:
                                                  e.activation(
                                    colscr[:, cs0 + 1:cs0 + 2], colscr[:, cs0:cs0 + 1],
                                    ACT.Exp, scale=-1.0)))
                                t_egl = tk(sch.op("a", t_nd + pe, lambda e, c=c:
                                                  e.activation(
                                    colscr[:, cs0 + 2:cs0 + 3],
                                    Dneg[:, c * 128 + 127:c * 128 + 128],
                                    ACT.Exp, scale=-1.0)))
                                t_kd3 = tk(sch.op("v", [t_kd2] + t_knt + pe, lambda e:
                                                  e.tensor_scalar(
                                    kdt[:], kntok[:, csl],
                                    colscr[:, cs0 + 1:cs0 + 2], None, OP.mult)))
                                depss = bw_deps(5) + [t_kd3, t_vn]
                                t_su = tk(sch.op("t", depss, lambda e: e.matmul(
                                    B[5][:, 128:256], kdt[:], vnew[:],
                                    start=True, stop=True, skip_group_check=True)))
                                bank_write(5, t_su)
                                t_sn = tk(sch.op("v", [t_su, t_egl], lambda e, h=h:
                                                 e.scalar_tensor_tensor(
                                    S_sb[:, h * 128:(h + 1) * 128],
                                    S_sb[:, h * 128:(h + 1) * 128],
                                    colscr[:, cs0 + 2:cs0 + 3], B[5][:, 128:256],
                                    OP.mult, OP.add)))
                                bank_read(5, t_sn)
                                st["sb_out"] = tk(sch.op("v", [t_sn], lambda e, h=h:
                                                         e.tensor_copy(
                                    Sb16[:], S_sb[:, h * 128:(h + 1) * 128])))

                            return st, s_prep, s_grams, s_x0, s_lev, s_wf, s_scan

                        t_sb_chain = t_sb
                        for c0 in range(0, LIM_CHUNKS, 2):
                            cA, cB = c0, c0 + 1
                            has_b = cB < LIM_CHUNKS
                            stA, prepA, gramsA, x0A, levA, wfA, scanA = emit_chunk(cA)
                            if has_b:
                                stB, prepB, gramsB, x0B, levB, wfB, scanB = emit_chunk(cB)
                            prepA()
                            if has_b:
                                prepB()
                            drain(2)
                            gramsA()
                            if has_b:
                                gramsB()
                            drain(2)
                            x0A()
                            if has_b:
                                x0B()
                            drain(1)
                            for lev in range(7):
                                levA(lev)
                                drain(1)
                                if has_b:
                                    levB(lev)
                                drain(1)
                            wfA()
                            if has_b:
                                wfB()
                            drain(2)
                            stA["sb_in"] = t_sb_chain
                            scanA()
                            drain(2)
                            t_sb_chain = stA["sb_out"]
                            if has_b:
                                stB["sb_in"] = t_sb_chain
                                scanB()
                                drain(2)
                                t_sb_chain = stB["sb_out"]
                            prev_end[cA % 2] = stA["toks"]
                            if has_b:
                                prev_end[cB % 2] = stB["toks"]


                        drain(len(fillq), 0)
                        t_zs = cur_stp["zs"]
                        chunks_done = sch.fence()
                        gate_toks = []
                        if h == NH - 1:
                            tap("t_s", S_sb[:], chunks_done)
                        t_rstd = sch.op("v", chunks_done, lambda e: e.tensor_scalar_mul(
                            rmsc[:], rmsc[:], 1.0 / 128.0))
                        t_rstd = sch.op("a", [t_rstd], lambda e: e.activation(
                            rmsc[:], rmsc[:], ACT.Ln, bias=EPS_L2[:, 0:1]))
                        t_rstd = sch.op("a", [t_rstd], lambda e: e.activation(
                            rmsc[:], rmsc[:], ACT.Exp, scale=-0.5))
                        for c in range(LIM_CHUNKS):
                            t_gt = sch.op("v", [t_rstd], lambda e, c=c: e.tensor_scalar(
                                coretok[:, c, :], coretok[:, c, :], rmsc[:, c:c + 1],
                                None, OP.mult))
                            bi = 2 + (c % 2)
                            depsc = bw_deps(bi) + [t_gt]
                            t_tp = sch.op("t", depsc, lambda e, c=c, bi=bi: e.matmul(
                                B[bi][:, 0:128], coretok[:, c, :], IDENT[:],
                                start=True, stop=True))
                            bank_write(bi, t_tp)
                            t_cf = sch.op("v", [t_tp, t_zs], lambda e, c=c, h=h, bi=bi:
                                          e.tensor_mul(coreF[:, h, c * 128:(c + 1) * 128],
                                                       B[bi][:, 0:128],
                                                       zsil[:, c * 128:(c + 1) * 128]))
                            bank_read(bi, t_cf)
                            gate_toks.append(t_cf)

                        head_state["prev_gate"] = gate_toks
                        fillq.extend(zthN)
                        stpA = stpN

                heads_done = sch.fence()
                tap("t_coref", coreF[:], heads_done)

                with ExitStack() as wo_scope:
                    wo_sb = wo_scope.enter_context(
                        nc.sbuf_tensor(f"wo_p{p}", [128, 2, KC, 256], BF16))
                    xog = wo_scope.enter_context(
                        nc.sbuf_tensor(f"xog_p{p}", [128, 2, 256], F32))
                    t_wo = [None, None]
                    t_wo_rd = [None, None]
                    t_xog = [None, None]
                    for og in range(2 * NOG if not SKIP_WOUT else 0):
                        ws = og % 2
                        dep_wo = ([t_wo[ws]] if t_wo[ws] else heads_done) + \
                            ([t_wo_rd[ws]] if t_wo_rd[ws] else [])
                        t_l = sch.dma("g", f"dwo{p}s{ws}", dep_wo, lambda e, og=og, ws=ws:
                                      e.dma_start(
                            out=wo_sb[:, ws],
                            in_=wout_d[:, og * 256:(og + 1) * 256]
                            .rearrange("(k p) c -> p k c", p=128)))
                        t_wos = sch.op("v", [t_l], lambda e, ws=ws: e.tensor_scalar_mul(
                            wo_sb[:, ws].rearrange("p k c -> p (k c)"),
                            wo_sb[:, ws].rearrange("p k c -> p (k c)"), nw_sb[:]))
                        t_wo[ws] = t_wos
                        for tc in range(NTC):
                            oslot = tc % 2
                            dep_x = [t_xog[oslot]] if t_xog[oslot] else list(heads_done)
                            t_x = sch.dma("s", f"dxo{p}s{oslot}", dep_x,
                                          lambda e, tc=tc, og=og, oslot=oslot:
                                          e.dma_start(
                                out=xog[:, oslot, :],
                                in_=x_d[t0 + tc * 128: t0 + (tc + 1) * 128,
                                        og * 256:(og + 1) * 256]))
                            bslot = 4 + (tc % 2)
                            deps0 = bw_deps(bslot) + [t_wos] + heads_done
                            tk = None
                            for k in range(KC):
                                tk = sch.op("t", deps0 if k == 0 else [],
                                            lambda e, k=k, tc=tc, bslot=bslot, ws=ws:
                                            e.matmul(
                                    B[bslot][:, 0:256],
                                    coreF[:, k, tc * 128:(tc + 1) * 128],
                                    wo_sb[:, ws, k, :], start=(k == 0),
                                    stop=(k == KC - 1)))
                            bank_write(bslot, tk)
                            t_wo_rd[ws] = tk
                            t_e = sch.op("v", [tk, t_x], lambda e, tc=tc, og=og,
                                         oslot=oslot, bslot=bslot: e.tensor_add(
                                acc[:, tc, og * 256:(og + 1) * 256],
                                B[bslot][:, 0:256], xog[:, oslot, :]))
                            bank_read(bslot, t_e)
                            t_xog[oslot] = t_e
                wo_done = sch.fence()

                with ExitStack() as ffn_scope:
                    fsb = lambda n, s, d: ffn_scope.enter_context(
                        nc.sbuf_tensor(f"{n}_p{p}", s, d))
                    acts = fsb("acts", [128, 2, 8, T], BF16)
                    wgu_sb = fsb("wgu", [128, 2, 2, KC, 128], BF16)
                    wd_sb = fsb("wd", [128, 2, 2, 8, 256], BF16)
                    obuf = fsb("obuf", [128, 2, 512], F32)
                    gsil = fsb("gsil", [128, 512], BF16)
                    upsc = fsb("upsc", [128, 512], BF16)
                    cwbt = fsb("cwbt", [128, T], BF16)

                    t_wgu = None
                    wgu_tok = [None, None]
                    wgu_rd = [None, None]
                    t_ob = [None, None]
                    t_ac_last = None
                    wd_toks = {}

                    for gi, grp in enumerate(EGRPS if not SKIP_FFN else []):
                        last_grp = gi == len(EGRPS) - 1
                        gfence = sch.fence()
                        act_toks = []
                        for ei, ex in enumerate(grp):
                            t_cb = []
                            for n in range(2):
                                deps = bw_deps(0) + comb_done + gfence
                                t_mm = sch.op("t", deps, lambda e, ex=ex, n=n: e.matmul(
                                    B[0][:, 0:512], SEL8B[:, ex * 128:(ex + 1) * 128],
                                    CW8[:, n * 512:(n + 1) * 512],
                                    start=True, stop=True))
                                bank_write(0, t_mm)
                                t_e = sch.op("v", [t_mm], lambda e, n=n: e.tensor_copy(
                                    cwbt[:, n * 512:(n + 1) * 512], B[0][:, 0:512]))
                                bank_read(0, t_e)
                                t_cb.append(t_e)
                            for fc in range(8):
                                fslot = fc % 2
                                dep_w = ([wgu_tok[fslot]] if wgu_tok[fslot] else gfence) + \
                                    ([wgu_rd[fslot]] if wgu_rd[fslot] else [])
                                t_wg1 = sch.dma("g", f"dwg{p}s{fslot}", dep_w,
                                                lambda e, ex=ex, fc=fc, fslot=fslot:
                                                e.dma_start(
                                    out=wgu_sb[:, fslot, 0, :, :],
                                    in_=wg_d[ex, :, fc * 128:(fc + 1) * 128]
                                    .rearrange("(k p) c -> p k c", p=128)))
                                t_wgu = sch.dma("g", f"dwu{p}s{fslot}", [],
                                                lambda e, ex=ex, fc=fc, fslot=fslot:
                                                e.dma_start(
                                    out=wgu_sb[:, fslot, 1, :, :],
                                    in_=wu_d[ex, :, fc * 128:(fc + 1) * 128]
                                    .rearrange("(k p) c -> p k c", p=128)))
                                wgu_tok[fslot] = t_wgu
                                gb = (1, 2) if fc % 2 == 0 else (4, 5)
                                ub = (3, 6) if fc % 2 == 0 else (0, 7)
                                depsk = (bw_deps(gb[0]) + bw_deps(gb[1]) +
                                         bw_deps(ub[0]) + bw_deps(ub[1]) +
                                         [t_wg1, t_wgu])
                                tg = tu = None
                                for k in range(KC):
                                    st = k == 0
                                    sp = k == KC - 1
                                    for n in range(2):
                                        tg = sch.op("t", depsk if (k == 0 and n == 0)
                                                    else [],
                                                    lambda e, k=k, n=n, fslot=fslot,
                                                    st=st, sp=sp, gb=gb: e.matmul(
                                            B[gb[n]][:, 0:512],
                                            wgu_sb[:, fslot, 0, k, :],
                                            xnF[:, k, n * 512:(n + 1) * 512],
                                            start=st, stop=sp,
                                            skip_group_check=True))
                                    for n in range(2):
                                        tu = sch.op("t", [], lambda e, k=k, n=n,
                                                    fslot=fslot, st=st, sp=sp, ub=ub:
                                                    e.matmul(
                                            B[ub[n]][:, 0:512],
                                            wgu_sb[:, fslot, 1, k, :],
                                            xnF[:, k, n * 512:(n + 1) * 512],
                                            start=st, stop=sp,
                                            skip_group_check=True))
                                for bb in (*gb, *ub):
                                    bank_write(bb, tu)
                                wgu_rd[fslot] = tu
                                for n in range(2):
                                    t_sg = sch.op("a", [tu, t_ac_last],
                                                  lambda e, n=n, gb=gb: e.activation(
                                        gsil[:], B[gb[n]][:, 0:512], ACT.Silu))
                                    bank_read(gb[n], t_sg)
                                    t_up = sch.op("v", [tu, t_ac_last] + t_cb,
                                                  lambda e, n=n, ub=ub: e.tensor_mul(
                                        upsc[:], B[ub[n]][:, 0:512],
                                        cwbt[:, n * 512:(n + 1) * 512]))
                                    bank_read(ub[n], t_up)
                                    t_ac = sch.op("v", [t_sg, t_up],
                                                  lambda e, ei=ei, fc=fc, n=n:
                                                  e.tensor_mul(
                                        acts[:, ei, fc, n * 512:(n + 1) * 512],
                                        gsil[:], upsc[:]))
                                    act_toks.append(t_ac)
                                    t_ac_last = t_ac
                        if gi == 0:
                            tap("t_acts", acts[:], act_toks)

                        t_down_last = [None, None]
                        for og in range(2 * NOG):
                            ws = og % 2
                            new_wd = []
                            for ei, ex in enumerate(grp):
                                dep_wd = ([wd_toks[(ws, ei)]] if wd_toks.get((ws, ei))
                                          else list(wo_done)) + \
                                    ([t_down_last[ws]] if t_down_last[ws] else [])
                                wd_toks[(ws, ei)] = sch.dma(
                                    "g", f"dwd{p}s{ws}e{ei}", dep_wd,
                                    lambda e, ei=ei, ex=ex, og=og, ws=ws: e.dma_start(
                                        out=wd_sb[:, ws, ei, :, :],
                                        in_=wd_d[ex, :, og * 256:(og + 1) * 256]
                                        .rearrange("(f p) c -> p f c", p=128)))
                                new_wd.append(wd_toks[(ws, ei)])
                            for tc in range(NTC):
                                bslot = 4 + (tc % 2)
                                nmm = len(grp) * 8
                                deps0 = bw_deps(bslot) + act_toks + new_wd + wo_done
                                tk = None
                                i = 0
                                for ei in range(len(grp)):
                                    for fc in range(8):
                                        first = i == 0
                                        last = i == nmm - 1
                                        tk = sch.op("t", deps0 if first else [],
                                                    lambda e, ei=ei, fc=fc, tc=tc,
                                                    bslot=bslot, first=first, last=last,
                                                    ws=ws: e.matmul(
                                            B[bslot][:, 0:256],
                                            acts[:, ei, fc, tc * 128:(tc + 1) * 128],
                                            wd_sb[:, ws, ei, fc, :],
                                            start=first, stop=last,
                                            skip_group_check=True))
                                        i += 1
                                t_down_last[ws] = tk
                                bank_write(bslot, tk)
                                if not last_grp:
                                    t_e = sch.op("v", [tk], lambda e, tc=tc, og=og,
                                                 bslot=bslot: e.tensor_add(
                                        acc[:, tc, og * 256:(og + 1) * 256],
                                        acc[:, tc, og * 256:(og + 1) * 256],
                                        B[bslot][:, 0:256]))
                                    bank_read(bslot, t_e)
                                else:
                                    oslot = tc % 2
                                    dep_o = [t_ob[oslot]] if t_ob[oslot] else []
                                    t_s1 = sch.op("v", [tk] + dep_o,
                                                  lambda e, tc=tc, og=og, oslot=oslot,
                                                  bslot=bslot: e.tensor_add(
                                        obuf[:, oslot, 0:256], B[bslot][:, 0:256],
                                        acc[:, tc, og * 256:(og + 1) * 256]))
                                    bank_read(bslot, t_s1)
                                    t_st = sch.dma("s", f"do{p}s{oslot}", [t_s1],
                                                   lambda e, tc=tc, og=og, oslot=oslot:
                                                   e.dma_start(
                                        out=out_d[t0 + tc * 128: t0 + (tc + 1) * 128,
                                                  og * 256:(og + 1) * 256],
                                        in_=obuf[:, oslot, 0:256]))
                                    t_ob[oslot] = t_st

                sch.flush(block)

        sch.raw("s", sch.fence(), lambda e: e.nop())
        sch.flush(block)

    return nc


def _prep_inputs(inp):
    x = np.ascontiguousarray(inp["L0_attn_hidden_states"])
    wba = np.ascontiguousarray(inp["L0_attn_W_ba"])
    wba_re = np.ascontiguousarray(wba[:, list(range(0, 32, 2)) + list(range(1, 32, 2))])
    cw = np.ascontiguousarray(inp["L0_attn_conv_weight"])
    convw_re = np.zeros((128, 192), np.float32)
    for part in range(3):
        for h in range(NH):
            blk = cw[part * 2048 + h * 128: part * 2048 + (h + 1) * 128, :]
            convw_re[:, (part * 16 + h) * 4:(part * 16 + h) * 4 + 4] = blk
    shared = {
        "wqkvz": np.ascontiguousarray(inp["L0_attn_W_qkvz"]),
        "wba": wba_re,
        "convw": convw_re,
        "dtb": np.ascontiguousarray(inp["L0_attn_dt_bias"].reshape(16, 1)),
        "alog": np.ascontiguousarray(inp["L0_attn_A_log"].reshape(16, 1)),
        "nw": np.ascontiguousarray(inp["L0_attn_norm_weight"].reshape(128, 1)),
        "wout": np.ascontiguousarray(inp["L0_attn_W_out"]),
        "rw": np.ascontiguousarray(inp["L0_ffn_router_w"]),
        "wg": np.ascontiguousarray(inp["L0_ffn_gate_proj"]),
        "wu": np.ascontiguousarray(inp["L0_ffn_up_proj"]),
        "wd": np.ascontiguousarray(inp["L0_ffn_down_proj"]),
    }
    in_maps = []
    for c in range(NCORES):
        m = dict(shared)
        m["x"] = np.ascontiguousarray(x[c])
        in_maps.append(m)
    return in_maps


def kernel(**inputs):
    in_maps = _prep_inputs(inputs)
    nc = build_nc()
    res = run_bass_kernel_spmd(nc, in_maps, core_ids=list(range(NCORES)))
    out = np.stack([np.asarray(r["out"]) for r in res.results], axis=0)
    return out.astype(np.float32)


# version 16
# speedup vs baseline: 1.0224x; 1.0201x over previous
from contextlib import ExitStack

import numpy as np
import ml_dtypes

import concourse.bass as bass
import concourse.mybir as mybir
from concourse.bass_utils import run_bass_kernel_spmd

F32 = mybir.dt.float32
BF16 = mybir.dt.bfloat16
OP = mybir.AluOpType
ACT = mybir.ActivationFunctionType
AX = mybir.AxisListType.X

NCORES = 8
SEQ = 2048
D = 2048
NH = 16
DK = 128
NPASS = 2
T = SEQ // NPASS
CH = 128
NCH = T // CH
NTC = T // 128
KC = D // 128
FF = 1024
NEXP = 8
EGRPS = [(0, 1), (2, 3), (4, 5), (6, 7)]
NOG = D // 512

DEBUG_TAPS = ()
LIM_HEADS = NH
LIM_CHUNKS = NCH
CHUNK_LVL = 6
SKIP_FFN = False
SKIP_WOUT = False

ENGS = ("t", "v", "a", "g", "s")


class Tok:
    __slots__ = ("sem", "val")

    def __init__(self, sem, val):
        self.sem = sem
        self.val = val


class Sched:
    def __init__(self, nc, ctx):
        self.nc = nc
        self.ctx = ctx
        self.sems = {}
        self.counts = {}
        self.dma_streams = set()
        for name in ("t", "v", "a", "g", "s"):
            self.sems[name] = ctx.enter_context(nc.semaphore(f"sem_{name}"))
            self.counts[name] = 0
        self.hw = {e: {} for e in ENGS}
        self.pend = {e: [] for e in ENGS}

    def stream(self, name):
        if name not in self.sems:
            self.sems[name] = self.ctx.enter_context(self.nc.semaphore(f"sem_{name}"))
            self.counts[name] = 0
            self.dma_streams.add(name)
        return name

    def _waits(self, eng, deps):
        want = {}
        for d in deps:
            if d is None:
                continue
            if d.sem == eng and eng == "t":
                continue
            v = self.counts[d.sem] if d.sem in self.dma_streams else d.val
            if want.get(d.sem, 0) < v:
                want[d.sem] = v
        ws = []
        for s, v in want.items():
            if self.hw[eng].get(s, 0) >= v:
                continue
            self.hw[eng][s] = v
            ws.append((self.sems[s], v))
        return ws

    def op(self, eng, deps, fn):
        ws = self._waits(eng, deps)
        self.counts[eng] += 1
        val = self.counts[eng]
        sem = self.sems[eng]

        def run(e, ws=ws, fn=fn, sem=sem):
            for s, v in ws:
                e.wait_ge(s, v)
            fn(e).then_inc(sem, 1)

        self.pend[eng].append(run)
        return Tok(eng, val)

    def dma(self, eng, stream, deps, fn):
        stream = self.stream(stream)
        ws = self._waits(eng, deps)
        self.counts[stream] += 16
        val = self.counts[stream]
        sem = self.sems[stream]

        def run(e, ws=ws, fn=fn, sem=sem):
            for s, v in ws:
                e.wait_ge(s, v)
            fn(e).then_inc(sem, 16)

        self.pend[eng].append(run)
        return Tok(stream, val)

    def raw(self, eng, deps, fn):
        ws = self._waits(eng, deps)

        def run(e, ws=ws, fn=fn):
            for s, v in ws:
                e.wait_ge(s, v)
            fn(e)

        self.pend[eng].append(run)

    def fence(self):
        return [Tok(k, self.counts[k]) for k in self.counts]

    def flush(self, block):
        for eng, fns in self.pend.items():
            if not fns:
                continue
            meth = {"t": block.tensor, "v": block.vector, "a": block.scalar,
                    "g": block.gpsimd, "s": block.sync}[eng]

            def body(e, fns=list(fns)):
                for f in fns:
                    f(e)
            meth(body)
            self.pend[eng] = []


def _host_consts():
    bf = ml_dtypes.bfloat16
    i = np.arange(CH)
    c = {}
    c["IDENT"] = np.eye(128, dtype=bf)
    c["IDENT16F"] = np.eye(16, dtype=np.float32)
    c["M_SL"] = (i[:, None] > i[None, :]).astype(bf)
    c["M_SU"] = (i[:, None] < i[None, :]).astype(bf)
    c["M_IU"] = (i[:, None] <= i[None, :]).astype(bf)
    c["MB_SL"] = np.where(i[:, None] > i[None, :], 0.0, -1e30).astype(np.float32)
    c["MB_IU"] = np.where(i[:, None] <= i[None, :], 0.0, -1e30).astype(np.float32)
    sel16 = np.zeros((NH, NH, 128), dtype=bf)
    for h in range(NH):
        sel16[h, h, :] = 1.0
    c["SEL16B"] = sel16
    nsel16 = np.zeros((NH, NH, 128), dtype=np.float32)
    for h in range(NH):
        nsel16[h, h, :] = -1.0
    c["NSEL16F"] = nsel16
    sel8 = np.zeros((NEXP, NEXP, 128), dtype=bf)
    for e in range(NEXP):
        sel8[e, e, :] = 1.0
    c["SEL8B"] = sel8
    c["ONES1"] = np.ones((1, 128), dtype=bf)
    c["ONESC"] = np.full((1, 128), DK ** -0.5, dtype=bf)
    c["ONECOL"] = np.ones((128, 1), dtype=bf)
    c["CONSTS"] = np.broadcast_to(
        np.array([1e-5, 1e-6, 1.0, 0.0], np.float32), (128, 4)).copy()
    return c


def build_nc():
    nc = bass.Bass("TRN2", target_bir_lowering=False)

    x_d = nc.dram_tensor("x", [SEQ, D], F32, kind="ExternalInput").ap()
    wqkvz_d = nc.dram_tensor("wqkvz", [D, 8192], F32, kind="ExternalInput").ap()
    wba_d = nc.dram_tensor("wba", [D, 32], F32, kind="ExternalInput").ap()
    convw_d = nc.dram_tensor("convw", [128, 192], F32, kind="ExternalInput").ap()
    dtb_d = nc.dram_tensor("dtb", [16, 1], F32, kind="ExternalInput").ap()
    alog_d = nc.dram_tensor("alog", [16, 1], F32, kind="ExternalInput").ap()
    nw_d = nc.dram_tensor("nw", [128, 1], F32, kind="ExternalInput").ap()
    wout_d = nc.dram_tensor("wout", [D, D], F32, kind="ExternalInput").ap()
    rw_d = nc.dram_tensor("rw", [D, 8], F32, kind="ExternalInput").ap()
    wg_d = nc.dram_tensor("wg", [NEXP, D, FF], F32, kind="ExternalInput").ap()
    wu_d = nc.dram_tensor("wu", [NEXP, D, FF], F32, kind="ExternalInput").ap()
    wd_d = nc.dram_tensor("wd", [NEXP, FF, D], F32, kind="ExternalInput").ap()
    out_d = nc.dram_tensor("out", [SEQ, D], F32, kind="ExternalOutput").ap()

    taps = {}
    tapshapes = {
        "t_xnf": ([128, KC, T], BF16), "t_gc16": ([16, T], F32),
        "t_b16": ([16, T], F32), "t_qf": ([128, T], BF16),
        "t_kf": ([128, T], BF16), "t_vf": ([128, T], BF16),
        "t_coref": ([128, NH, T], BF16), "t_cw8": ([8, T], BF16),
        "t_acts": ([128, 2, 8, T], BF16), "t_zsil": ([128, T], BF16),
        "t_kn": ([128, T], BF16), "t_x0": ([128, 256], BF16),
        "t_xfin": ([128, 256], BF16), "t_s": ([128, NH * 128], F32),
        "t_attnt": ([128, CH], BF16), "t_lc": ([128, CH], BF16),
    }
    for name in DEBUG_TAPS:
        shp, dt = tapshapes[name]
        taps[name] = nc.dram_tensor(name, shp, dt, kind="ExternalOutput").ap()

    hc = _host_consts()
    inl = {k: nc.inline_tensor(v, name=k.lower()) for k, v in hc.items()}

    ctx = ExitStack()
    with ctx:
        sb = lambda name, shape, dt: ctx.enter_context(nc.sbuf_tensor(name, shape, dt))
        sch = Sched(nc, ctx)
        block = ctx.enter_context(nc.Block())

        B = [ctx.enter_context(nc.psum_tensor(f"bank{i}", [128, 512], F32))
             for i in range(8)]
        bank_w = [None] * 8
        bank_r = [[] for _ in range(8)]

        def bank_write(i, tok):
            bank_w[i] = tok
            bank_r[i] = []

        def bw_deps(i):
            return list(bank_r[i]) + ([bank_w[i]] if bank_w[i] else [])

        def bank_read(i, tok):
            bank_r[i].append(tok)

        def tap(name, src_ap, deps):
            if name in taps:
                return [sch.dma("s", "dtap", deps, lambda e, a=src_ap, d=taps[name]:
                                e.dma_start(out=d, in_=a))]
            return []

        IDENT = sb("c_ident", [128, 128], BF16)
        IDENT16F = sb("c_id16f", [16, 16], F32)
        M_SL = sb("c_msl", [128, 128], BF16)
        M_SU = sb("c_msu", [128, 128], BF16)
        M_IU = sb("c_miu", [128, 128], BF16)
        MB_SL = sb("c_mbsl", [128, 128], F32)
        MB_IU = sb("c_mbiu", [128, 128], F32)
        SEL16B = sb("c_sel16", [16, NH * 128], BF16)
        NSEL16F = sb("c_nsel16", [16, NH * 128], F32)
        SEL8B = sb("c_sel8", [8, NEXP * 128], BF16)
        ONES1 = sb("c_ones1", [1, 128], BF16)
        ONESC = sb("c_onesc", [1, 128], BF16)
        ONECOL = sb("c_onecol", [128, 1], BF16)
        CONSTS = sb("c_consts", [128, 4], F32)

        wba_sb = sb("wba_sb", [128, KC, 32], BF16)
        rw_sb = sb("rw_sb", [128, KC, 8], BF16)
        convw_sb = sb("convw_sb", [128, 192], F32)
        dtb_sb = sb("dtb_sb", [16, 1], F32)
        alog_sb = sb("alog_sb", [16, 1], F32)
        negea_sb = sb("negea_sb", [16, 1], F32)
        nw_sb = sb("nw_sb", [128, 1], F32)

        xnF = sb("xnF", [128, KC, T], BF16)
        coreF = sb("coreF", [128, NH, T], BF16)
        S_sb = sb("S_sb", [128, NH * 128], F32)
        halo = sb("halo", [128, 48, 3], BF16)

        B16b = sb("B16b", [16, T], BF16)
        GC16 = sb("GC16", [16, T], F32)
        EGC16b = sb("EGC16b", [16, T], BF16)
        GCtok = sb("GCtok", [128, NCH * 16], F32)
        Btok = sb("Btok", [128, NCH * 16], F32)
        EGCtok = sb("EGCtok", [128, NCH * 16], F32)
        BEGCtok = sb("BEGCtok", [128, NCH * 16], F32)
        CW8 = sb("CW8", [8, T], BF16)

        f0 = sch.fence()
        ld = []
        for name, dst in [("IDENT", IDENT), ("IDENT16F", IDENT16F), ("M_SL", M_SL),
                          ("M_SU", M_SU), ("M_IU", M_IU), ("MB_SL", MB_SL),
                          ("MB_IU", MB_IU), ("ONES1", ONES1),
                          ("ONESC", ONESC), ("ONECOL", ONECOL), ("CONSTS", CONSTS)]:
            src = inl[name].ap()
            ld.append(sch.dma("s", "dc", f0, lambda e, s=src, d=dst: e.dma_start(out=d[:], in_=s)))
        for name, dst, nh in [("SEL16B", SEL16B, NH), ("NSEL16F", NSEL16F, NH),
                              ("SEL8B", SEL8B, NEXP)]:
            src = inl[name].ap().rearrange("h p m -> p h m")
            d3 = dst[:].rearrange("p (h m) -> p h m", h=nh)
            ld.append(sch.dma("s", "dc", f0, lambda e, s=src, d=d3: e.dma_start(out=d, in_=s)))
        ld.append(sch.dma("g", "dcg", f0, lambda e: e.dma_start(
            out=wba_sb[:], in_=wba_d.rearrange("(k p) c -> p k c", p=128))))
        ld.append(sch.dma("g", "dcg", f0, lambda e: e.dma_start(
            out=rw_sb[:], in_=rw_d.rearrange("(k p) c -> p k c", p=128))))
        ld.append(sch.dma("s", "dc", f0, lambda e: e.dma_start(out=convw_sb[:], in_=convw_d)))
        ld.append(sch.dma("s", "dc", f0, lambda e: e.dma_start(out=dtb_sb[:], in_=dtb_d)))
        ld.append(sch.dma("s", "dc", f0, lambda e: e.dma_start(out=alog_sb[:], in_=alog_d)))
        ld.append(sch.dma("s", "dc", f0, lambda e: e.dma_start(out=nw_sb[:], in_=nw_d)))
        t_ea = sch.op("a", ld, lambda e: e.activation(negea_sb[:], alog_sb[:], ACT.Exp))
        t_ea = sch.op("v", [t_ea], lambda e: e.tensor_scalar_mul(negea_sb[:], negea_sb[:], -1.0))
        sch.op("v", f0, lambda e: e.memset(S_sb[:], 0.0))
        sch.op("v", f0, lambda e: e.memset(halo[:], 0.0))

        EPS_LN = CONSTS[:, 0:1]
        EPS_L2 = CONSTS[:, 1:2]
        ONE_C = CONSTS[:, 2:3]

        sch.flush(block)

        for p in range(NPASS):
            t0 = p * T
            with ExitStack() as pass_ctx:
                psb = lambda n, s, d: pass_ctx.enter_context(
                    nc.sbuf_tensor(f"{n}_p{p}", s, d))
                acc = psb("acc", [128, NTC, D], BF16)

                pfence = sch.fence()
                ph1_scope = ExitStack()
                p1sb = lambda n, s, d: ph1_scope.enter_context(
                    nc.sbuf_tensor(f"{n}_p{p}", s, d))
                xln = p1sb("xln0", [128, D], F32)
                xnt = p1sb("xnt", [128, D], BF16)
                stat = p1sb("stat", [128, 8], F32)
                B16 = p1sb("B16", [16, T], F32)
                A16 = p1sb("A16", [16, T], F32)
                G16 = p1sb("G16", [16, T], F32)
                xln_tok = None
                prev_rs = None
                prev_xn = None
                prev_tp = None

                for tc in range(NTC):
                    dep = xln_tok if xln_tok else list(pfence)
                    t_x = sch.dma("s", f"dx{p}", dep, lambda e, tc=tc: e.dma_start(
                        out=xln[:], in_=x_d[t0 + tc * 128: t0 + (tc + 1) * 128, :]))
                    t_mean = sch.op("v", [t_x, prev_rs], lambda e: e.tensor_reduce(
                        stat[:, 0:1], xln[:], AX, OP.add))
                    t_sq = sch.op("a", [t_x, prev_xn], lambda e: e.activation(
                        xnt[:], xln[:], ACT.Square, accum_out=stat[:, 1:2]))
                    t_mean = sch.op("v", [t_mean], lambda e: e.tensor_scalar_mul(
                        stat[:, 0:1], stat[:, 0:1], 1.0 / D))
                    t_msq = sch.op("v", [t_mean], lambda e: e.tensor_mul(
                        stat[:, 2:3], stat[:, 0:1], stat[:, 0:1]))
                    t_var = sch.op("v", [t_msq, t_sq, prev_rs], lambda e: e.scalar_tensor_tensor(
                        stat[:, 3:4], stat[:, 1:2], 1.0 / D, stat[:, 2:3],
                        OP.mult, OP.subtract))
                    t_ln = sch.op("a", [t_var], lambda e: e.activation(
                        stat[:, 4:5], stat[:, 3:4], ACT.Ln, bias=EPS_LN))
                    t_rs = sch.op("a", [t_ln, prev_xn], lambda e: e.activation(
                        stat[:, 4:5], stat[:, 4:5], ACT.Exp, scale=-0.5))
                    t_xn = sch.op("v", [t_rs, t_mean, prev_tp, t_sq], lambda e: e.tensor_scalar(
                        xnt[:], xln[:], stat[:, 0:1], stat[:, 4:5],
                        OP.subtract, OP.mult))
                    xln_tok = [t_xn, t_sq]
                    prev_rs = t_rs
                    prev_xn = t_xn
                    for k in range(KC):
                        bi = k % 2
                        t_mm = sch.op("t", [t_xn] + bw_deps(bi),
                                      lambda e, k=k, bi=bi: e.matmul(
                                          B[bi][:, 0:128], xnt[:, k * 128:(k + 1) * 128],
                                          IDENT[:], start=True, stop=True))
                        bank_write(bi, t_mm)
                        eng = "v" if (k % 4) < 2 else "a"
                        if eng == "v":
                            t_e = sch.op("v", [t_mm], lambda e, k=k, tc=tc, bi=bi:
                                         e.tensor_copy(xnF[:, k, tc * 128:(tc + 1) * 128],
                                                       B[bi][:, 0:128]))
                        else:
                            t_e = sch.op("a", [t_mm], lambda e, k=k, tc=tc, bi=bi:
                                         e.copy(xnF[:, k, tc * 128:(tc + 1) * 128],
                                                B[bi][:, 0:128]))
                        bank_read(bi, t_e)
                        prev_tp = t_mm

                xnF_done = sch.fence()
                tap("t_xnf", xnF[:], xnF_done)

                for n in range(2):
                    for part, dst in ((0, B16), (1, A16)):
                        bi = 2 + part
                        deps = bw_deps(bi) + xnF_done
                        tk = None
                        for k in range(KC):
                            tk = sch.op("t", deps if k == 0 else [],
                                        lambda e, k=k, n=n, part=part, bi=bi: e.matmul(
                                B[bi][0:16, 0:512],
                                wba_sb[:, k, 16 * part:16 * part + 16],
                                xnF[:, k, n * 512:(n + 1) * 512],
                                start=(k == 0), stop=(k == KC - 1)))
                        bank_write(bi, tk)
                        t_e = sch.op("v", [tk], lambda e, n=n, dst=dst, bi=bi:
                                     e.tensor_copy(dst[:, n * 512:(n + 1) * 512],
                                                   B[bi][0:16, 0:512]))
                        bank_read(bi, t_e)

                ba_done = sch.fence()
                t1 = sch.op("a", ba_done, lambda e: e.activation(
                    B16[:], B16[:], ACT.Exp, scale=-1.0))
                t1 = sch.op("v", [t1], lambda e: e.tensor_scalar_add(B16[:], B16[:], 1.0))
                t1 = sch.op("v", [t1], lambda e: e.reciprocal(B16[:], B16[:]))
                tbb = sch.op("v", [t1], lambda e: e.tensor_copy(B16b[:], B16[:]))
                t2 = sch.op("v", ba_done, lambda e: e.tensor_scalar(
                    A16[:], A16[:], dtb_sb[:], 85.0, OP.add, OP.min))
                t2 = sch.op("a", [t2], lambda e: e.activation(G16[:], A16[:], ACT.Exp))
                t2 = sch.op("a", [t2], lambda e: e.activation(
                    G16[:], G16[:], ACT.Ln, bias=ONE_C[0:16, 0:1]))
                t2 = sch.op("v", [t2, t_ea], lambda e: e.tensor_scalar_mul(
                    G16[:], G16[:], negea_sb[:]))
                tgc = t2
                for c in range(NCH):
                    tgc = sch.op("v", [tgc], lambda e, c=c: e.tensor_tensor_scan(
                        GC16[:, c * CH:(c + 1) * CH], G16[:, c * CH:(c + 1) * CH],
                        G16[:, c * CH:(c + 1) * CH], 0.0, OP.add, OP.bypass))
                tegc = sch.op("a", [tgc], lambda e: e.activation(
                    EGC16b[:], GC16[:], ACT.Exp))
                tap("t_gc16", GC16[:], [tgc])
                tap("t_b16", B16[:], [t1])

                tok_toks = [tgc, t1]
                for c in range(NCH):
                    bi = 4 + (c % 2)
                    for src, dst in ((GC16, GCtok), (B16, Btok)):
                        deps = bw_deps(bi) + [tgc, t1]
                        t_mm = sch.op("t", deps, lambda e, c=c, bi=bi, src=src: e.matmul(
                            B[bi][:, 0:16], src[:, c * CH:(c + 1) * CH],
                            IDENT16F[:], start=True, stop=True))
                        bank_write(bi, t_mm)
                        t_e = sch.op("v", [t_mm], lambda e, c=c, bi=bi, dst=dst:
                                     e.tensor_copy(dst[:, c * 16:(c + 1) * 16],
                                                   B[bi][:, 0:16]))
                        bank_read(bi, t_e)
                        tok_toks.append(t_e)
                t_eg = sch.op("a", tok_toks, lambda e: e.activation(
                    EGCtok[:], GCtok[:], ACT.Exp))
                t_bg = sch.op("v", [t_eg], lambda e: e.tensor_mul(
                    BEGCtok[:], Btok[:], EGCtok[:]))
                scal_done = sch.fence()
                ph1_scope.close()

                ph2_scope = ExitStack()
                rsc = ph2_scope.enter_context(nc.sbuf_tensor(f"rsc_p{p}", [128, 8], F32))
                rtmp = ph2_scope.enter_context(nc.sbuf_tensor(f"rtmp_p{p}", [128, 16], F32))
                cwb = ph2_scope.enter_context(nc.sbuf_tensor(f"cwb_p{p}", [128, 8], BF16))
                prev_cw = None
                for tc in range(NTC):
                    deps = bw_deps(6) + xnF_done
                    tk = None
                    for k in range(KC):
                        tk = sch.op("t", deps if k == 0 else [], lambda e, k=k, tc=tc:
                                    e.matmul(B[6][:, 0:8],
                                             xnF[:, k, tc * 128:(tc + 1) * 128],
                                             rw_sb[:, k, :], start=(k == 0),
                                             stop=(k == KC - 1)))
                    bank_write(6, tk)
                    t_e = sch.op("v", [tk, prev_cw] + scal_done, lambda e: e.tensor_copy(
                        rsc[:], B[6][:, 0:8]))
                    bank_read(6, t_e)
                    t_m = sch.op("v", [t_e], lambda e: e.tensor_reduce(
                        rtmp[:, 0:1], rsc[:], AX, OP.max, negate=True))
                    t_x2 = sch.op("a", [t_m], lambda e: e.activation(
                        rsc[:], rsc[:], ACT.Exp, bias=rtmp[:, 0:1]))
                    t_sm = sch.op("v", [t_x2], lambda e: e.tensor_reduce(
                        rtmp[:, 1:2], rsc[:], AX, OP.add))
                    t_r = sch.op("v", [t_sm], lambda e: e.reciprocal(
                        rtmp[:, 2:3], rtmp[:, 1:2]))
                    t_p = sch.op("v", [t_r], lambda e: e.tensor_scalar_mul(
                        rsc[:], rsc[:], rtmp[:, 2:3]))
                    t_m8 = sch.op("v", [t_p], lambda e: e.max(out=rtmp[:, 8:16], in_=rsc[:]))
                    t_dn = sch.op("v", [t_m8], lambda e: e.tensor_add(
                        rtmp[:, 3:4], rtmp[:, 8:9], rtmp[:, 9:10]))
                    t_dn = sch.op("v", [t_dn], lambda e: e.reciprocal(
                        rtmp[:, 4:5], rtmp[:, 3:4]))
                    t_msk = sch.op("v", [t_dn], lambda e: e.tensor_scalar(
                        rtmp[:, 8:16], rsc[:], rtmp[:, 9:10], None, OP.is_ge))
                    t_cw = sch.op("v", [t_msk], lambda e: e.tensor_mul(
                        rsc[:], rsc[:], rtmp[:, 8:16]))
                    t_cw = sch.op("v", [t_cw], lambda e: e.tensor_scalar(
                        cwb[:], rsc[:], rtmp[:, 4:5], None, OP.mult))
                    deps2 = bw_deps(7) + [t_cw]
                    t_tp = sch.op("t", deps2, lambda e: e.matmul(
                        B[7][0:8, 0:128], cwb[:], IDENT[:], start=True, stop=True))
                    bank_write(7, t_tp)
                    t_ec = sch.op("v", [t_tp], lambda e, tc=tc: e.tensor_copy(
                        CW8[:, tc * 128:(tc + 1) * 128], B[7][0:8, 0:128]))
                    bank_read(7, t_ec)
                    prev_cw = t_ec
                comb_done = sch.fence()
                ph2_scope.close()
                tap("t_cw8", CW8[:], comb_done)

                with ExitStack() as head_scope:
                    hsb = lambda n, s, d: head_scope.enter_context(
                        nc.sbuf_tensor(f"{n}_p{p}", s, d))
                    wq_buf = [hsb("wq0", [128, KC, 256], BF16),
                              hsb("wq1", [128, KC, 256], BF16)]
                    wq_tok = [None, None]
                    wq_rd = [None, None]
                    xmix = hsb("xmix", [128, 3, T + 3], BF16)
                    zraw = hsb("zraw", [128, T], BF16)
                    zsil = zraw
                    qF = hsb("qF", [128, T], BF16)
                    kF = hsb("kF", [128, T], BF16)
                    vF = hsb("vF", [128, T], BF16)
                    sqs = hsb("sqs", [128, 512], BF16)
                    rrow = hsb("rrow", [1, 2 * T], BF16)
                    diag = hsb("diag", [128, 8, 128], BF16)
                    Dneg = hsb("Dneg", [128, T], F32)
                    BCb = hsb("BCb", [128, T], BF16)
                    BCe = hsb("BCe", [128, T], BF16)
                    BCq = hsb("BCq", [128, T], BF16)
                    BCk = hsb("BCk", [128, T], BF16)
                    KN = hsb("KN", [128, T], BF16)
                    KB = hsb("KB", [128, T], BF16)
                    QN = hsb("QN", [128, T], BF16)
                    QG = hsb("QG", [128, T], BF16)
                    kntok = hsb("kntok", [128, T], BF16)
                    vtok = hsb("vtok", [128, T], BF16)
                    Dch2 = [hsb("Dch0", [128, CH], F32), hsb("Dch1", [128, CH], F32)]
                    ETch2 = [hsb("ETch0", [128, CH], F32), hsb("ETch1", [128, CH], F32)]
                    EmS2 = [hsb("EmS0", [128, CH], BF16), hsb("EmS1", [128, CH], BF16)]
                    ETmU2 = [hsb("ETmU0", [128, CH], BF16), hsb("ETmU1", [128, CH], BF16)]
                    ETmI2 = [hsb("ETmI0", [128, CH], BF16), hsb("ETmI1", [128, CH], BF16)]
                    Lc2 = [hsb("Lc0", [128, CH], BF16), hsb("Lc1", [128, CH], BF16)]
                    LcT2 = [hsb("LcT0", [128, CH], BF16), hsb("LcT1", [128, CH], BF16)]
                    Lp22 = [hsb("Lp20", [128, 2 * CH], BF16), hsb("Lp21", [128, 2 * CH], BF16)]
                    attnT2 = [hsb("attnT0", [128, CH], BF16), hsb("attnT1", [128, CH], BF16)]
                    Xsb2 = [hsb("Xsb0", [128, 256], BF16), hsb("Xsb1", [128, 256], BF16)]
                    WFn2 = [hsb("WFn0", [128, CH], BF16), hsb("WFn1", [128, CH], BF16)]
                    vnew2 = [hsb("vnew0", [128, 128], BF16), hsb("vnew1", [128, 128], BF16)]
                    Sb16 = hsb("Sb16", [128, 128], BF16)
                    kdt2 = [hsb("kdt0", [128, 128], BF16), hsb("kdt1", [128, 128], BF16)]
                    colscr = hsb("colscr", [128, 8], F32)
                    rmsjunk = hsb("rmsjunk", [128, 128], BF16)
                    coretok = hsb("coretok", [128, NCH, 128], BF16)
                    rmsc = hsb("rmsc", [128, NCH], F32)

                    def prefetch_wq(m, deps):
                        slot = m % 2
                        src = wqkvz_d[:, 256 * m:256 * (m + 1)].rearrange(
                            "(k p) c -> p k c", p=128)
                        extra = [wq_rd[slot]] if wq_rd[slot] else []
                        wq_tok[slot] = sch.dma("g", f"dwq{p}s{slot}", deps + extra,
                                               lambda e, s=src, d=wq_buf[slot]:
                                               e.dma_start(out=d[:], in_=s))

                    prefetch_wq(0, comb_done)
                    prefetch_wq(1, comb_done)

                    from collections import deque

                    head_state = {"prev_conv": [], "prev_ho": None,
                                  "prev_gate": [], "sb_chain": None}

                    def make_prepA(h):
                        stp = {"ev": {}, "convt": {}, "conv_mms": []}
                        th = []
                        zth = []
                        gate_x = list(head_state["prev_conv"]) + \
                            ([head_state["prev_ho"]] if head_state["prev_ho"] else [])

                        def t_hi_f(stp=stp, h=h, gate_x=gate_x):
                            stp["hi"] = sch.op("v", gate_x, lambda e: e.tensor_copy(
                                xmix[:, :, 0:3], halo[:, 3 * h:3 * h + 3, :]))
                        th.append(t_hi_f)

                        for ob in range(3):
                            m = 2 * h + (ob // 2)
                            slot = m % 2
                            obh = ob % 2
                            for n in range(2):
                                bi = (ob * 2 + n) % 2
                                for kq in range(4):
                                    def mmf(ob=ob, n=n, bi=bi, kq=kq, slot=slot,
                                            obh=obh, stp=stp, gate_x=gate_x):
                                        wq = wq_buf[slot]
                                        deps = (bw_deps(bi) + [wq_tok[slot]]
                                                ) if kq == 0 else []
                                        tk = None
                                        for k in range(4 * kq, 4 * kq + 4):
                                            tk = sch.op("t", deps if k == 4 * kq == 0 or
                                                        (kq == 0 and k == 0) else [],
                                                        lambda e, k=k, obh=obh, n=n,
                                                        bi=bi, wq=wq: e.matmul(
                                                B[bi][:, 0:512],
                                                wq[:, k, obh * 128:(obh + 1) * 128],
                                                xnF[:, k, n * 512:(n + 1) * 512],
                                                start=(k == 0), stop=(k == KC - 1),
                                                skip_group_check=True))
                                        if kq == 3:
                                            bank_write(bi, tk)
                                            wq_rd[slot] = tk
                                            t_e = sch.op("a", [tk] + gate_x,
                                                         lambda e, ob=ob, n=n, bi=bi:
                                                         e.copy(
                                                xmix[:, ob, 3 + n * 512:
                                                     3 + (n + 1) * 512],
                                                B[bi][:, 0:512]))
                                            bank_read(bi, t_e)
                                            stp["ev"][(ob, n)] = t_e
                                    th.append(mmf)

                        def t_ho_f(stp=stp, h=h):
                            stp["ho"] = sch.op("v", [stp["ev"][(0, 1)], stp["ev"][(1, 1)],
                                                     stp["ev"][(2, 1)], stp["hi"]],
                                               lambda e: e.tensor_copy(
                                halo[:, 3 * h:3 * h + 3, :], xmix[:, :, T:T + 3]))
                        th.append(t_ho_f)

                        prev_cmm = {}
                        for part, dstn in ((0, "qF"), (1, "kF"), (2, "vF")):
                            dst = {"qF": qF, "kF": kF, "vF": vF}[dstn]
                            ps = part % 2

                            def diagf(part=part, ps=ps, h=h, stp=stp):
                                gate_d = list(head_state["prev_conv"]) + \
                                    ([prev_cmm[ps]] if ps in prev_cmm else [])
                                stp[f"dg{part}"] = []
                                for j in range(4):
                                    col = (part * 16 + h) * 4 + j
                                    stp[f"dg{part}"].append(sch.op(
                                        "v", gate_d, lambda e, ps=ps, j=j, col=col:
                                        e.tensor_scalar_mul(diag[:, ps * 4 + j, :],
                                                            IDENT[:],
                                                            convw_sb[:, col:col + 1])))
                            th.append(diagf)
                            for n in range(2):
                                bi = 2 + (part * 2 + n) % 2
                                bi = (part * 2 + n) % 2

                                def convf(part=part, n=n, bi=bi, ps=ps, dst=dst,
                                          dstn=dstn, stp=stp):
                                    deps = (bw_deps(bi) + [stp["hi"],
                                            stp["ev"][(part, 0)], stp["ev"][(part, 1)]]
                                            + stp[f"dg{part}"])
                                    tk = None
                                    for j in range(4):
                                        tk = sch.op("t", deps if j == 0 else [],
                                                    lambda e, j=j, part=part, n=n,
                                                    bi=bi, ps=ps: e.matmul(
                                            B[bi][:, 0:512], diag[:, ps * 4 + j, :],
                                            xmix[:, part, j + n * 512:
                                                 j + n * 512 + 512],
                                            start=(j == 0), stop=(j == 3),
                                            skip_group_check=True))
                                    bank_write(bi, tk)
                                    prev_cmm[ps] = tk
                                    stp["conv_mms"].append(tk)
                                    t_e = sch.op("a", [tk], lambda e, dst=dst, n=n,
                                                 bi=bi: e.activation(
                                        dst[:, n * 512:(n + 1) * 512],
                                        B[bi][:, 0:512], ACT.Silu))
                                    bank_read(bi, t_e)
                                    stp["convt"][(part, n)] = t_e
                                th.append(convf)

                        def l2f(stp=stp):
                            ss_toks = []
                            t_sq2 = None
                            for idx, srcn in enumerate(("qF", "kF")):
                                srct = {"qF": qF, "kF": kF}[srcn]
                                cv = [stp["convt"][(idx, 0)], stp["convt"][(idx, 1)]]
                                for n in range(2):
                                    dep_s = cv + ([t_sq2] if t_sq2 else [])
                                    t_sq2 = sch.op("v", dep_s, lambda e, srct=srct,
                                                   n=n: e.tensor_mul(
                                        sqs[:], srct[:, n * 512:(n + 1) * 512],
                                        srct[:, n * 512:(n + 1) * 512]))
                                    bi = n % 2
                                    deps = bw_deps(bi) + [t_sq2]
                                    t_mm = sch.op("t", deps, lambda e, bi=bi: e.matmul(
                                        B[bi][0:1, 0:512], ONECOL[:], sqs[:],
                                        start=True, stop=True))
                                    bank_write(bi, t_mm)
                                    t_sq2 = t_mm
                                    t_e = sch.op("v", [t_mm], lambda e, idx=idx, n=n,
                                                 bi=bi: e.tensor_copy(
                                        rrow[:, idx * T + n * 512:
                                             idx * T + (n + 1) * 512],
                                        B[bi][0:1, 0:512]))
                                    bank_read(bi, t_e)
                                    ss_toks.append(t_e)
                            t_ln2 = sch.op("a", ss_toks, lambda e: e.activation(
                                rrow[:], rrow[:], ACT.Ln, bias=EPS_L2[0:1, 0:1]))
                            stp["rr"] = sch.op("a", [t_ln2], lambda e: e.activation(
                                rrow[:], rrow[:], ACT.Exp, scale=-0.5))
                        th.append(l2f)

                        if h + 1 < NH:
                            def pf(h=h):
                                prefetch_wq(2 * h + 2, [])
                            th.append(pf)

                        m = 2 * h + 1
                        slot = m % 2
                        for n in range(2):
                            bi = 6 + n
                            bi = n % 2
                            for kq in range(4):
                                def zmf(n=n, bi=bi, kq=kq, slot=slot, stp=stp):
                                    wq = wq_buf[slot]
                                    gate_z = list(head_state["prev_gate"])
                                    deps = (bw_deps(bi) + [wq_tok[slot]]
                                            ) if kq == 0 else []
                                    tk = None
                                    for k in range(4 * kq, 4 * kq + 4):
                                        tk = sch.op("t", deps if (kq == 0 and k == 0)
                                                    else [],
                                                    lambda e, k=k, n=n, bi=bi, wq=wq:
                                                    e.matmul(
                                            B[bi][:, 0:512], wq[:, k, 128:256],
                                            xnF[:, k, n * 512:(n + 1) * 512],
                                            start=(k == 0), stop=(k == KC - 1),
                                            skip_group_check=True))
                                    if kq == 3:
                                        bank_write(bi, tk)
                                        wq_rd[slot] = tk
                                        t_e = sch.op("a", [tk] + gate_z,
                                                     lambda e, n=n, bi=bi: e.copy(
                                            zraw[:, n * 512:(n + 1) * 512],
                                            B[bi][:, 0:512]))
                                        bank_read(bi, t_e)
                                        stp["ev"][(3, n)] = t_e
                                zth.append(zmf)

                        def zsf(stp=stp, h=h):
                            stp["zs"] = sch.op("a", [stp["ev"][(3, 0)],
                                                     stp["ev"][(3, 1)]],
                                               lambda e: e.activation(
                                zsil[:], zraw[:], ACT.Silu))
                            if h + 1 < NH:
                                prefetch_wq(2 * h + 3, [])
                        zth.append(zsf)
                        return th, zth, stp

                    def emit_prepB(h, stp):
                        out = {}
                        conv_q = [stp["convt"][(0, 0)], stp["convt"][(0, 1)]]
                        conv_k = [stp["convt"][(1, 0)], stp["convt"][(1, 1)]]
                        conv_v = [stp["convt"][(2, 0)], stp["convt"][(2, 1)]]
                        gate_b = list(head_state["prev_gate"])
                        t_kn = t_kb = t_qn = t_qg = None
                        for i, (kind, srcf, dstf) in enumerate((
                                ("k", kF, KN), ("b", None, KB),
                                ("q", qF, QN), ("e", None, QG))):
                            for n in range(2):
                                bi = 4 + (n % 2)
                                deps = bw_deps(bi) + [stp["rr"], tbb, tegc] + gate_b
                                if kind == "b":
                                    t_mm = sch.op("t", deps, lambda e, h=h, n=n, bi=bi:
                                                  e.matmul(
                                        B[bi][:, 0:512],
                                        SEL16B[:, h * 128:(h + 1) * 128],
                                        B16b[:, n * 512:(n + 1) * 512],
                                        start=True, stop=True))
                                elif kind == "e":
                                    t_mm = sch.op("t", deps, lambda e, h=h, n=n, bi=bi:
                                                  e.matmul(
                                        B[bi][:, 0:512],
                                        SEL16B[:, h * 128:(h + 1) * 128],
                                        EGC16b[:, n * 512:(n + 1) * 512],
                                        start=True, stop=True))
                                elif kind == "q":
                                    t_mm = sch.op("t", deps, lambda e, n=n, bi=bi:
                                                  e.matmul(
                                        B[bi][:, 0:512], ONESC[:],
                                        rrow[:, n * 512:(n + 1) * 512],
                                        start=True, stop=True))
                                else:
                                    t_mm = sch.op("t", deps, lambda e, n=n, bi=bi:
                                                  e.matmul(
                                        B[bi][:, 0:512], ONES1[:],
                                        rrow[:, T + n * 512: T + (n + 1) * 512],
                                        start=True, stop=True))
                                bank_write(bi, t_mm)
                                if kind == "k":
                                    dep2 = conv_k + [t_mm] + gate_b
                                    t = sch.op("v", dep2, lambda e, n=n, bi=bi:
                                               e.tensor_mul(
                                        KN[:, n * 512:(n + 1) * 512],
                                        kF[:, n * 512:(n + 1) * 512],
                                        B[bi][:, 0:512]))
                                    t_kn = t if n == 1 else t_kn or t
                                    out.setdefault("kn", []).append(t)
                                elif kind == "b":
                                    dep2 = out["kn"] + [t_mm]
                                    t = sch.op("v", dep2, lambda e, n=n, bi=bi:
                                               e.tensor_mul(
                                        KB[:, n * 512:(n + 1) * 512],
                                        KN[:, n * 512:(n + 1) * 512],
                                        B[bi][:, 0:512]))
                                    out.setdefault("kb", []).append(t)
                                elif kind == "q":
                                    dep2 = conv_q + [t_mm] + gate_b
                                    t = sch.op("v", dep2, lambda e, n=n, bi=bi:
                                               e.tensor_mul(
                                        QN[:, n * 512:(n + 1) * 512],
                                        qF[:, n * 512:(n + 1) * 512],
                                        B[bi][:, 0:512]))
                                    out.setdefault("qn", []).append(t)
                                else:
                                    dep2 = out["qn"] + [t_mm]
                                    t = sch.op("v", dep2, lambda e, n=n, bi=bi:
                                               e.tensor_mul(
                                        QG[:, n * 512:(n + 1) * 512],
                                        QN[:, n * 512:(n + 1) * 512],
                                        B[bi][:, 0:512]))
                                    out.setdefault("qg", []).append(t)
                                bank_read(bi, t)
                        out["t_kb"] = out["kb"][-1]
                        out["t_qg"] = out["qg"][-1]

                        t_knt, t_vt = [], []
                        for c in range(NCH):
                            bi = 6 + (c % 2)
                            for srcd, dst, tok_list, dd in (
                                    (KN, kntok, t_knt, out["kn"]),
                                    (vF, vtok, t_vt, conv_v + gate_b)):
                                deps = bw_deps(bi) + dd
                                t_mm = sch.op("t", deps, lambda e, c=c, bi=bi,
                                              srcd=srcd: e.matmul(
                                    B[bi][:, 0:128], srcd[:, c * 128:(c + 1) * 128],
                                    IDENT[:], start=True, stop=True))
                                bank_write(bi, t_mm)
                                if c % 2 == 0:
                                    t_e = sch.op("v", [t_mm], lambda e, c=c, bi=bi,
                                                 dst=dst: e.tensor_copy(
                                        dst[:, c * 128:(c + 1) * 128], B[bi][:, 0:128]))
                                else:
                                    t_e = sch.op("a", [t_mm], lambda e, c=c, bi=bi,
                                                 dst=dst: e.copy(
                                        dst[:, c * 128:(c + 1) * 128], B[bi][:, 0:128]))
                                bank_read(bi, t_e)
                                tok_list.append(t_e)
                        out["t_knt"] = t_knt
                        out["t_vt"] = t_vt

                        t_nd = []
                        for n in range(2):
                            bi = 2 + n
                            deps = bw_deps(bi) + [tgc] + gate_b
                            t_mm = sch.op("t", deps, lambda e, h=h, n=n, bi=bi:
                                          e.matmul(
                                B[bi][:, 0:512], NSEL16F[:, h * 128:(h + 1) * 128],
                                GC16[:, n * 512:(n + 1) * 512],
                                start=True, stop=True))
                            bank_write(bi, t_mm)
                            t_e = sch.op("v", [t_mm] + gate_b, lambda e, n=n, bi=bi:
                                         e.tensor_copy(
                                Dneg[:, n * 512:(n + 1) * 512], B[bi][:, 0:512]))
                            bank_read(bi, t_e)
                            t_nd.append(t_e)
                        out["t_nd"] = t_nd
                        out["t_sb"] = sch.op("v", gate_b + t_nd, lambda e, h=h:
                                             e.tensor_copy(
                            Sb16[:], S_sb[:, h * 128:(h + 1) * 128]))
                        return out

                    thA, zthA, stpA = make_prepA(0)
                    for f in thA:
                        f()
                    for f in zthA:
                        f()

                    fillq = deque()

                    def drain(n=1, reserve=12):
                        for _ in range(n):
                            if len(fillq) > reserve:
                                fillq.popleft()()

                    for h in range(LIM_HEADS):
                        drain(6, 0)
                        while "rr" not in stpA and fillq:
                            fillq.popleft()()
                        pb_out = emit_prepB(h, stpA)
                        t_kb = pb_out["t_kb"]
                        t_qg = pb_out["t_qg"]
                        t_knt = pb_out["t_knt"]
                        t_vt = pb_out["t_vt"]
                        t_nd = pb_out["t_nd"]
                        t_sb = pb_out["t_sb"]
                        head_state["prev_conv"] = list(stpA["conv_mms"])
                        head_state["prev_ho"] = stpA["ho"]
                        cur_stp = stpA
                        if h + 1 < LIM_HEADS:
                            thN, zthN, stpN = make_prepA(h + 1)
                            fillq.extend(thN)
                        else:
                            thN, zthN, stpN = [], [], None
                        prev_end = [[], []]

                        def emit_chunk(c):
                            pb = c % 2
                            XB = 3 if pb == 0 else 6
                            QB = 4 if pb == 0 else 7
                            pe = prev_end[pb]
                            Dch, ETch = Dch2[pb], ETch2[pb]
                            EmS, ETmU, ETmI = EmS2[pb], ETmU2[pb], ETmI2[pb]
                            Lc, LcT, Lp2 = Lc2[pb], LcT2[pb], Lp22[pb]
                            attnT, Xsb, WFn = attnT2[pb], Xsb2[pb], WFn2[pb]
                            vnew, kdt = vnew2[pb], kdt2[pb]
                            cs0 = 4 * pb
                            gc_col = GCtok[:, c * 16 + h: c * 16 + h + 1]
                            b_col = Btok[:, c * 16 + h: c * 16 + h + 1]
                            be_col = BEGCtok[:, c * 16 + h: c * 16 + h + 1]
                            csl = slice(c * 128, (c + 1) * 128)
                            st = {"toks": []}

                            def tk(t):
                                st["toks"].append(t)
                                return t

                            def s_prep():
                                st["da"] = tk(sch.op("v", t_nd + pe, lambda e:
                                                     e.tensor_scalar(
                                    Dch[:], Dneg[:, csl], gc_col, None, OP.add)))
                                e2p = tk(sch.op("v", [st["da"]] + pe, lambda e:
                                                e.scalar_tensor_tensor(
                                    ETch[:], Dch[:], -1.0, MB_IU[:],
                                    OP.mult, OP.add)))
                                e1p = tk(sch.op("v", [e2p], lambda e:
                                                e.tensor_add(Dch[:], Dch[:], MB_SL[:])))
                                st["m1"] = tk(sch.op("a", [e1p] + pe, lambda e:
                                                     e.activation(EmS[:], Dch[:], ACT.Exp)))
                                st["m3"] = tk(sch.op("a", [e2p] + pe, lambda e:
                                                     e.activation(ETmI[:], ETch[:], ACT.Exp)))
                                st["m2"] = tk(sch.op("v", [st["m3"]] + pe, lambda e:
                                                     e.tensor_mul(ETmU[:], ETmI[:], M_SU[:])))

                            def s_grams():
                                depsg = bw_deps(2) + [t_kb, t_qg]
                                g2 = tk(sch.op("t", depsg, lambda e: e.matmul(
                                    B[2][:, 0:128], KB[:, csl], KN[:, csl],
                                    start=True, stop=True)))
                                g2t = tk(sch.op("t", [], lambda e: e.matmul(
                                    B[2][:, 128:256], KN[:, csl], KB[:, csl],
                                    start=True, stop=True)))
                                aqk = tk(sch.op("t", [], lambda e: e.matmul(
                                    B[2][:, 256:384], KN[:, csl], QN[:, csl],
                                    start=True, stop=True)))
                                bank_write(2, aqk)
                                st["L"] = tk(sch.op("v", [g2, st["m1"]] + pe, lambda e:
                                                    e.scalar_tensor_tensor(
                                    Lc[:], B[2][:, 0:128], -1.0, EmS[:],
                                    OP.mult, OP.mult)))
                                st["LT"] = tk(sch.op("v", [g2t, st["m2"]] + pe, lambda e:
                                                     e.scalar_tensor_tensor(
                                    LcT[:], B[2][:, 128:256], -1.0, ETmU[:],
                                    OP.mult, OP.mult)))
                                st["at"] = tk(sch.op("v", [aqk, st["m3"]] + pe, lambda e:
                                                     e.tensor_mul(
                                    attnT[:], B[2][:, 256:384], ETmI[:])))
                                for tt in (st["L"], st["LT"], st["at"]):
                                    bank_read(2, tt)

                            def s_x0():
                                st["x0a"] = tk(sch.op("v", t_vt + pe, lambda e:
                                                      e.tensor_scalar(
                                    Xsb[:, 0:128], vtok[:, csl], b_col, None, OP.mult)))
                                st["x0b"] = tk(sch.op("v", t_knt + [t_bg] + pe, lambda e:
                                                      e.tensor_scalar(
                                    Xsb[:, 128:256], kntok[:, csl], be_col,
                                    None, OP.mult)))
                                st["lp"] = (Lc, st["L"])
                                st["lpt"] = (LcT, st["LT"])
                                st["xev"] = None

                            def s_lev(lev):
                                sch.op("t", [], lambda e: e.ldweights(IDENT[:, 0:8]))
                                lp_cur, t_lp = st["lp"]
                                lpt_cur, t_lpt = st["lpt"]
                                if lev == 0:
                                    depsx = bw_deps(XB) + [st["x0a"], st["x0b"]]
                                    tk(sch.op("t", depsx, lambda e: e.matmul(
                                        B[XB][:, 0:256], IDENT[:], Xsb[:],
                                        start=True, stop=False, skip_group_check=True)))
                                    t_ap = tk(sch.op("t", [t_lpt], lambda e, lpt=lpt_cur:
                                                     e.matmul(
                                        B[XB][:, 0:256], lpt[:], Xsb[:],
                                        start=False, stop=False, skip_group_check=True)))
                                else:
                                    t_ap = tk(sch.op("t", [t_lpt, st["xev"]],
                                                     lambda e, lpt=lpt_cur, lev=lev:
                                                     e.matmul(
                                        B[XB][:, 0:256], lpt[:], Xsb[:],
                                        start=False, stop=(lev == 6),
                                        skip_group_check=True)))
                                bank_write(XB, t_ap)
                                if lev % 2 == 0:
                                    st["xev"] = tk(sch.op("v", [t_ap], lambda e:
                                                          e.tensor_copy(Xsb[:], B[XB][:, 0:256])))
                                else:
                                    st["xev"] = tk(sch.op("a", [t_ap], lambda e:
                                                          e.copy(Xsb[:], B[XB][:, 0:256])))
                                bank_read(XB, st["xev"])
                                if lev < 6:
                                    depsq = bw_deps(QB) + [t_lp, t_lpt]
                                    tk(sch.op("t", depsq, lambda e, lp=lp_cur,
                                              lpt=lpt_cur: e.matmul(
                                        B[QB][:, 0:128], lpt[:], lp[:],
                                        start=True, stop=True, skip_group_check=True)))
                                    t_s2 = tk(sch.op("t", [], lambda e, lp=lp_cur,
                                                     lpt=lpt_cur: e.matmul(
                                        B[QB][:, 128:256], lp[:], lpt[:],
                                        start=True, stop=True, skip_group_check=True)))
                                    bank_write(QB, t_s2)
                                    if lev % 2 == 0:
                                        t_qe = tk(sch.op("v", [t_s2] + pe, lambda e:
                                                         e.tensor_copy(Lp2[:], B[QB][:, 0:256])))
                                    else:
                                        t_qe = tk(sch.op("a", [t_s2] + pe, lambda e:
                                                         e.copy(Lp2[:], B[QB][:, 0:256])))
                                    bank_read(QB, t_qe)
                                    st["lp"] = (Lp2[:, 0:128], t_qe)
                                    st["lpt"] = (Lp2[:, 128:256], t_qe)

                            def s_wf():
                                depsw = bw_deps(XB) + [st["xev"]]
                                t_wf = tk(sch.op("t", depsw, lambda e: e.matmul(
                                    B[XB][:, 256:384], Xsb[:, 128:256], IDENT[:],
                                    start=True, stop=True, skip_group_check=True)))
                                bank_write(XB, t_wf)
                                st["wfe"] = tk(sch.op("v", [t_wf] + pe, lambda e:
                                                      e.tensor_scalar_mul(
                                    WFn[:], B[XB][:, 256:384], -1.0)))
                                bank_read(XB, st["wfe"])

                            def s_scan():
                                sch.op("t", [], lambda e: e.ldweights(IDENT[:, 0:8]))
                                t_sb_in = st["sb_in"]
                                depsv = bw_deps(5) + [st["xev"], t_sb_in]
                                t_v1 = tk(sch.op("t", depsv, lambda e: e.matmul(
                                    B[5][:, 0:128], IDENT[:], Xsb[:, 0:128],
                                    start=True, stop=False, skip_group_check=True)))
                                t_v2 = tk(sch.op("t", [st["wfe"]], lambda e: e.matmul(
                                    B[5][:, 0:128], WFn[:], Sb16[:],
                                    start=False, stop=True, skip_group_check=True)))
                                bank_write(5, t_v2)
                                t_vn = tk(sch.op("v", [t_v2] + pe, lambda e:
                                                 e.tensor_copy(vnew[:], B[5][:, 0:128])))
                                bank_read(5, t_vn)
                                depso = bw_deps(XB) + [t_qg, t_sb_in, st["wfe"],
                                                       st["xev"]]
                                t_o1 = tk(sch.op("t", depso, lambda e: e.matmul(
                                    B[XB][:, 384:512], QG[:, csl], Sb16[:],
                                    start=True, stop=False, skip_group_check=True)))
                                t_o2 = tk(sch.op("t", [t_vn, st["at"]], lambda e:
                                                 e.matmul(
                                    B[XB][:, 384:512], attnT[:], vnew[:],
                                    start=False, stop=True, skip_group_check=True)))
                                bank_write(XB, t_o2)
                                t_ce = tk(sch.op("v", [t_o2], lambda e, c=c:
                                                 e.tensor_copy(coretok[:, c, :],
                                                               B[XB][:, 384:512])))
                                bank_read(XB, t_ce)
                                tk(sch.op("a", [t_ce], lambda e, c=c: e.activation(
                                    rmsjunk[:], coretok[:, c, :], ACT.Square,
                                    accum_out=rmsc[:, c:c + 1])))
                                t_kd1 = tk(sch.op("v", t_nd + pe, lambda e, c=c:
                                                  e.tensor_scalar(
                                    colscr[:, cs0:cs0 + 1], gc_col,
                                    Dneg[:, c * 128 + 127:c * 128 + 128],
                                    None, OP.add)))
                                t_kd2 = tk(sch.op("a", [t_kd1] + pe, lambda e:
                                                  e.activation(
                                    colscr[:, cs0 + 1:cs0 + 2], colscr[:, cs0:cs0 + 1],
                                    ACT.Exp, scale=-1.0)))
                                t_egl = tk(sch.op("a", t_nd + pe, lambda e, c=c:
                                                  e.activation(
                                    colscr[:, cs0 + 2:cs0 + 3],
                                    Dneg[:, c * 128 + 127:c * 128 + 128],
                                    ACT.Exp, scale=-1.0)))
                                t_kd3 = tk(sch.op("v", [t_kd2] + t_knt + pe, lambda e:
                                                  e.tensor_scalar(
                                    kdt[:], kntok[:, csl],
                                    colscr[:, cs0 + 1:cs0 + 2], None, OP.mult)))
                                depss = bw_deps(5) + [t_kd3, t_vn]
                                t_su = tk(sch.op("t", depss, lambda e: e.matmul(
                                    B[5][:, 128:256], kdt[:], vnew[:],
                                    start=True, stop=True, skip_group_check=True)))
                                bank_write(5, t_su)
                                t_sn = tk(sch.op("v", [t_su, t_egl], lambda e, h=h:
                                                 e.scalar_tensor_tensor(
                                    S_sb[:, h * 128:(h + 1) * 128],
                                    S_sb[:, h * 128:(h + 1) * 128],
                                    colscr[:, cs0 + 2:cs0 + 3], B[5][:, 128:256],
                                    OP.mult, OP.add)))
                                bank_read(5, t_sn)
                                st["sb_out"] = tk(sch.op("v", [t_sn], lambda e, h=h:
                                                         e.tensor_copy(
                                    Sb16[:], S_sb[:, h * 128:(h + 1) * 128])))

                            return st, s_prep, s_grams, s_x0, s_lev, s_wf, s_scan

                        t_sb_chain = t_sb
                        for c0 in range(0, LIM_CHUNKS, 2):
                            cA, cB = c0, c0 + 1
                            has_b = cB < LIM_CHUNKS
                            stA, prepA, gramsA, x0A, levA, wfA, scanA = emit_chunk(cA)
                            if has_b:
                                stB, prepB, gramsB, x0B, levB, wfB, scanB = emit_chunk(cB)
                            prepA()
                            if has_b:
                                prepB()
                            drain(2)
                            gramsA()
                            if has_b:
                                gramsB()
                            drain(2)
                            x0A()
                            if has_b:
                                x0B()
                            drain(1)
                            for lev in range(7):
                                levA(lev)
                                drain(1)
                                if has_b:
                                    levB(lev)
                                drain(1)
                            wfA()
                            if has_b:
                                wfB()
                            drain(2)
                            stA["sb_in"] = t_sb_chain
                            scanA()
                            drain(2)
                            t_sb_chain = stA["sb_out"]
                            if has_b:
                                stB["sb_in"] = t_sb_chain
                                scanB()
                                drain(2)
                                t_sb_chain = stB["sb_out"]
                            prev_end[cA % 2] = stA["toks"]
                            if has_b:
                                prev_end[cB % 2] = stB["toks"]


                        drain(len(fillq), 0)
                        t_zs = cur_stp["zs"]
                        chunks_done = sch.fence()
                        gate_toks = []
                        if h == NH - 1:
                            tap("t_s", S_sb[:], chunks_done)
                        t_rstd = sch.op("v", chunks_done, lambda e: e.tensor_scalar_mul(
                            rmsc[:], rmsc[:], 1.0 / 128.0))
                        t_rstd = sch.op("a", [t_rstd], lambda e: e.activation(
                            rmsc[:], rmsc[:], ACT.Ln, bias=EPS_L2[:, 0:1]))
                        t_rstd = sch.op("a", [t_rstd], lambda e: e.activation(
                            rmsc[:], rmsc[:], ACT.Exp, scale=-0.5))
                        for c in range(LIM_CHUNKS):
                            t_gt = sch.op("v", [t_rstd], lambda e, c=c: e.tensor_scalar(
                                coretok[:, c, :], coretok[:, c, :], rmsc[:, c:c + 1],
                                None, OP.mult))
                            bi = 2 + (c % 2)
                            depsc = bw_deps(bi) + [t_gt]
                            t_tp = sch.op("t", depsc, lambda e, c=c, bi=bi: e.matmul(
                                B[bi][:, 0:128], coretok[:, c, :], IDENT[:],
                                start=True, stop=True))
                            bank_write(bi, t_tp)
                            t_cf = sch.op("v", [t_tp, t_zs], lambda e, c=c, h=h, bi=bi:
                                          e.tensor_mul(coreF[:, h, c * 128:(c + 1) * 128],
                                                       B[bi][:, 0:128],
                                                       zsil[:, c * 128:(c + 1) * 128]))
                            bank_read(bi, t_cf)
                            gate_toks.append(t_cf)

                        head_state["prev_gate"] = gate_toks
                        fillq.extend(zthN)
                        stpA = stpN

                heads_done = sch.fence()
                tap("t_coref", coreF[:], heads_done)

                with ExitStack() as wo_scope:
                    wo_sb = wo_scope.enter_context(
                        nc.sbuf_tensor(f"wo_p{p}", [128, 2, KC, 256], BF16))
                    xog = wo_scope.enter_context(
                        nc.sbuf_tensor(f"xog_p{p}", [128, 2, 256], F32))
                    t_wo = [None, None]
                    t_wo_rd = [None, None]
                    t_xog = [None, None]
                    for og in range(2 * NOG if not SKIP_WOUT else 0):
                        ws = og % 2
                        dep_wo = ([t_wo[ws]] if t_wo[ws] else heads_done) + \
                            ([t_wo_rd[ws]] if t_wo_rd[ws] else [])
                        t_l = sch.dma("g", f"dwo{p}s{ws}", dep_wo, lambda e, og=og, ws=ws:
                                      e.dma_start(
                            out=wo_sb[:, ws],
                            in_=wout_d[:, og * 256:(og + 1) * 256]
                            .rearrange("(k p) c -> p k c", p=128)))
                        t_wos = sch.op("v", [t_l], lambda e, ws=ws: e.tensor_scalar_mul(
                            wo_sb[:, ws].rearrange("p k c -> p (k c)"),
                            wo_sb[:, ws].rearrange("p k c -> p (k c)"), nw_sb[:]))
                        t_wo[ws] = t_wos
                        for tc in range(NTC):
                            oslot = tc % 2
                            dep_x = [t_xog[oslot]] if t_xog[oslot] else list(heads_done)
                            t_x = sch.dma("s", f"dxo{p}s{oslot}", dep_x,
                                          lambda e, tc=tc, og=og, oslot=oslot:
                                          e.dma_start(
                                out=xog[:, oslot, :],
                                in_=x_d[t0 + tc * 128: t0 + (tc + 1) * 128,
                                        og * 256:(og + 1) * 256]))
                            bslot = 4 + (tc % 2)
                            deps0 = bw_deps(bslot) + [t_wos] + heads_done
                            tk = None
                            for k in range(KC):
                                tk = sch.op("t", deps0 if k == 0 else [],
                                            lambda e, k=k, tc=tc, bslot=bslot, ws=ws:
                                            e.matmul(
                                    B[bslot][:, 0:256],
                                    coreF[:, k, tc * 128:(tc + 1) * 128],
                                    wo_sb[:, ws, k, :], start=(k == 0),
                                    stop=(k == KC - 1)))
                            bank_write(bslot, tk)
                            t_wo_rd[ws] = tk
                            t_e = sch.op("v", [tk, t_x], lambda e, tc=tc, og=og,
                                         oslot=oslot, bslot=bslot: e.tensor_add(
                                acc[:, tc, og * 256:(og + 1) * 256],
                                B[bslot][:, 0:256], xog[:, oslot, :]))
                            bank_read(bslot, t_e)
                            t_xog[oslot] = t_e
                wo_done = sch.fence()

                with ExitStack() as ffn_scope:
                    fsb = lambda n, s, d: ffn_scope.enter_context(
                        nc.sbuf_tensor(f"{n}_p{p}", s, d))
                    acts = fsb("acts", [128, 2, 8, T], BF16)
                    wgu_sb = fsb("wgu", [128, 2, 2, KC, 128], BF16)
                    wd_sb = fsb("wd", [128, 2, 2, 8, 256], BF16)
                    obuf = fsb("obuf", [128, 2, 512], F32)
                    gsil = fsb("gsil", [128, 512], BF16)
                    upsc = fsb("upsc", [128, 512], BF16)
                    cwbt = fsb("cwbt", [128, T], BF16)

                    t_wgu = None
                    wgu_tok = [None, None]
                    wgu_rd = [None, None]
                    t_ob = [None, None]
                    t_ac_last = None
                    wd_toks = {}

                    for gi, grp in enumerate(EGRPS if not SKIP_FFN else []):
                        last_grp = gi == len(EGRPS) - 1
                        gfence = sch.fence()
                        act_toks = []
                        for ei, ex in enumerate(grp):
                            t_cb = []
                            for n in range(2):
                                deps = bw_deps(0) + comb_done + gfence
                                t_mm = sch.op("t", deps, lambda e, ex=ex, n=n: e.matmul(
                                    B[0][:, 0:512], SEL8B[:, ex * 128:(ex + 1) * 128],
                                    CW8[:, n * 512:(n + 1) * 512],
                                    start=True, stop=True))
                                bank_write(0, t_mm)
                                t_e = sch.op("v", [t_mm], lambda e, n=n: e.tensor_copy(
                                    cwbt[:, n * 512:(n + 1) * 512], B[0][:, 0:512]))
                                bank_read(0, t_e)
                                t_cb.append(t_e)
                            for fc in range(8):
                                fslot = fc % 2
                                dep_w = ([wgu_tok[fslot]] if wgu_tok[fslot] else gfence) + \
                                    ([wgu_rd[fslot]] if wgu_rd[fslot] else [])
                                t_wg1 = sch.dma("g", f"dwg{p}s{fslot}", dep_w,
                                                lambda e, ex=ex, fc=fc, fslot=fslot:
                                                e.dma_start(
                                    out=wgu_sb[:, fslot, 0, :, :],
                                    in_=wg_d[ex, :, fc * 128:(fc + 1) * 128]
                                    .rearrange("(k p) c -> p k c", p=128)))
                                t_wgu = sch.dma("g", f"dwu{p}s{fslot}", [],
                                                lambda e, ex=ex, fc=fc, fslot=fslot:
                                                e.dma_start(
                                    out=wgu_sb[:, fslot, 1, :, :],
                                    in_=wu_d[ex, :, fc * 128:(fc + 1) * 128]
                                    .rearrange("(k p) c -> p k c", p=128)))
                                wgu_tok[fslot] = t_wgu
                                gb = (1, 2) if fc % 2 == 0 else (4, 5)
                                ub = (3, 6) if fc % 2 == 0 else (0, 7)
                                depsk = (bw_deps(gb[0]) + bw_deps(gb[1]) +
                                         bw_deps(ub[0]) + bw_deps(ub[1]) +
                                         [t_wg1, t_wgu])
                                tg = tu = None
                                for k in range(KC):
                                    st = k == 0
                                    sp = k == KC - 1
                                    for n in range(2):
                                        tg = sch.op("t", depsk if (k == 0 and n == 0)
                                                    else [],
                                                    lambda e, k=k, n=n, fslot=fslot,
                                                    st=st, sp=sp, gb=gb: e.matmul(
                                            B[gb[n]][:, 0:512],
                                            wgu_sb[:, fslot, 0, k, :],
                                            xnF[:, k, n * 512:(n + 1) * 512],
                                            start=st, stop=sp,
                                            skip_group_check=True))
                                    for n in range(2):
                                        tu = sch.op("t", [], lambda e, k=k, n=n,
                                                    fslot=fslot, st=st, sp=sp, ub=ub:
                                                    e.matmul(
                                            B[ub[n]][:, 0:512],
                                            wgu_sb[:, fslot, 1, k, :],
                                            xnF[:, k, n * 512:(n + 1) * 512],
                                            start=st, stop=sp,
                                            skip_group_check=True))
                                for bb in (*gb, *ub):
                                    bank_write(bb, tu)
                                wgu_rd[fslot] = tu
                                for n in range(2):
                                    t_sg = sch.op("a", [tu, t_ac_last],
                                                  lambda e, n=n, gb=gb: e.activation(
                                        gsil[:], B[gb[n]][:, 0:512], ACT.Silu))
                                    bank_read(gb[n], t_sg)
                                    t_up = sch.op("v", [tu, t_ac_last] + t_cb,
                                                  lambda e, n=n, ub=ub: e.tensor_mul(
                                        upsc[:], B[ub[n]][:, 0:512],
                                        cwbt[:, n * 512:(n + 1) * 512]))
                                    bank_read(ub[n], t_up)
                                    t_ac = sch.op("v", [t_sg, t_up],
                                                  lambda e, ei=ei, fc=fc, n=n:
                                                  e.tensor_mul(
                                        acts[:, ei, fc, n * 512:(n + 1) * 512],
                                        gsil[:], upsc[:]))
                                    act_toks.append(t_ac)
                                    t_ac_last = t_ac
                        if gi == 0:
                            tap("t_acts", acts[:], act_toks)

                        t_down_last = [None, None]
                        for og in range(2 * NOG):
                            ws = og % 2
                            new_wd = []
                            for ei, ex in enumerate(grp):
                                dep_wd = ([wd_toks[(ws, ei)]] if wd_toks.get((ws, ei))
                                          else list(wo_done)) + \
                                    ([t_down_last[ws]] if t_down_last[ws] else [])
                                wd_toks[(ws, ei)] = sch.dma(
                                    "g", f"dwd{p}s{ws}e{ei}", dep_wd,
                                    lambda e, ei=ei, ex=ex, og=og, ws=ws: e.dma_start(
                                        out=wd_sb[:, ws, ei, :, :],
                                        in_=wd_d[ex, :, og * 256:(og + 1) * 256]
                                        .rearrange("(f p) c -> p f c", p=128)))
                                new_wd.append(wd_toks[(ws, ei)])
                            for tc in range(NTC):
                                bslot = 4 + (tc % 2)
                                nmm = len(grp) * 8
                                deps0 = bw_deps(bslot) + act_toks + new_wd + wo_done
                                tk = None
                                i = 0
                                for ei in range(len(grp)):
                                    for fc in range(8):
                                        first = i == 0
                                        last = i == nmm - 1
                                        tk = sch.op("t", deps0 if first else [],
                                                    lambda e, ei=ei, fc=fc, tc=tc,
                                                    bslot=bslot, first=first, last=last,
                                                    ws=ws: e.matmul(
                                            B[bslot][:, 0:256],
                                            acts[:, ei, fc, tc * 128:(tc + 1) * 128],
                                            wd_sb[:, ws, ei, fc, :],
                                            start=first, stop=last,
                                            skip_group_check=True))
                                        i += 1
                                t_down_last[ws] = tk
                                bank_write(bslot, tk)
                                if not last_grp:
                                    t_e = sch.op("v", [tk], lambda e, tc=tc, og=og,
                                                 bslot=bslot: e.tensor_add(
                                        acc[:, tc, og * 256:(og + 1) * 256],
                                        acc[:, tc, og * 256:(og + 1) * 256],
                                        B[bslot][:, 0:256]))
                                    bank_read(bslot, t_e)
                                else:
                                    oslot = tc % 2
                                    dep_o = [t_ob[oslot]] if t_ob[oslot] else []
                                    t_s1 = sch.op("v", [tk] + dep_o,
                                                  lambda e, tc=tc, og=og, oslot=oslot,
                                                  bslot=bslot: e.tensor_add(
                                        obuf[:, oslot, 0:256], B[bslot][:, 0:256],
                                        acc[:, tc, og * 256:(og + 1) * 256]))
                                    bank_read(bslot, t_s1)
                                    t_st = sch.dma("s", f"do{p}s{oslot}", [t_s1],
                                                   lambda e, tc=tc, og=og, oslot=oslot:
                                                   e.dma_start(
                                        out=out_d[t0 + tc * 128: t0 + (tc + 1) * 128,
                                                  og * 256:(og + 1) * 256],
                                        in_=obuf[:, oslot, 0:256]))
                                    t_ob[oslot] = t_st

                sch.flush(block)

        sch.raw("s", sch.fence(), lambda e: e.nop())
        sch.flush(block)

    return nc


def _prep_inputs(inp):
    x = np.ascontiguousarray(inp["L0_attn_hidden_states"])
    wba = np.ascontiguousarray(inp["L0_attn_W_ba"])
    wba_re = np.ascontiguousarray(wba[:, list(range(0, 32, 2)) + list(range(1, 32, 2))])
    cw = np.ascontiguousarray(inp["L0_attn_conv_weight"])
    convw_re = np.zeros((128, 192), np.float32)
    for part in range(3):
        for h in range(NH):
            blk = cw[part * 2048 + h * 128: part * 2048 + (h + 1) * 128, :]
            convw_re[:, (part * 16 + h) * 4:(part * 16 + h) * 4 + 4] = blk
    shared = {
        "wqkvz": np.ascontiguousarray(inp["L0_attn_W_qkvz"]),
        "wba": wba_re,
        "convw": convw_re,
        "dtb": np.ascontiguousarray(inp["L0_attn_dt_bias"].reshape(16, 1)),
        "alog": np.ascontiguousarray(inp["L0_attn_A_log"].reshape(16, 1)),
        "nw": np.ascontiguousarray(inp["L0_attn_norm_weight"].reshape(128, 1)),
        "wout": np.ascontiguousarray(inp["L0_attn_W_out"]),
        "rw": np.ascontiguousarray(inp["L0_ffn_router_w"]),
        "wg": np.ascontiguousarray(inp["L0_ffn_gate_proj"]),
        "wu": np.ascontiguousarray(inp["L0_ffn_up_proj"]),
        "wd": np.ascontiguousarray(inp["L0_ffn_down_proj"]),
    }
    in_maps = []
    for c in range(NCORES):
        m = dict(shared)
        m["x"] = np.ascontiguousarray(x[c])
        in_maps.append(m)
    return in_maps


def kernel(**inputs):
    in_maps = _prep_inputs(inputs)
    nc = build_nc()
    res = run_bass_kernel_spmd(nc, in_maps, core_ids=list(range(NCORES)))
    out = np.stack([np.asarray(r["out"]) for r in res.results], axis=0)
    return out.astype(np.float32)


# version 17
# speedup vs baseline: 1.0252x; 1.0028x over previous
from contextlib import ExitStack

import numpy as np
import ml_dtypes

import concourse.bass as bass
import concourse.mybir as mybir
from concourse.bass_utils import run_bass_kernel_spmd

F32 = mybir.dt.float32
BF16 = mybir.dt.bfloat16
OP = mybir.AluOpType
ACT = mybir.ActivationFunctionType
AX = mybir.AxisListType.X

NCORES = 8
SEQ = 2048
D = 2048
NH = 16
DK = 128
NPASS = 2
T = SEQ // NPASS
CH = 128
NCH = T // CH
NTC = T // 128
KC = D // 128
FF = 1024
NEXP = 8
EGRPS = [(0, 1), (2, 3), (4, 5), (6, 7)]
NOG = D // 512

DEBUG_TAPS = ()
LIM_HEADS = NH
LIM_CHUNKS = NCH
CHUNK_LVL = 6
SKIP_FFN = False
SKIP_WOUT = False

ENGS = ("t", "v", "a", "g", "s")


class Tok:
    __slots__ = ("sem", "val")

    def __init__(self, sem, val):
        self.sem = sem
        self.val = val


class Sched:
    def __init__(self, nc, ctx):
        self.nc = nc
        self.ctx = ctx
        self.sems = {}
        self.counts = {}
        self.dma_streams = set()
        for name in ("t", "v", "a", "g", "s"):
            self.sems[name] = ctx.enter_context(nc.semaphore(f"sem_{name}"))
            self.counts[name] = 0
        self.hw = {e: {} for e in ENGS}
        self.pend = {e: [] for e in ENGS}

    def stream(self, name):
        if name not in self.sems:
            self.sems[name] = self.ctx.enter_context(self.nc.semaphore(f"sem_{name}"))
            self.counts[name] = 0
            self.dma_streams.add(name)
        return name

    def _waits(self, eng, deps):
        want = {}
        for d in deps:
            if d is None:
                continue
            if d.sem == eng and eng == "t":
                continue
            v = self.counts[d.sem] if d.sem in self.dma_streams else d.val
            if want.get(d.sem, 0) < v:
                want[d.sem] = v
        ws = []
        for s, v in want.items():
            if self.hw[eng].get(s, 0) >= v:
                continue
            self.hw[eng][s] = v
            ws.append((self.sems[s], v))
        return ws

    def op(self, eng, deps, fn):
        ws = self._waits(eng, deps)
        self.counts[eng] += 1
        val = self.counts[eng]
        sem = self.sems[eng]

        def run(e, ws=ws, fn=fn, sem=sem):
            for s, v in ws:
                e.wait_ge(s, v)
            fn(e).then_inc(sem, 1)

        self.pend[eng].append(run)
        return Tok(eng, val)

    def dma(self, eng, stream, deps, fn):
        stream = self.stream(stream)
        ws = self._waits(eng, deps)
        self.counts[stream] += 16
        val = self.counts[stream]
        sem = self.sems[stream]

        def run(e, ws=ws, fn=fn, sem=sem):
            for s, v in ws:
                e.wait_ge(s, v)
            fn(e).then_inc(sem, 16)

        self.pend[eng].append(run)
        return Tok(stream, val)

    def raw(self, eng, deps, fn):
        ws = self._waits(eng, deps)

        def run(e, ws=ws, fn=fn):
            for s, v in ws:
                e.wait_ge(s, v)
            fn(e)

        self.pend[eng].append(run)

    def fence(self):
        return [Tok(k, self.counts[k]) for k in self.counts]

    def flush(self, block):
        for eng, fns in self.pend.items():
            if not fns:
                continue
            meth = {"t": block.tensor, "v": block.vector, "a": block.scalar,
                    "g": block.gpsimd, "s": block.sync}[eng]

            def body(e, fns=list(fns)):
                for f in fns:
                    f(e)
            meth(body)
            self.pend[eng] = []


def _host_consts():
    bf = ml_dtypes.bfloat16
    i = np.arange(CH)
    c = {}
    c["IDENT"] = np.eye(128, dtype=bf)
    c["IDENT16F"] = np.eye(16, dtype=np.float32)
    c["M_SL"] = (i[:, None] > i[None, :]).astype(bf)
    c["M_SU"] = (i[:, None] < i[None, :]).astype(bf)
    c["M_IU"] = (i[:, None] <= i[None, :]).astype(bf)
    c["MB_SL"] = np.where(i[:, None] > i[None, :], 0.0, -1e30).astype(np.float32)
    c["MB_IU"] = np.where(i[:, None] <= i[None, :], 0.0, -1e30).astype(np.float32)
    sel16 = np.zeros((NH, NH, 128), dtype=bf)
    for h in range(NH):
        sel16[h, h, :] = 1.0
    c["SEL16B"] = sel16
    nsel16 = np.zeros((NH, NH, 128), dtype=np.float32)
    for h in range(NH):
        nsel16[h, h, :] = -1.0
    c["NSEL16F"] = nsel16
    sel8 = np.zeros((NEXP, NEXP, 128), dtype=bf)
    for e in range(NEXP):
        sel8[e, e, :] = 1.0
    c["SEL8B"] = sel8
    c["ONES1"] = np.ones((1, 128), dtype=bf)
    c["ONESC"] = np.full((1, 128), DK ** -0.5, dtype=bf)
    c["ONECOL"] = np.ones((128, 1), dtype=bf)
    c["CONSTS"] = np.broadcast_to(
        np.array([1e-5, 1e-6, 1.0, 0.0], np.float32), (128, 4)).copy()
    return c


def build_nc():
    nc = bass.Bass("TRN2", target_bir_lowering=False)

    x_d = nc.dram_tensor("x", [SEQ, D], F32, kind="ExternalInput").ap()
    wqkvz_d = nc.dram_tensor("wqkvz", [D, 8192], F32, kind="ExternalInput").ap()
    wba_d = nc.dram_tensor("wba", [D, 32], F32, kind="ExternalInput").ap()
    convw_d = nc.dram_tensor("convw", [128, 192], F32, kind="ExternalInput").ap()
    dtb_d = nc.dram_tensor("dtb", [16, 1], F32, kind="ExternalInput").ap()
    alog_d = nc.dram_tensor("alog", [16, 1], F32, kind="ExternalInput").ap()
    nw_d = nc.dram_tensor("nw", [128, 1], F32, kind="ExternalInput").ap()
    wout_d = nc.dram_tensor("wout", [D, D], F32, kind="ExternalInput").ap()
    rw_d = nc.dram_tensor("rw", [D, 8], F32, kind="ExternalInput").ap()
    wg_d = nc.dram_tensor("wg", [NEXP, D, FF], F32, kind="ExternalInput").ap()
    wu_d = nc.dram_tensor("wu", [NEXP, D, FF], F32, kind="ExternalInput").ap()
    wd_d = nc.dram_tensor("wd", [NEXP, FF, D], F32, kind="ExternalInput").ap()
    out_d = nc.dram_tensor("out", [SEQ, D], F32, kind="ExternalOutput").ap()

    taps = {}
    tapshapes = {
        "t_xnf": ([128, KC, T], BF16), "t_gc16": ([16, T], F32),
        "t_b16": ([16, T], F32), "t_qf": ([128, T], BF16),
        "t_kf": ([128, T], BF16), "t_vf": ([128, T], BF16),
        "t_coref": ([128, NH, T], BF16), "t_cw8": ([8, T], BF16),
        "t_acts": ([128, 2, 8, T], BF16), "t_zsil": ([128, T], BF16),
        "t_kn": ([128, T], BF16), "t_x0": ([128, 256], BF16),
        "t_xfin": ([128, 256], BF16), "t_s": ([128, NH * 128], F32),
        "t_attnt": ([128, CH], BF16), "t_lc": ([128, CH], BF16),
    }
    for name in DEBUG_TAPS:
        shp, dt = tapshapes[name]
        taps[name] = nc.dram_tensor(name, shp, dt, kind="ExternalOutput").ap()

    hc = _host_consts()
    inl = {k: nc.inline_tensor(v, name=k.lower()) for k, v in hc.items()}

    ctx = ExitStack()
    with ctx:
        sb = lambda name, shape, dt: ctx.enter_context(nc.sbuf_tensor(name, shape, dt))
        sch = Sched(nc, ctx)
        block = ctx.enter_context(nc.Block())

        B = [ctx.enter_context(nc.psum_tensor(f"bank{i}", [128, 512], F32))
             for i in range(8)]
        bank_w = [None] * 8
        bank_r = [[] for _ in range(8)]

        def bank_write(i, tok):
            bank_w[i] = tok
            bank_r[i] = []

        def bw_deps(i):
            return list(bank_r[i]) + ([bank_w[i]] if bank_w[i] else [])

        def bank_read(i, tok):
            bank_r[i].append(tok)

        def tap(name, src_ap, deps):
            if name in taps:
                return [sch.dma("s", "dtap", deps, lambda e, a=src_ap, d=taps[name]:
                                e.dma_start(out=d, in_=a))]
            return []

        IDENT = sb("c_ident", [128, 128], BF16)
        IDENT16F = sb("c_id16f", [16, 16], F32)
        M_SL = sb("c_msl", [128, 128], BF16)
        M_SU = sb("c_msu", [128, 128], BF16)
        M_IU = sb("c_miu", [128, 128], BF16)
        MB_SL = sb("c_mbsl", [128, 128], F32)
        MB_IU = sb("c_mbiu", [128, 128], F32)
        SEL16B = sb("c_sel16", [16, NH * 128], BF16)
        NSEL16F = sb("c_nsel16", [16, NH * 128], F32)
        SEL8B = sb("c_sel8", [8, NEXP * 128], BF16)
        ONES1 = sb("c_ones1", [1, 128], BF16)
        ONESC = sb("c_onesc", [1, 128], BF16)
        ONECOL = sb("c_onecol", [128, 1], BF16)
        CONSTS = sb("c_consts", [128, 4], F32)

        wba_sb = sb("wba_sb", [128, KC, 32], BF16)
        rw_sb = sb("rw_sb", [128, KC, 8], BF16)
        convw_sb = sb("convw_sb", [128, 192], F32)
        dtb_sb = sb("dtb_sb", [16, 1], F32)
        alog_sb = sb("alog_sb", [16, 1], F32)
        negea_sb = sb("negea_sb", [16, 1], F32)
        nw_sb = sb("nw_sb", [128, 1], F32)

        xnF = sb("xnF", [128, KC, T], BF16)
        coreF = sb("coreF", [128, NH, T], BF16)
        S_sb = sb("S_sb", [128, NH * 128], F32)
        halo = sb("halo", [128, 48, 3], BF16)

        B16b = sb("B16b", [16, T], BF16)
        GC16 = sb("GC16", [16, T], F32)
        EGC16b = sb("EGC16b", [16, T], BF16)
        GCtok = sb("GCtok", [128, NCH * 16], F32)
        Btok = sb("Btok", [128, NCH * 16], F32)
        EGCtok = sb("EGCtok", [128, NCH * 16], F32)
        BEGCtok = sb("BEGCtok", [128, NCH * 16], F32)
        CW8 = sb("CW8", [8, T], BF16)

        f0 = sch.fence()
        ld = []
        for name, dst in [("IDENT", IDENT), ("IDENT16F", IDENT16F), ("M_SL", M_SL),
                          ("M_SU", M_SU), ("M_IU", M_IU), ("MB_SL", MB_SL),
                          ("MB_IU", MB_IU), ("ONES1", ONES1),
                          ("ONESC", ONESC), ("ONECOL", ONECOL), ("CONSTS", CONSTS)]:
            src = inl[name].ap()
            ld.append(sch.dma("s", "dc", f0, lambda e, s=src, d=dst: e.dma_start(out=d[:], in_=s)))
        for name, dst, nh in [("SEL16B", SEL16B, NH), ("NSEL16F", NSEL16F, NH),
                              ("SEL8B", SEL8B, NEXP)]:
            src = inl[name].ap().rearrange("h p m -> p h m")
            d3 = dst[:].rearrange("p (h m) -> p h m", h=nh)
            ld.append(sch.dma("s", "dc", f0, lambda e, s=src, d=d3: e.dma_start(out=d, in_=s)))
        ld.append(sch.dma("g", "dcg", f0, lambda e: e.dma_start(
            out=wba_sb[:], in_=wba_d.rearrange("(k p) c -> p k c", p=128))))
        ld.append(sch.dma("g", "dcg", f0, lambda e: e.dma_start(
            out=rw_sb[:], in_=rw_d.rearrange("(k p) c -> p k c", p=128))))
        ld.append(sch.dma("s", "dc", f0, lambda e: e.dma_start(out=convw_sb[:], in_=convw_d)))
        ld.append(sch.dma("s", "dc", f0, lambda e: e.dma_start(out=dtb_sb[:], in_=dtb_d)))
        ld.append(sch.dma("s", "dc", f0, lambda e: e.dma_start(out=alog_sb[:], in_=alog_d)))
        ld.append(sch.dma("s", "dc", f0, lambda e: e.dma_start(out=nw_sb[:], in_=nw_d)))
        t_ea = sch.op("a", ld, lambda e: e.activation(negea_sb[:], alog_sb[:], ACT.Exp))
        t_ea = sch.op("v", [t_ea], lambda e: e.tensor_scalar_mul(negea_sb[:], negea_sb[:], -1.0))
        sch.op("v", f0, lambda e: e.memset(S_sb[:], 0.0))
        sch.op("v", f0, lambda e: e.memset(halo[:], 0.0))

        EPS_LN = CONSTS[:, 0:1]
        EPS_L2 = CONSTS[:, 1:2]
        ONE_C = CONSTS[:, 2:3]

        sch.flush(block)

        for p in range(NPASS):
            t0 = p * T
            with ExitStack() as pass_ctx:
                psb = lambda n, s, d: pass_ctx.enter_context(
                    nc.sbuf_tensor(f"{n}_p{p}", s, d))
                acc = psb("acc", [128, NTC, D], BF16)

                pfence = sch.fence()
                ph1_scope = ExitStack()
                p1sb = lambda n, s, d: ph1_scope.enter_context(
                    nc.sbuf_tensor(f"{n}_p{p}", s, d))
                xln = p1sb("xln0", [128, D], F32)
                xnt = p1sb("xnt", [128, D], BF16)
                stat = p1sb("stat", [128, 8], F32)
                B16 = p1sb("B16", [16, T], F32)
                A16 = p1sb("A16", [16, T], F32)
                G16 = p1sb("G16", [16, T], F32)
                xln_tok = None
                prev_rs = None
                prev_xn = None
                prev_tp = None

                for tc in range(NTC):
                    dep = xln_tok if xln_tok else list(pfence)
                    t_x = sch.dma("s", f"dx{p}", dep, lambda e, tc=tc: e.dma_start(
                        out=xln[:], in_=x_d[t0 + tc * 128: t0 + (tc + 1) * 128, :]))
                    t_mean = sch.op("v", [t_x, prev_rs], lambda e: e.tensor_reduce(
                        stat[:, 0:1], xln[:], AX, OP.add))
                    t_sq = sch.op("a", [t_x, prev_xn], lambda e: e.activation(
                        xnt[:], xln[:], ACT.Square, accum_out=stat[:, 1:2]))
                    t_mean = sch.op("v", [t_mean], lambda e: e.tensor_scalar_mul(
                        stat[:, 0:1], stat[:, 0:1], 1.0 / D))
                    t_msq = sch.op("v", [t_mean], lambda e: e.tensor_mul(
                        stat[:, 2:3], stat[:, 0:1], stat[:, 0:1]))
                    t_var = sch.op("v", [t_msq, t_sq, prev_rs], lambda e: e.scalar_tensor_tensor(
                        stat[:, 3:4], stat[:, 1:2], 1.0 / D, stat[:, 2:3],
                        OP.mult, OP.subtract))
                    t_ln = sch.op("a", [t_var], lambda e: e.activation(
                        stat[:, 4:5], stat[:, 3:4], ACT.Ln, bias=EPS_LN))
                    t_rs = sch.op("a", [t_ln, prev_xn], lambda e: e.activation(
                        stat[:, 4:5], stat[:, 4:5], ACT.Exp, scale=-0.5))
                    t_xn = sch.op("v", [t_rs, t_mean, prev_tp, t_sq], lambda e: e.tensor_scalar(
                        xnt[:], xln[:], stat[:, 0:1], stat[:, 4:5],
                        OP.subtract, OP.mult))
                    xln_tok = [t_xn, t_sq]
                    prev_rs = t_rs
                    prev_xn = t_xn
                    for k in range(KC):
                        bi = k % 2
                        t_mm = sch.op("t", [t_xn] + bw_deps(bi),
                                      lambda e, k=k, bi=bi: e.matmul(
                                          B[bi][:, 0:128], xnt[:, k * 128:(k + 1) * 128],
                                          IDENT[:], start=True, stop=True))
                        bank_write(bi, t_mm)
                        eng = "v" if (k % 4) < 2 else "a"
                        if eng == "v":
                            t_e = sch.op("v", [t_mm], lambda e, k=k, tc=tc, bi=bi:
                                         e.tensor_copy(xnF[:, k, tc * 128:(tc + 1) * 128],
                                                       B[bi][:, 0:128]))
                        else:
                            t_e = sch.op("a", [t_mm], lambda e, k=k, tc=tc, bi=bi:
                                         e.copy(xnF[:, k, tc * 128:(tc + 1) * 128],
                                                B[bi][:, 0:128]))
                        bank_read(bi, t_e)
                        prev_tp = t_mm

                xnF_done = sch.fence()
                tap("t_xnf", xnF[:], xnF_done)

                for n in range(2):
                    for part, dst in ((0, B16), (1, A16)):
                        bi = 2 + part
                        deps = bw_deps(bi) + xnF_done
                        tk = None
                        for k in range(KC):
                            tk = sch.op("t", deps if k == 0 else [],
                                        lambda e, k=k, n=n, part=part, bi=bi: e.matmul(
                                B[bi][0:16, 0:512],
                                wba_sb[:, k, 16 * part:16 * part + 16],
                                xnF[:, k, n * 512:(n + 1) * 512],
                                start=(k == 0), stop=(k == KC - 1)))
                        bank_write(bi, tk)
                        t_e = sch.op("v", [tk], lambda e, n=n, dst=dst, bi=bi:
                                     e.tensor_copy(dst[:, n * 512:(n + 1) * 512],
                                                   B[bi][0:16, 0:512]))
                        bank_read(bi, t_e)

                ba_done = sch.fence()
                t1 = sch.op("a", ba_done, lambda e: e.activation(
                    B16[:], B16[:], ACT.Exp, scale=-1.0))
                t1 = sch.op("v", [t1], lambda e: e.tensor_scalar_add(B16[:], B16[:], 1.0))
                t1 = sch.op("v", [t1], lambda e: e.reciprocal(B16[:], B16[:]))
                tbb = sch.op("v", [t1], lambda e: e.tensor_copy(B16b[:], B16[:]))
                t2 = sch.op("v", ba_done, lambda e: e.tensor_scalar(
                    A16[:], A16[:], dtb_sb[:], 85.0, OP.add, OP.min))
                t2 = sch.op("a", [t2], lambda e: e.activation(G16[:], A16[:], ACT.Exp))
                t2 = sch.op("a", [t2], lambda e: e.activation(
                    G16[:], G16[:], ACT.Ln, bias=ONE_C[0:16, 0:1]))
                t2 = sch.op("v", [t2, t_ea], lambda e: e.tensor_scalar_mul(
                    G16[:], G16[:], negea_sb[:]))
                tgc = t2
                for c in range(NCH):
                    tgc = sch.op("v", [tgc], lambda e, c=c: e.tensor_tensor_scan(
                        GC16[:, c * CH:(c + 1) * CH], G16[:, c * CH:(c + 1) * CH],
                        G16[:, c * CH:(c + 1) * CH], 0.0, OP.add, OP.bypass))
                tegc = sch.op("a", [tgc], lambda e: e.activation(
                    EGC16b[:], GC16[:], ACT.Exp))
                tap("t_gc16", GC16[:], [tgc])
                tap("t_b16", B16[:], [t1])

                tok_toks = [tgc, t1]
                for c in range(NCH):
                    bi = 4 + (c % 2)
                    for src, dst in ((GC16, GCtok), (B16, Btok)):
                        deps = bw_deps(bi) + [tgc, t1]
                        t_mm = sch.op("t", deps, lambda e, c=c, bi=bi, src=src: e.matmul(
                            B[bi][:, 0:16], src[:, c * CH:(c + 1) * CH],
                            IDENT16F[:], start=True, stop=True))
                        bank_write(bi, t_mm)
                        t_e = sch.op("v", [t_mm], lambda e, c=c, bi=bi, dst=dst:
                                     e.tensor_copy(dst[:, c * 16:(c + 1) * 16],
                                                   B[bi][:, 0:16]))
                        bank_read(bi, t_e)
                        tok_toks.append(t_e)
                t_eg = sch.op("a", tok_toks, lambda e: e.activation(
                    EGCtok[:], GCtok[:], ACT.Exp))
                t_bg = sch.op("v", [t_eg], lambda e: e.tensor_mul(
                    BEGCtok[:], Btok[:], EGCtok[:]))
                scal_done = sch.fence()
                ph1_scope.close()

                ph2_scope = ExitStack()
                rsc = ph2_scope.enter_context(nc.sbuf_tensor(f"rsc_p{p}", [128, 8], F32))
                rtmp = ph2_scope.enter_context(nc.sbuf_tensor(f"rtmp_p{p}", [128, 16], F32))
                cwb = ph2_scope.enter_context(nc.sbuf_tensor(f"cwb_p{p}", [128, 8], BF16))
                prev_cw = None
                for tc in range(NTC):
                    deps = bw_deps(6) + xnF_done
                    tk = None
                    for k in range(KC):
                        tk = sch.op("t", deps if k == 0 else [], lambda e, k=k, tc=tc:
                                    e.matmul(B[6][:, 0:8],
                                             xnF[:, k, tc * 128:(tc + 1) * 128],
                                             rw_sb[:, k, :], start=(k == 0),
                                             stop=(k == KC - 1)))
                    bank_write(6, tk)
                    t_e = sch.op("v", [tk, prev_cw] + scal_done, lambda e: e.tensor_copy(
                        rsc[:], B[6][:, 0:8]))
                    bank_read(6, t_e)
                    t_m = sch.op("v", [t_e], lambda e: e.tensor_reduce(
                        rtmp[:, 0:1], rsc[:], AX, OP.max, negate=True))
                    t_x2 = sch.op("a", [t_m], lambda e: e.activation(
                        rsc[:], rsc[:], ACT.Exp, bias=rtmp[:, 0:1]))
                    t_sm = sch.op("v", [t_x2], lambda e: e.tensor_reduce(
                        rtmp[:, 1:2], rsc[:], AX, OP.add))
                    t_r = sch.op("v", [t_sm], lambda e: e.reciprocal(
                        rtmp[:, 2:3], rtmp[:, 1:2]))
                    t_p = sch.op("v", [t_r], lambda e: e.tensor_scalar_mul(
                        rsc[:], rsc[:], rtmp[:, 2:3]))
                    t_m8 = sch.op("v", [t_p], lambda e: e.max(out=rtmp[:, 8:16], in_=rsc[:]))
                    t_dn = sch.op("v", [t_m8], lambda e: e.tensor_add(
                        rtmp[:, 3:4], rtmp[:, 8:9], rtmp[:, 9:10]))
                    t_dn = sch.op("v", [t_dn], lambda e: e.reciprocal(
                        rtmp[:, 4:5], rtmp[:, 3:4]))
                    t_msk = sch.op("v", [t_dn], lambda e: e.tensor_scalar(
                        rtmp[:, 8:16], rsc[:], rtmp[:, 9:10], None, OP.is_ge))
                    t_cw = sch.op("v", [t_msk], lambda e: e.tensor_mul(
                        rsc[:], rsc[:], rtmp[:, 8:16]))
                    t_cw = sch.op("v", [t_cw], lambda e: e.tensor_scalar(
                        cwb[:], rsc[:], rtmp[:, 4:5], None, OP.mult))
                    deps2 = bw_deps(7) + [t_cw]
                    t_tp = sch.op("t", deps2, lambda e: e.matmul(
                        B[7][0:8, 0:128], cwb[:], IDENT[:], start=True, stop=True))
                    bank_write(7, t_tp)
                    t_ec = sch.op("v", [t_tp], lambda e, tc=tc: e.tensor_copy(
                        CW8[:, tc * 128:(tc + 1) * 128], B[7][0:8, 0:128]))
                    bank_read(7, t_ec)
                    prev_cw = t_ec
                comb_done = sch.fence()
                ph2_scope.close()
                tap("t_cw8", CW8[:], comb_done)

                with ExitStack() as head_scope:
                    hsb = lambda n, s, d: head_scope.enter_context(
                        nc.sbuf_tensor(f"{n}_p{p}", s, d))
                    wq_buf = [hsb("wq0", [128, KC, 256], BF16),
                              hsb("wq1", [128, KC, 256], BF16)]
                    wq_tok = [None, None]
                    wq_rd = [None, None]
                    xmix = hsb("xmix", [128, 3, T + 3], BF16)
                    zraw = hsb("zraw", [128, T], BF16)
                    zsil = zraw
                    qF = hsb("qF", [128, T], BF16)
                    kF = hsb("kF", [128, T], BF16)
                    vF = hsb("vF", [128, T], BF16)
                    sqs = hsb("sqs", [128, 512], BF16)
                    rrow = hsb("rrow", [1, 2 * T], BF16)
                    diag = hsb("diag", [128, 8, 128], BF16)
                    Dneg = hsb("Dneg", [128, T], F32)
                    BCb = hsb("BCb", [128, T], BF16)
                    BCe = hsb("BCe", [128, T], BF16)
                    BCq = hsb("BCq", [128, T], BF16)
                    BCk = hsb("BCk", [128, T], BF16)
                    KN = hsb("KN", [128, T], BF16)
                    KB = hsb("KB", [128, T], BF16)
                    QN = hsb("QN", [128, T], BF16)
                    QG = hsb("QG", [128, T], BF16)
                    kntok = hsb("kntok", [128, T], BF16)
                    vtok = hsb("vtok", [128, T], BF16)
                    Dch2 = [hsb("Dch0", [128, CH], F32), hsb("Dch1", [128, CH], F32)]
                    ETch2 = [hsb("ETch0", [128, CH], F32), hsb("ETch1", [128, CH], F32)]
                    EmS2 = [hsb("EmS0", [128, CH], BF16), hsb("EmS1", [128, CH], BF16)]
                    ETmU2 = [hsb("ETmU0", [128, CH], BF16), hsb("ETmU1", [128, CH], BF16)]
                    ETmI2 = [hsb("ETmI0", [128, CH], BF16), hsb("ETmI1", [128, CH], BF16)]
                    Lc2 = [hsb("Lc0", [128, CH], BF16), hsb("Lc1", [128, CH], BF16)]
                    LcT2 = [hsb("LcT0", [128, CH], BF16), hsb("LcT1", [128, CH], BF16)]
                    Lp22 = [hsb("Lp20", [128, 2 * CH], BF16), hsb("Lp21", [128, 2 * CH], BF16)]
                    attnT2 = [hsb("attnT0", [128, CH], BF16), hsb("attnT1", [128, CH], BF16)]
                    Xsb2 = [hsb("Xsb0", [128, 256], BF16), hsb("Xsb1", [128, 256], BF16)]
                    WFn2 = [hsb("WFn0", [128, CH], BF16), hsb("WFn1", [128, CH], BF16)]
                    vnew2 = [hsb("vnew0", [128, 128], BF16), hsb("vnew1", [128, 128], BF16)]
                    Sb16 = hsb("Sb16", [128, 128], BF16)
                    kdt2 = [hsb("kdt0", [128, 128], BF16), hsb("kdt1", [128, 128], BF16)]
                    colscr = hsb("colscr", [128, 8], F32)
                    rmsjunk = hsb("rmsjunk", [128, 128], BF16)
                    coretok = hsb("coretok", [128, NCH, 128], BF16)
                    rmsc = hsb("rmsc", [128, NCH], F32)

                    def prefetch_wq(m, deps):
                        slot = m % 2
                        src = wqkvz_d[:, 256 * m:256 * (m + 1)].rearrange(
                            "(k p) c -> p k c", p=128)
                        extra = [wq_rd[slot]] if wq_rd[slot] else []
                        wq_tok[slot] = sch.dma("g", f"dwq{p}s{slot}", deps + extra,
                                               lambda e, s=src, d=wq_buf[slot]:
                                               e.dma_start(out=d[:], in_=s))

                    prefetch_wq(0, comb_done)
                    prefetch_wq(1, comb_done)

                    from collections import deque

                    head_state = {"prev_conv": [], "prev_ho": None,
                                  "prev_gate": [], "sb_chain": None}

                    def make_prepA(h):
                        stp = {"ev": {}, "convt": {}, "conv_mms": []}
                        th = []
                        zth = []
                        gate_x = list(head_state["prev_conv"]) + \
                            ([head_state["prev_ho"]] if head_state["prev_ho"] else [])

                        def t_hi_f(stp=stp, h=h, gate_x=gate_x):
                            stp["hi"] = sch.op("v", gate_x, lambda e: e.tensor_copy(
                                xmix[:, :, 0:3], halo[:, 3 * h:3 * h + 3, :]))
                        th.append(t_hi_f)

                        for ob in range(3):
                            m = 2 * h + (ob // 2)
                            slot = m % 2
                            obh = ob % 2
                            for n in range(2):
                                bi = (ob * 2 + n) % 2
                                for kq in range(4):
                                    def mmf(ob=ob, n=n, bi=bi, kq=kq, slot=slot,
                                            obh=obh, stp=stp, gate_x=gate_x):
                                        wq = wq_buf[slot]
                                        deps = (bw_deps(bi) + [wq_tok[slot]]
                                                ) if kq == 0 else []
                                        tk = None
                                        for k in range(4 * kq, 4 * kq + 4):
                                            tk = sch.op("t", deps if k == 4 * kq == 0 or
                                                        (kq == 0 and k == 0) else [],
                                                        lambda e, k=k, obh=obh, n=n,
                                                        bi=bi, wq=wq: e.matmul(
                                                B[bi][:, 0:512],
                                                wq[:, k, obh * 128:(obh + 1) * 128],
                                                xnF[:, k, n * 512:(n + 1) * 512],
                                                start=(k == 0), stop=(k == KC - 1),
                                                skip_group_check=True))
                                        if kq == 3:
                                            bank_write(bi, tk)
                                            wq_rd[slot] = tk
                                            t_e = sch.op("a", [tk] + gate_x,
                                                         lambda e, ob=ob, n=n, bi=bi:
                                                         e.copy(
                                                xmix[:, ob, 3 + n * 512:
                                                     3 + (n + 1) * 512],
                                                B[bi][:, 0:512]))
                                            bank_read(bi, t_e)
                                            stp["ev"][(ob, n)] = t_e
                                    th.append(mmf)

                        def t_ho_f(stp=stp, h=h):
                            stp["ho"] = sch.op("v", [stp["ev"][(0, 1)], stp["ev"][(1, 1)],
                                                     stp["ev"][(2, 1)], stp["hi"]],
                                               lambda e: e.tensor_copy(
                                halo[:, 3 * h:3 * h + 3, :], xmix[:, :, T:T + 3]))
                        th.append(t_ho_f)

                        prev_cmm = {}
                        for part, dstn in ((0, "qF"), (1, "kF"), (2, "vF")):
                            dst = {"qF": qF, "kF": kF, "vF": vF}[dstn]
                            ps = part % 2

                            def diagf(part=part, ps=ps, h=h, stp=stp):
                                gate_d = list(head_state["prev_conv"]) + \
                                    ([prev_cmm[ps]] if ps in prev_cmm else [])
                                stp[f"dg{part}"] = []
                                for j in range(4):
                                    col = (part * 16 + h) * 4 + j
                                    stp[f"dg{part}"].append(sch.op(
                                        "v", gate_d, lambda e, ps=ps, j=j, col=col:
                                        e.tensor_scalar_mul(diag[:, ps * 4 + j, :],
                                                            IDENT[:],
                                                            convw_sb[:, col:col + 1])))
                            th.append(diagf)
                            for n in range(2):
                                bi = 2 + (part * 2 + n) % 2
                                bi = (part * 2 + n) % 2

                                def convf(part=part, n=n, bi=bi, ps=ps, dst=dst,
                                          dstn=dstn, stp=stp):
                                    deps = (bw_deps(bi) + [stp["hi"],
                                            stp["ev"][(part, 0)], stp["ev"][(part, 1)]]
                                            + stp[f"dg{part}"])
                                    tk = None
                                    for j in range(4):
                                        tk = sch.op("t", deps if j == 0 else [],
                                                    lambda e, j=j, part=part, n=n,
                                                    bi=bi, ps=ps: e.matmul(
                                            B[bi][:, 0:512], diag[:, ps * 4 + j, :],
                                            xmix[:, part, j + n * 512:
                                                 j + n * 512 + 512],
                                            start=(j == 0), stop=(j == 3),
                                            skip_group_check=True))
                                    bank_write(bi, tk)
                                    prev_cmm[ps] = tk
                                    stp["conv_mms"].append(tk)
                                    t_e = sch.op("a", [tk], lambda e, dst=dst, n=n,
                                                 bi=bi: e.activation(
                                        dst[:, n * 512:(n + 1) * 512],
                                        B[bi][:, 0:512], ACT.Silu))
                                    bank_read(bi, t_e)
                                    stp["convt"][(part, n)] = t_e
                                th.append(convf)

                        def l2f(stp=stp):
                            ss_toks = []
                            t_sq2 = None
                            for idx, srcn in enumerate(("qF", "kF")):
                                srct = {"qF": qF, "kF": kF}[srcn]
                                cv = [stp["convt"][(idx, 0)], stp["convt"][(idx, 1)]]
                                for n in range(2):
                                    dep_s = cv + ([t_sq2] if t_sq2 else [])
                                    t_sq2 = sch.op("v", dep_s, lambda e, srct=srct,
                                                   n=n: e.tensor_mul(
                                        sqs[:], srct[:, n * 512:(n + 1) * 512],
                                        srct[:, n * 512:(n + 1) * 512]))
                                    bi = n % 2
                                    deps = bw_deps(bi) + [t_sq2]
                                    t_mm = sch.op("t", deps, lambda e, bi=bi: e.matmul(
                                        B[bi][0:1, 0:512], ONECOL[:], sqs[:],
                                        start=True, stop=True))
                                    bank_write(bi, t_mm)
                                    t_sq2 = t_mm
                                    t_e = sch.op("v", [t_mm], lambda e, idx=idx, n=n,
                                                 bi=bi: e.tensor_copy(
                                        rrow[:, idx * T + n * 512:
                                             idx * T + (n + 1) * 512],
                                        B[bi][0:1, 0:512]))
                                    bank_read(bi, t_e)
                                    ss_toks.append(t_e)
                            t_ln2 = sch.op("a", ss_toks, lambda e: e.activation(
                                rrow[:], rrow[:], ACT.Ln, bias=EPS_L2[0:1, 0:1]))
                            stp["rr"] = sch.op("a", [t_ln2], lambda e: e.activation(
                                rrow[:], rrow[:], ACT.Exp, scale=-0.5))
                        th.append(l2f)

                        if h + 1 < NH:
                            def pf(h=h):
                                prefetch_wq(2 * h + 2, [])
                            th.append(pf)

                        m = 2 * h + 1
                        slot = m % 2
                        for n in range(2):
                            bi = 6 + n
                            bi = n % 2
                            for kq in range(4):
                                def zmf(n=n, bi=bi, kq=kq, slot=slot, stp=stp):
                                    wq = wq_buf[slot]
                                    gate_z = list(head_state["prev_gate"])
                                    deps = (bw_deps(bi) + [wq_tok[slot]]
                                            ) if kq == 0 else []
                                    tk = None
                                    for k in range(4 * kq, 4 * kq + 4):
                                        tk = sch.op("t", deps if (kq == 0 and k == 0)
                                                    else [],
                                                    lambda e, k=k, n=n, bi=bi, wq=wq:
                                                    e.matmul(
                                            B[bi][:, 0:512], wq[:, k, 128:256],
                                            xnF[:, k, n * 512:(n + 1) * 512],
                                            start=(k == 0), stop=(k == KC - 1),
                                            skip_group_check=True))
                                    if kq == 3:
                                        bank_write(bi, tk)
                                        wq_rd[slot] = tk
                                        t_e = sch.op("a", [tk] + gate_z,
                                                     lambda e, n=n, bi=bi: e.copy(
                                            zraw[:, n * 512:(n + 1) * 512],
                                            B[bi][:, 0:512]))
                                        bank_read(bi, t_e)
                                        stp["ev"][(3, n)] = t_e
                                zth.append(zmf)

                        def zsf(stp=stp, h=h):
                            stp["zs"] = sch.op("a", [stp["ev"][(3, 0)],
                                                     stp["ev"][(3, 1)]],
                                               lambda e: e.activation(
                                zsil[:], zraw[:], ACT.Silu))
                            if h + 1 < NH:
                                prefetch_wq(2 * h + 3, [])
                        zth.append(zsf)
                        return th, zth, stp

                    def emit_prepB(h, stp):
                        out = {}
                        conv_q = [stp["convt"][(0, 0)], stp["convt"][(0, 1)]]
                        conv_k = [stp["convt"][(1, 0)], stp["convt"][(1, 1)]]
                        conv_v = [stp["convt"][(2, 0)], stp["convt"][(2, 1)]]
                        gate_b = list(head_state["prev_gate"])
                        t_kn = t_kb = t_qn = t_qg = None
                        for i, (kind, srcf, dstf) in enumerate((
                                ("k", kF, KN), ("b", None, KB),
                                ("q", qF, QN), ("e", None, QG))):
                            for n in range(2):
                                bi = 4 + (n % 2)
                                deps = bw_deps(bi) + [stp["rr"], tbb, tegc] + gate_b
                                if kind == "b":
                                    t_mm = sch.op("t", deps, lambda e, h=h, n=n, bi=bi:
                                                  e.matmul(
                                        B[bi][:, 0:512],
                                        SEL16B[:, h * 128:(h + 1) * 128],
                                        B16b[:, n * 512:(n + 1) * 512],
                                        start=True, stop=True))
                                elif kind == "e":
                                    t_mm = sch.op("t", deps, lambda e, h=h, n=n, bi=bi:
                                                  e.matmul(
                                        B[bi][:, 0:512],
                                        SEL16B[:, h * 128:(h + 1) * 128],
                                        EGC16b[:, n * 512:(n + 1) * 512],
                                        start=True, stop=True))
                                elif kind == "q":
                                    t_mm = sch.op("t", deps, lambda e, n=n, bi=bi:
                                                  e.matmul(
                                        B[bi][:, 0:512], ONESC[:],
                                        rrow[:, n * 512:(n + 1) * 512],
                                        start=True, stop=True))
                                else:
                                    t_mm = sch.op("t", deps, lambda e, n=n, bi=bi:
                                                  e.matmul(
                                        B[bi][:, 0:512], ONES1[:],
                                        rrow[:, T + n * 512: T + (n + 1) * 512],
                                        start=True, stop=True))
                                bank_write(bi, t_mm)
                                if kind == "k":
                                    dep2 = conv_k + [t_mm] + gate_b
                                    t = sch.op("v", dep2, lambda e, n=n, bi=bi:
                                               e.tensor_mul(
                                        KN[:, n * 512:(n + 1) * 512],
                                        kF[:, n * 512:(n + 1) * 512],
                                        B[bi][:, 0:512]))
                                    t_kn = t if n == 1 else t_kn or t
                                    out.setdefault("kn", []).append(t)
                                elif kind == "b":
                                    dep2 = out["kn"] + [t_mm]
                                    t = sch.op("v", dep2, lambda e, n=n, bi=bi:
                                               e.tensor_mul(
                                        KB[:, n * 512:(n + 1) * 512],
                                        KN[:, n * 512:(n + 1) * 512],
                                        B[bi][:, 0:512]))
                                    out.setdefault("kb", []).append(t)
                                elif kind == "q":
                                    dep2 = conv_q + [t_mm] + gate_b
                                    t = sch.op("v", dep2, lambda e, n=n, bi=bi:
                                               e.tensor_mul(
                                        QN[:, n * 512:(n + 1) * 512],
                                        qF[:, n * 512:(n + 1) * 512],
                                        B[bi][:, 0:512]))
                                    out.setdefault("qn", []).append(t)
                                else:
                                    dep2 = out["qn"] + [t_mm]
                                    t = sch.op("v", dep2, lambda e, n=n, bi=bi:
                                               e.tensor_mul(
                                        QG[:, n * 512:(n + 1) * 512],
                                        QN[:, n * 512:(n + 1) * 512],
                                        B[bi][:, 0:512]))
                                    out.setdefault("qg", []).append(t)
                                bank_read(bi, t)
                        out["t_kb"] = out["kb"][-1]
                        out["t_qg"] = out["qg"][-1]

                        t_knt, t_vt = [], []
                        for c in range(NCH):
                            bi = 6 + (c % 2)
                            for srcd, dst, tok_list, dd in (
                                    (KN, kntok, t_knt, out["kn"]),
                                    (vF, vtok, t_vt, conv_v + gate_b)):
                                deps = bw_deps(bi) + dd
                                t_mm = sch.op("t", deps, lambda e, c=c, bi=bi,
                                              srcd=srcd: e.matmul(
                                    B[bi][:, 0:128], srcd[:, c * 128:(c + 1) * 128],
                                    IDENT[:], start=True, stop=True))
                                bank_write(bi, t_mm)
                                if c % 2 == 0:
                                    t_e = sch.op("v", [t_mm], lambda e, c=c, bi=bi,
                                                 dst=dst: e.tensor_copy(
                                        dst[:, c * 128:(c + 1) * 128], B[bi][:, 0:128]))
                                else:
                                    t_e = sch.op("a", [t_mm], lambda e, c=c, bi=bi,
                                                 dst=dst: e.copy(
                                        dst[:, c * 128:(c + 1) * 128], B[bi][:, 0:128]))
                                bank_read(bi, t_e)
                                tok_list.append(t_e)
                        out["t_knt"] = t_knt
                        out["t_vt"] = t_vt

                        t_nd = []
                        for n in range(2):
                            bi = 2 + n
                            deps = bw_deps(bi) + [tgc] + gate_b
                            t_mm = sch.op("t", deps, lambda e, h=h, n=n, bi=bi:
                                          e.matmul(
                                B[bi][:, 0:512], NSEL16F[:, h * 128:(h + 1) * 128],
                                GC16[:, n * 512:(n + 1) * 512],
                                start=True, stop=True))
                            bank_write(bi, t_mm)
                            t_e = sch.op("v", [t_mm] + gate_b, lambda e, n=n, bi=bi:
                                         e.tensor_copy(
                                Dneg[:, n * 512:(n + 1) * 512], B[bi][:, 0:512]))
                            bank_read(bi, t_e)
                            t_nd.append(t_e)
                        out["t_nd"] = t_nd
                        out["t_sb"] = sch.op("v", gate_b + t_nd, lambda e, h=h:
                                             e.tensor_copy(
                            Sb16[:], S_sb[:, h * 128:(h + 1) * 128]))
                        return out

                    thA, zthA, stpA = make_prepA(0)
                    for f in thA:
                        f()
                    for f in zthA:
                        f()

                    fillq = deque()

                    def drain(n=1, reserve=12):
                        for _ in range(n):
                            if len(fillq) > reserve:
                                fillq.popleft()()

                    for h in range(LIM_HEADS):
                        drain(6, 0)
                        while "rr" not in stpA and fillq:
                            fillq.popleft()()
                        pb_out = emit_prepB(h, stpA)
                        t_kb = pb_out["t_kb"]
                        t_qg = pb_out["t_qg"]
                        t_knt = pb_out["t_knt"]
                        t_vt = pb_out["t_vt"]
                        t_nd = pb_out["t_nd"]
                        t_sb = pb_out["t_sb"]
                        head_state["prev_conv"] = list(stpA["conv_mms"])
                        head_state["prev_ho"] = stpA["ho"]
                        cur_stp = stpA
                        if h + 1 < LIM_HEADS:
                            thN, zthN, stpN = make_prepA(h + 1)
                            fillq.extend(thN)
                        else:
                            thN, zthN, stpN = [], [], None
                        prev_end = [[], []]

                        def emit_chunk(c):
                            pb = c % 2
                            XB = 3 if pb == 0 else 6
                            QB = 4 if pb == 0 else 7
                            pe = prev_end[pb]
                            Dch, ETch = Dch2[pb], ETch2[pb]
                            EmS, ETmU, ETmI = EmS2[pb], ETmU2[pb], ETmI2[pb]
                            Lc, LcT, Lp2 = Lc2[pb], LcT2[pb], Lp22[pb]
                            attnT, Xsb, WFn = attnT2[pb], Xsb2[pb], WFn2[pb]
                            vnew, kdt = vnew2[pb], kdt2[pb]
                            cs0 = 4 * pb
                            gc_col = GCtok[:, c * 16 + h: c * 16 + h + 1]
                            b_col = Btok[:, c * 16 + h: c * 16 + h + 1]
                            be_col = BEGCtok[:, c * 16 + h: c * 16 + h + 1]
                            csl = slice(c * 128, (c + 1) * 128)
                            st = {"toks": []}

                            def tk(t):
                                st["toks"].append(t)
                                return t

                            def s_prep():
                                st["da"] = tk(sch.op("v", t_nd + pe, lambda e:
                                                     e.tensor_scalar(
                                    Dch[:], Dneg[:, csl], gc_col, None, OP.add)))
                                e2p = tk(sch.op("v", [st["da"]] + pe, lambda e:
                                                e.scalar_tensor_tensor(
                                    ETch[:], Dch[:], -1.0, MB_IU[:],
                                    OP.mult, OP.add)))
                                e1p = tk(sch.op("v", [e2p], lambda e:
                                                e.tensor_add(Dch[:], Dch[:], MB_SL[:])))
                                st["m1"] = tk(sch.op("a", [e1p] + pe, lambda e:
                                                     e.activation(EmS[:], Dch[:], ACT.Exp)))
                                st["m3"] = tk(sch.op("a", [e2p] + pe, lambda e:
                                                     e.activation(ETmI[:], ETch[:], ACT.Exp)))
                                st["m2"] = tk(sch.op("v", [st["m3"]] + pe, lambda e:
                                                     e.tensor_mul(ETmU[:], ETmI[:], M_SU[:])))

                            def s_grams():
                                depsg = bw_deps(2) + [t_kb, t_qg]
                                g2 = tk(sch.op("t", depsg, lambda e: e.matmul(
                                    B[2][:, 0:128], KB[:, csl], KN[:, csl],
                                    start=True, stop=True)))
                                g2t = tk(sch.op("t", [], lambda e: e.matmul(
                                    B[2][:, 128:256], KN[:, csl], KB[:, csl],
                                    start=True, stop=True)))
                                aqk = tk(sch.op("t", [], lambda e: e.matmul(
                                    B[2][:, 256:384], KN[:, csl], QN[:, csl],
                                    start=True, stop=True)))
                                bank_write(2, aqk)
                                st["L"] = tk(sch.op("v", [g2, st["m1"]] + pe, lambda e:
                                                    e.scalar_tensor_tensor(
                                    Lc[:], B[2][:, 0:128], -1.0, EmS[:],
                                    OP.mult, OP.mult)))
                                st["LT"] = tk(sch.op("v", [g2t, st["m2"]] + pe, lambda e:
                                                     e.scalar_tensor_tensor(
                                    LcT[:], B[2][:, 128:256], -1.0, ETmU[:],
                                    OP.mult, OP.mult)))
                                st["at"] = tk(sch.op("v", [aqk, st["m3"]] + pe, lambda e:
                                                     e.tensor_mul(
                                    attnT[:], B[2][:, 256:384], ETmI[:])))
                                for tt in (st["L"], st["LT"], st["at"]):
                                    bank_read(2, tt)

                            def s_x0():
                                st["x0a"] = tk(sch.op("v", t_vt + pe, lambda e:
                                                      e.tensor_scalar(
                                    Xsb[:, 0:128], vtok[:, csl], b_col, None, OP.mult)))
                                st["x0b"] = tk(sch.op("v", t_knt + [t_bg] + pe, lambda e:
                                                      e.tensor_scalar(
                                    Xsb[:, 128:256], kntok[:, csl], be_col,
                                    None, OP.mult)))
                                st["lp"] = (Lc, st["L"])
                                st["lpt"] = (LcT, st["LT"])
                                st["xev"] = None

                            def s_lev(lev):
                                lp_cur, t_lp = st["lp"]
                                lpt_cur, t_lpt = st["lpt"]
                                if lev == 0:
                                    depsx = bw_deps(XB) + [st["x0a"], st["x0b"]]
                                    tk(sch.op("t", depsx, lambda e: e.matmul(
                                        B[XB][:, 0:256], IDENT[:], Xsb[:],
                                        start=True, stop=False, skip_group_check=True)))
                                    t_ap = tk(sch.op("t", [t_lpt], lambda e, lpt=lpt_cur:
                                                     e.matmul(
                                        B[XB][:, 0:256], lpt[:], Xsb[:],
                                        start=False, stop=False, skip_group_check=True)))
                                else:
                                    t_ap = tk(sch.op("t", [t_lpt, st["xev"]],
                                                     lambda e, lpt=lpt_cur, lev=lev:
                                                     e.matmul(
                                        B[XB][:, 0:256], lpt[:], Xsb[:],
                                        start=False, stop=(lev == 6),
                                        skip_group_check=True)))
                                bank_write(XB, t_ap)
                                if lev % 2 == 0:
                                    st["xev"] = tk(sch.op("v", [t_ap], lambda e:
                                                          e.tensor_copy(Xsb[:], B[XB][:, 0:256])))
                                else:
                                    st["xev"] = tk(sch.op("a", [t_ap], lambda e:
                                                          e.copy(Xsb[:], B[XB][:, 0:256])))
                                bank_read(XB, st["xev"])
                                if lev < 6:
                                    depsq = bw_deps(QB) + [t_lp, t_lpt]
                                    tk(sch.op("t", depsq, lambda e, lp=lp_cur,
                                              lpt=lpt_cur: e.matmul(
                                        B[QB][:, 0:128], lpt[:], lp[:],
                                        start=True, stop=True, skip_group_check=True)))
                                    t_s2 = tk(sch.op("t", [], lambda e, lp=lp_cur,
                                                     lpt=lpt_cur: e.matmul(
                                        B[QB][:, 128:256], lp[:], lpt[:],
                                        start=True, stop=True, skip_group_check=True)))
                                    bank_write(QB, t_s2)
                                    if lev % 2 == 0:
                                        t_qe = tk(sch.op("v", [t_s2] + pe, lambda e:
                                                         e.tensor_copy(Lp2[:], B[QB][:, 0:256])))
                                    else:
                                        t_qe = tk(sch.op("a", [t_s2] + pe, lambda e:
                                                         e.copy(Lp2[:], B[QB][:, 0:256])))
                                    bank_read(QB, t_qe)
                                    st["lp"] = (Lp2[:, 0:128], t_qe)
                                    st["lpt"] = (Lp2[:, 128:256], t_qe)

                            def s_wf():
                                depsw = bw_deps(XB) + [st["xev"]]
                                t_wf = tk(sch.op("t", depsw, lambda e: e.matmul(
                                    B[XB][:, 256:384], Xsb[:, 128:256], IDENT[:],
                                    start=True, stop=True, skip_group_check=True)))
                                bank_write(XB, t_wf)
                                st["wfe"] = tk(sch.op("v", [t_wf] + pe, lambda e:
                                                      e.tensor_scalar_mul(
                                    WFn[:], B[XB][:, 256:384], -1.0)))
                                bank_read(XB, st["wfe"])

                            def s_scan():
                                t_sb_in = st["sb_in"]
                                depsv = bw_deps(5) + [st["xev"], t_sb_in]
                                t_v1 = tk(sch.op("t", depsv, lambda e: e.matmul(
                                    B[5][:, 0:128], IDENT[:], Xsb[:, 0:128],
                                    start=True, stop=False, skip_group_check=True)))
                                t_v2 = tk(sch.op("t", [st["wfe"]], lambda e: e.matmul(
                                    B[5][:, 0:128], WFn[:], Sb16[:],
                                    start=False, stop=True, skip_group_check=True)))
                                bank_write(5, t_v2)
                                t_vn = tk(sch.op("a", [t_v2] + pe, lambda e:
                                                 e.copy(vnew[:], B[5][:, 0:128])))
                                bank_read(5, t_vn)
                                depso = bw_deps(XB) + [t_qg, t_sb_in, st["wfe"],
                                                       st["xev"]]
                                t_o1 = tk(sch.op("t", depso, lambda e: e.matmul(
                                    B[XB][:, 384:512], QG[:, csl], Sb16[:],
                                    start=True, stop=False, skip_group_check=True)))
                                t_o2 = tk(sch.op("t", [t_vn, st["at"]], lambda e:
                                                 e.matmul(
                                    B[XB][:, 384:512], attnT[:], vnew[:],
                                    start=False, stop=True, skip_group_check=True)))
                                bank_write(XB, t_o2)
                                t_ce = tk(sch.op("a", [t_o2], lambda e, c=c:
                                                 e.copy(coretok[:, c, :],
                                                        B[XB][:, 384:512])))
                                bank_read(XB, t_ce)
                                tk(sch.op("a", [t_ce], lambda e, c=c: e.activation(
                                    rmsjunk[:], coretok[:, c, :], ACT.Square,
                                    accum_out=rmsc[:, c:c + 1])))
                                t_kd1 = tk(sch.op("v", t_nd + pe, lambda e, c=c:
                                                  e.tensor_scalar(
                                    colscr[:, cs0:cs0 + 1], gc_col,
                                    Dneg[:, c * 128 + 127:c * 128 + 128],
                                    None, OP.add)))
                                t_kd2 = tk(sch.op("a", [t_kd1] + pe, lambda e:
                                                  e.activation(
                                    colscr[:, cs0 + 1:cs0 + 2], colscr[:, cs0:cs0 + 1],
                                    ACT.Exp, scale=-1.0)))
                                t_egl = tk(sch.op("a", t_nd + pe, lambda e, c=c:
                                                  e.activation(
                                    colscr[:, cs0 + 2:cs0 + 3],
                                    Dneg[:, c * 128 + 127:c * 128 + 128],
                                    ACT.Exp, scale=-1.0)))
                                t_kd3 = tk(sch.op("v", [t_kd2] + t_knt + pe, lambda e:
                                                  e.tensor_scalar(
                                    kdt[:], kntok[:, csl],
                                    colscr[:, cs0 + 1:cs0 + 2], None, OP.mult)))
                                depss = bw_deps(5) + [t_kd3, t_vn]
                                t_su = tk(sch.op("t", depss, lambda e: e.matmul(
                                    B[5][:, 128:256], kdt[:], vnew[:],
                                    start=True, stop=True, skip_group_check=True)))
                                bank_write(5, t_su)
                                t_sn = tk(sch.op("v", [t_su, t_egl], lambda e, h=h:
                                                 e.scalar_tensor_tensor(
                                    S_sb[:, h * 128:(h + 1) * 128],
                                    S_sb[:, h * 128:(h + 1) * 128],
                                    colscr[:, cs0 + 2:cs0 + 3], B[5][:, 128:256],
                                    OP.mult, OP.add)))
                                bank_read(5, t_sn)
                                st["sb_out"] = tk(sch.op("a", [t_sn], lambda e, h=h:
                                                         e.copy(
                                    Sb16[:], S_sb[:, h * 128:(h + 1) * 128])))

                            return st, s_prep, s_grams, s_x0, s_lev, s_wf, s_scan

                        t_sb_chain = t_sb
                        for c0 in range(0, LIM_CHUNKS, 2):
                            cA, cB = c0, c0 + 1
                            has_b = cB < LIM_CHUNKS
                            stA, prepA, gramsA, x0A, levA, wfA, scanA = emit_chunk(cA)
                            if has_b:
                                stB, prepB, gramsB, x0B, levB, wfB, scanB = emit_chunk(cB)
                            prepA()
                            if has_b:
                                prepB()
                            drain(2)
                            gramsA()
                            if has_b:
                                gramsB()
                            drain(2)
                            x0A()
                            if has_b:
                                x0B()
                            drain(1)
                            for lev in range(7):
                                levA(lev)
                                drain(1)
                                if has_b:
                                    levB(lev)
                                drain(1)
                            wfA()
                            if has_b:
                                wfB()
                            drain(2)
                            stA["sb_in"] = t_sb_chain
                            scanA()
                            drain(2)
                            t_sb_chain = stA["sb_out"]
                            if has_b:
                                stB["sb_in"] = t_sb_chain
                                scanB()
                                drain(2)
                                t_sb_chain = stB["sb_out"]
                            prev_end[cA % 2] = stA["toks"]
                            if has_b:
                                prev_end[cB % 2] = stB["toks"]


                        drain(len(fillq), 0)
                        t_zs = cur_stp["zs"]
                        chunks_done = sch.fence()
                        gate_toks = []
                        if h == NH - 1:
                            tap("t_s", S_sb[:], chunks_done)
                        t_rstd = sch.op("v", chunks_done, lambda e: e.tensor_scalar_mul(
                            rmsc[:], rmsc[:], 1.0 / 128.0))
                        t_rstd = sch.op("a", [t_rstd], lambda e: e.activation(
                            rmsc[:], rmsc[:], ACT.Ln, bias=EPS_L2[:, 0:1]))
                        t_rstd = sch.op("a", [t_rstd], lambda e: e.activation(
                            rmsc[:], rmsc[:], ACT.Exp, scale=-0.5))
                        for c in range(LIM_CHUNKS):
                            t_gt = sch.op("v", [t_rstd], lambda e, c=c: e.tensor_scalar(
                                coretok[:, c, :], coretok[:, c, :], rmsc[:, c:c + 1],
                                None, OP.mult))
                            bi = 2 + (c % 2)
                            depsc = bw_deps(bi) + [t_gt]
                            t_tp = sch.op("t", depsc, lambda e, c=c, bi=bi: e.matmul(
                                B[bi][:, 0:128], coretok[:, c, :], IDENT[:],
                                start=True, stop=True))
                            bank_write(bi, t_tp)
                            t_cf = sch.op("v", [t_tp, t_zs], lambda e, c=c, h=h, bi=bi:
                                          e.tensor_mul(coreF[:, h, c * 128:(c + 1) * 128],
                                                       B[bi][:, 0:128],
                                                       zsil[:, c * 128:(c + 1) * 128]))
                            bank_read(bi, t_cf)
                            gate_toks.append(t_cf)

                        head_state["prev_gate"] = gate_toks
                        fillq.extend(zthN)
                        stpA = stpN

                heads_done = sch.fence()
                tap("t_coref", coreF[:], heads_done)

                with ExitStack() as wo_scope:
                    wo_sb = wo_scope.enter_context(
                        nc.sbuf_tensor(f"wo_p{p}", [128, 2, KC, 256], BF16))
                    xog = wo_scope.enter_context(
                        nc.sbuf_tensor(f"xog_p{p}", [128, 2, 256], F32))
                    t_wo = [None, None]
                    t_wo_rd = [None, None]
                    t_xog = [None, None]
                    for og in range(2 * NOG if not SKIP_WOUT else 0):
                        ws = og % 2
                        dep_wo = ([t_wo[ws]] if t_wo[ws] else heads_done) + \
                            ([t_wo_rd[ws]] if t_wo_rd[ws] else [])
                        t_l = sch.dma("g", f"dwo{p}s{ws}", dep_wo, lambda e, og=og, ws=ws:
                                      e.dma_start(
                            out=wo_sb[:, ws],
                            in_=wout_d[:, og * 256:(og + 1) * 256]
                            .rearrange("(k p) c -> p k c", p=128)))
                        t_wos = sch.op("v", [t_l], lambda e, ws=ws: e.tensor_scalar_mul(
                            wo_sb[:, ws].rearrange("p k c -> p (k c)"),
                            wo_sb[:, ws].rearrange("p k c -> p (k c)"), nw_sb[:]))
                        t_wo[ws] = t_wos
                        for tc in range(NTC):
                            oslot = tc % 2
                            dep_x = [t_xog[oslot]] if t_xog[oslot] else list(heads_done)
                            t_x = sch.dma("s", f"dxo{p}s{oslot}", dep_x,
                                          lambda e, tc=tc, og=og, oslot=oslot:
                                          e.dma_start(
                                out=xog[:, oslot, :],
                                in_=x_d[t0 + tc * 128: t0 + (tc + 1) * 128,
                                        og * 256:(og + 1) * 256]))
                            bslot = 4 + (tc % 2)
                            deps0 = bw_deps(bslot) + [t_wos] + heads_done
                            tk = None
                            for k in range(KC):
                                tk = sch.op("t", deps0 if k == 0 else [],
                                            lambda e, k=k, tc=tc, bslot=bslot, ws=ws:
                                            e.matmul(
                                    B[bslot][:, 0:256],
                                    coreF[:, k, tc * 128:(tc + 1) * 128],
                                    wo_sb[:, ws, k, :], start=(k == 0),
                                    stop=(k == KC - 1)))
                            bank_write(bslot, tk)
                            t_wo_rd[ws] = tk
                            t_e = sch.op("v", [tk, t_x], lambda e, tc=tc, og=og,
                                         oslot=oslot, bslot=bslot: e.tensor_add(
                                acc[:, tc, og * 256:(og + 1) * 256],
                                B[bslot][:, 0:256], xog[:, oslot, :]))
                            bank_read(bslot, t_e)
                            t_xog[oslot] = t_e
                wo_done = sch.fence()

                with ExitStack() as ffn_scope:
                    fsb = lambda n, s, d: ffn_scope.enter_context(
                        nc.sbuf_tensor(f"{n}_p{p}", s, d))
                    acts = fsb("acts", [128, 2, 8, T], BF16)
                    wgu_sb = fsb("wgu", [128, 2, 2, KC, 128], BF16)
                    wd_sb = fsb("wd", [128, 2, 2, 8, 256], BF16)
                    obuf = fsb("obuf", [128, 2, 512], F32)
                    gsil = fsb("gsil", [128, 512], BF16)
                    upsc = fsb("upsc", [128, 512], BF16)
                    cwbt = fsb("cwbt", [128, T], BF16)

                    t_wgu = None
                    wgu_tok = [None, None]
                    wgu_rd = [None, None]
                    t_ob = [None, None]
                    t_ac_last = None
                    wd_toks = {}

                    for gi, grp in enumerate(EGRPS if not SKIP_FFN else []):
                        last_grp = gi == len(EGRPS) - 1
                        gfence = sch.fence()
                        act_toks = []
                        for ei, ex in enumerate(grp):
                            t_cb = []
                            for n in range(2):
                                deps = bw_deps(0) + comb_done + gfence
                                t_mm = sch.op("t", deps, lambda e, ex=ex, n=n: e.matmul(
                                    B[0][:, 0:512], SEL8B[:, ex * 128:(ex + 1) * 128],
                                    CW8[:, n * 512:(n + 1) * 512],
                                    start=True, stop=True))
                                bank_write(0, t_mm)
                                t_e = sch.op("v", [t_mm], lambda e, n=n: e.tensor_copy(
                                    cwbt[:, n * 512:(n + 1) * 512], B[0][:, 0:512]))
                                bank_read(0, t_e)
                                t_cb.append(t_e)
                            for fc in range(8):
                                fslot = fc % 2
                                dep_w = ([wgu_tok[fslot]] if wgu_tok[fslot] else gfence) + \
                                    ([wgu_rd[fslot]] if wgu_rd[fslot] else [])
                                t_wg1 = sch.dma("g", f"dwg{p}s{fslot}", dep_w,
                                                lambda e, ex=ex, fc=fc, fslot=fslot:
                                                e.dma_start(
                                    out=wgu_sb[:, fslot, 0, :, :],
                                    in_=wg_d[ex, :, fc * 128:(fc + 1) * 128]
                                    .rearrange("(k p) c -> p k c", p=128)))
                                t_wgu = sch.dma("g", f"dwu{p}s{fslot}", [],
                                                lambda e, ex=ex, fc=fc, fslot=fslot:
                                                e.dma_start(
                                    out=wgu_sb[:, fslot, 1, :, :],
                                    in_=wu_d[ex, :, fc * 128:(fc + 1) * 128]
                                    .rearrange("(k p) c -> p k c", p=128)))
                                wgu_tok[fslot] = t_wgu
                                gb = (1, 2) if fc % 2 == 0 else (4, 5)
                                ub = (3, 6) if fc % 2 == 0 else (0, 7)
                                depsk = (bw_deps(gb[0]) + bw_deps(gb[1]) +
                                         bw_deps(ub[0]) + bw_deps(ub[1]) +
                                         [t_wg1, t_wgu])
                                tg = tu = None
                                for k in range(KC):
                                    st = k == 0
                                    sp = k == KC - 1
                                    for n in range(2):
                                        tg = sch.op("t", depsk if (k == 0 and n == 0)
                                                    else [],
                                                    lambda e, k=k, n=n, fslot=fslot,
                                                    st=st, sp=sp, gb=gb: e.matmul(
                                            B[gb[n]][:, 0:512],
                                            wgu_sb[:, fslot, 0, k, :],
                                            xnF[:, k, n * 512:(n + 1) * 512],
                                            start=st, stop=sp,
                                            skip_group_check=True))
                                    for n in range(2):
                                        tu = sch.op("t", [], lambda e, k=k, n=n,
                                                    fslot=fslot, st=st, sp=sp, ub=ub:
                                                    e.matmul(
                                            B[ub[n]][:, 0:512],
                                            wgu_sb[:, fslot, 1, k, :],
                                            xnF[:, k, n * 512:(n + 1) * 512],
                                            start=st, stop=sp,
                                            skip_group_check=True))
                                for bb in (*gb, *ub):
                                    bank_write(bb, tu)
                                wgu_rd[fslot] = tu
                                for n in range(2):
                                    t_sg = sch.op("a", [tu, t_ac_last],
                                                  lambda e, n=n, gb=gb: e.activation(
                                        gsil[:], B[gb[n]][:, 0:512], ACT.Silu))
                                    bank_read(gb[n], t_sg)
                                    t_up = sch.op("v", [tu, t_ac_last] + t_cb,
                                                  lambda e, n=n, ub=ub: e.tensor_mul(
                                        upsc[:], B[ub[n]][:, 0:512],
                                        cwbt[:, n * 512:(n + 1) * 512]))
                                    bank_read(ub[n], t_up)
                                    t_ac = sch.op("v", [t_sg, t_up],
                                                  lambda e, ei=ei, fc=fc, n=n:
                                                  e.tensor_mul(
                                        acts[:, ei, fc, n * 512:(n + 1) * 512],
                                        gsil[:], upsc[:]))
                                    act_toks.append(t_ac)
                                    t_ac_last = t_ac
                        if gi == 0:
                            tap("t_acts", acts[:], act_toks)

                        t_down_last = [None, None]
                        for og in range(2 * NOG):
                            ws = og % 2
                            new_wd = []
                            for ei, ex in enumerate(grp):
                                dep_wd = ([wd_toks[(ws, ei)]] if wd_toks.get((ws, ei))
                                          else list(wo_done)) + \
                                    ([t_down_last[ws]] if t_down_last[ws] else [])
                                wd_toks[(ws, ei)] = sch.dma(
                                    "g", f"dwd{p}s{ws}e{ei}", dep_wd,
                                    lambda e, ei=ei, ex=ex, og=og, ws=ws: e.dma_start(
                                        out=wd_sb[:, ws, ei, :, :],
                                        in_=wd_d[ex, :, og * 256:(og + 1) * 256]
                                        .rearrange("(f p) c -> p f c", p=128)))
                                new_wd.append(wd_toks[(ws, ei)])
                            for tc in range(NTC):
                                bslot = 4 + (tc % 2)
                                nmm = len(grp) * 8
                                deps0 = bw_deps(bslot) + act_toks + new_wd + wo_done
                                tk = None
                                i = 0
                                for ei in range(len(grp)):
                                    for fc in range(8):
                                        first = i == 0
                                        last = i == nmm - 1
                                        tk = sch.op("t", deps0 if first else [],
                                                    lambda e, ei=ei, fc=fc, tc=tc,
                                                    bslot=bslot, first=first, last=last,
                                                    ws=ws: e.matmul(
                                            B[bslot][:, 0:256],
                                            acts[:, ei, fc, tc * 128:(tc + 1) * 128],
                                            wd_sb[:, ws, ei, fc, :],
                                            start=first, stop=last,
                                            skip_group_check=True))
                                        i += 1
                                t_down_last[ws] = tk
                                bank_write(bslot, tk)
                                if not last_grp:
                                    t_e = sch.op("v", [tk], lambda e, tc=tc, og=og,
                                                 bslot=bslot: e.tensor_add(
                                        acc[:, tc, og * 256:(og + 1) * 256],
                                        acc[:, tc, og * 256:(og + 1) * 256],
                                        B[bslot][:, 0:256]))
                                    bank_read(bslot, t_e)
                                else:
                                    oslot = tc % 2
                                    dep_o = [t_ob[oslot]] if t_ob[oslot] else []
                                    t_s1 = sch.op("v", [tk] + dep_o,
                                                  lambda e, tc=tc, og=og, oslot=oslot,
                                                  bslot=bslot: e.tensor_add(
                                        obuf[:, oslot, 0:256], B[bslot][:, 0:256],
                                        acc[:, tc, og * 256:(og + 1) * 256]))
                                    bank_read(bslot, t_s1)
                                    t_st = sch.dma("s", f"do{p}s{oslot}", [t_s1],
                                                   lambda e, tc=tc, og=og, oslot=oslot:
                                                   e.dma_start(
                                        out=out_d[t0 + tc * 128: t0 + (tc + 1) * 128,
                                                  og * 256:(og + 1) * 256],
                                        in_=obuf[:, oslot, 0:256]))
                                    t_ob[oslot] = t_st

                sch.flush(block)

        sch.raw("s", sch.fence(), lambda e: e.nop())
        sch.flush(block)

    return nc


def _prep_inputs(inp):
    x = np.ascontiguousarray(inp["L0_attn_hidden_states"])
    wba = np.ascontiguousarray(inp["L0_attn_W_ba"])
    wba_re = np.ascontiguousarray(wba[:, list(range(0, 32, 2)) + list(range(1, 32, 2))])
    cw = np.ascontiguousarray(inp["L0_attn_conv_weight"])
    convw_re = np.zeros((128, 192), np.float32)
    for part in range(3):
        for h in range(NH):
            blk = cw[part * 2048 + h * 128: part * 2048 + (h + 1) * 128, :]
            convw_re[:, (part * 16 + h) * 4:(part * 16 + h) * 4 + 4] = blk
    shared = {
        "wqkvz": np.ascontiguousarray(inp["L0_attn_W_qkvz"]),
        "wba": wba_re,
        "convw": convw_re,
        "dtb": np.ascontiguousarray(inp["L0_attn_dt_bias"].reshape(16, 1)),
        "alog": np.ascontiguousarray(inp["L0_attn_A_log"].reshape(16, 1)),
        "nw": np.ascontiguousarray(inp["L0_attn_norm_weight"].reshape(128, 1)),
        "wout": np.ascontiguousarray(inp["L0_attn_W_out"]),
        "rw": np.ascontiguousarray(inp["L0_ffn_router_w"]),
        "wg": np.ascontiguousarray(inp["L0_ffn_gate_proj"]),
        "wu": np.ascontiguousarray(inp["L0_ffn_up_proj"]),
        "wd": np.ascontiguousarray(inp["L0_ffn_down_proj"]),
    }
    in_maps = []
    for c in range(NCORES):
        m = dict(shared)
        m["x"] = np.ascontiguousarray(x[c])
        in_maps.append(m)
    return in_maps


def kernel(**inputs):
    in_maps = _prep_inputs(inputs)
    nc = build_nc()
    res = run_bass_kernel_spmd(nc, in_maps, core_ids=list(range(NCORES)))
    out = np.stack([np.asarray(r["out"]) for r in res.results], axis=0)
    return out.astype(np.float32)
